# version 02
# baseline (speedup 1.0000x reference)
import numpy as np
import ml_dtypes
from contextlib import ExitStack

import concourse.bass as bass
import concourse.mybir as mybir
from concourse.bass_utils import run_bass_kernel_spmd

F32 = mybir.dt.float32
BF16 = mybir.dt.bfloat16
U8 = mybir.dt.uint8
I32 = mybir.dt.int32
AF = mybir.ActivationFunctionType
ALU = mybir.AluOpType
AX = mybir.AxisListType
bfdt = ml_dtypes.bfloat16

NCORE = 8
T = 1024
HID = 4096
INTER = 11008
NJC = 32
NGI = 86
TP = 256
NPH = 4
SUB = 64
NSUB = 4

C_EXPB, C_INV4096, C_EPS, C_NEG1, C_ONE, C_NEG8 = 0, 1, 2, 3, 4, 5

ENGINES = ("tensor", "vector", "scalar", "gpsimd", "sync")


class Sched:
    def __init__(self, nc):
        self.nc = nc
        self.q = {e: [] for e in ENGINES}
        self.cnt = {}
        self.sems = {}
        self._stack = ExitStack()

    def sem(self, name):
        self.sems[name] = self._stack.enter_context(self.nc.semaphore(name))
        self.cnt[name] = 0

    def at(self, name):
        return self.cnt[name]

    def emit(self, eng, fn, waits=(), inc=None, amt=1):
        if inc is not None:
            self.cnt[inc] += amt
        self.q[eng].append((list(waits), fn, inc, amt))

    def dma(self, eng, out, in_, waits=(), inc=None):
        def fn(e, out=out, in_=in_):
            return e.dma_start(out, in_)
        self.emit(eng, fn, waits=waits, inc=inc, amt=16)

    def build(self, block):
        for eng in ENGINES:
            items = self.q[eng]
            if not items:
                continue

            def body(e, items=items):
                last_wait = {}
                for waits, fn, inc, amt in items:
                    for sname, v in waits:
                        if v > 0 and last_wait.get(sname, -1) < v:
                            e.wait_ge(self.sems[sname], v)
                            last_wait[sname] = v
                    ins = fn(e)
                    if inc is not None:
                        if isinstance(ins, list):
                            ins = ins[-1]
                        ins.then_inc(self.sems[inc], amt)
            getattr(block, eng)(body)


def ap(t, offset, dims):
    return bass.AP(t, offset, dims)


def build_nc(taps=(), stop_after="full", nph=NPH, attn_stage="full"):
    nc = bass.Bass("TRN2", target_bir_lowering=False)

    xin = nc.dram_tensor("xin", [T, HID], F32, kind="ExternalInput")
    wqt = nc.dram_tensor("wqt", [NJC * 128, HID], F32, kind="ExternalInput")
    wkt = nc.dram_tensor("wkt", [NJC * 128, HID], F32, kind="ExternalInput")
    wvt = nc.dram_tensor("wvt", [NJC * 128, HID], F32, kind="ExternalInput")
    wot = nc.dram_tensor("wot", [NJC * 128, HID], F32, kind="ExternalInput")
    maskq = nc.dram_tensor("maskq", [128, 3], F32, kind="ExternalInput")
    maskkv = nc.dram_tensor("maskkv", [128, 3], F32, kind="ExternalInput")
    consts = nc.dram_tensor("consts", [128, 8], F32, kind="ExternalInput")
    idmat = nc.dram_tensor("idmat", [128, 128], BF16, kind="ExternalInput")
    idmat2 = nc.dram_tensor("idmat2", [128, 128], BF16, kind="ExternalInput")
    ones128 = nc.dram_tensor("ones128", [128, 2], BF16, kind="ExternalInput")
    onesrow = nc.dram_tensor("onesrow", [1, 128], BF16, kind="ExternalInput")
    onesrow32 = nc.dram_tensor("onesrow32", [1, 128], F32, kind="ExternalInput")
    wide32 = nc.dram_tensor("wide32", [128, 63], BF16, kind="ExternalInput")
    wide86 = nc.dram_tensor("wide86", [128, 171], BF16, kind="ExternalInput")
    gbytes = nc.dram_tensor("gbytes", [NGI * 128, NJC * 128], U8, kind="ExternalInput")
    ubytes = nc.dram_tensor("ubytes", [NGI * 128, NJC * 128], U8, kind="ExternalInput")
    dbytes = nc.dram_tensor("dbytes", [NGI * 128, HID], U8, kind="ExternalInput")
    gscale = nc.dram_tensor("gscale", [NGI, NJC * 128], F32, kind="ExternalInput")
    uscale = nc.dram_tensor("uscale", [NGI, NJC * 128], F32, kind="ExternalInput")
    dscale = nc.dram_tensor("dscale", [NGI, HID], F32, kind="ExternalInput")
    gsc_nat = nc.dram_tensor("gsc_nat", [NJC, INTER], F32, kind="ExternalInput")
    usc_nat = nc.dram_tensor("usc_nat", [NJC, INTER], F32, kind="ExternalInput")
    gzer = nc.dram_tensor("gzer", [NJC, INTER // 8], I32, kind="ExternalInput")
    uzer = nc.dram_tensor("uzer", [NJC, INTER // 8], I32, kind="ExternalInput")
    dzer = nc.dram_tensor("dzer", [NGI, HID // 8], I32, kind="ExternalInput")
    shiftpp = nc.dram_tensor("shiftpp", [128, 1], U8, kind="ExternalInput")

    y = nc.dram_tensor("y", [T, HID], F32, kind="ExternalOutput")

    h_dram = nc.dram_tensor("h_scr", [NGI * 128, T], BF16, kind="Internal")
    wd_dram = nc.dram_tensor("wd_scr", [NGI * 128, HID], BF16, kind="Internal")
    cg_dram = nc.dram_tensor("cg_scr", [32, INTER], BF16, kind="Internal")
    cu_dram = nc.dram_tensor("cu_scr", [32, INTER], BF16, kind="Internal")
    cd_dram = nc.dram_tensor("cd_scr", [NGI, HID], BF16, kind="Internal")

    tap_t = {}
    if "qkv" in taps:
        for n in ("tq", "tk", "tv"):
            tap_t[n] = nc.dram_tensor(n, [128, 8192], BF16, kind="ExternalOutput")
    if "attn" in taps:
        tap_t["to"] = nc.dram_tensor("to", [128, 8192], BF16, kind="ExternalOutput")
    if "xn" in taps:
        tap_t["txn"] = nc.dram_tensor("txn", [128, NJC * T], BF16, kind="ExternalOutput")
    if "h" in taps:
        tap_t["th"] = nc.dram_tensor("th", [NGI * 128, T], BF16, kind="ExternalOutput")
    if "wd" in taps:
        tap_t["twd"] = nc.dram_tensor("twd", [256, HID], BF16, kind="ExternalOutput")

    st = ExitStack()
    sb = lambda name, shape, dt: st.enter_context(nc.sbuf_tensor(name, shape, dt))
    psal = lambda name, shape, dt: st.enter_context(nc.psum_tensor(name, shape, dt))

    xn = sb("xn", [128, NJC * T], BF16)
    xfm = sb("xfm", [128, NJC * TP], BF16)
    Abig = sb("Abig", [128, 32768], BF16)
    wstr = sb("wstr", [128, 2 * HID], BF16)
    bstr = sb("bstr", [128, 2 * HID], U8)
    rstr = sb("rstr", [128, HID], BF16)
    soar = sb("soar", [128, 4096], BF16)
    wsb2 = sb("wsb2", [128, 4096], BF16)
    qsub = sb("qsub", [128, 2048], BF16)
    ksub = sb("ksub", [128, 2048], BF16)
    vsub = sb("vsub", [128, 2048], BF16)
    zsb = sb("zsb", [128, 32], F32)
    rzbf = sb("rzbf", [128, 32], BF16)
    addr = lambda t_: nc.lookup_mloc(t_).addr
    w2 = nc.alloc_sbuf_tensor_at("w2", [128, 4096], BF16, offset=addr(bstr))
    cstr = nc.alloc_sbuf_tensor_at("cstr", [128, HID], BF16, offset=addr(Abig))
    sum32 = nc.alloc_sbuf_tensor_at("sum32", [128, 8192], F32, offset=addr(Abig))
    cbuf = nc.alloc_sbuf_tensor_at("cbuf", [128, 2048], BF16, offset=addr(xfm))
    assert addr(ksub) == addr(qsub) + 4096, (addr(qsub), addr(ksub))
    xtm = nc.alloc_sbuf_tensor_at("xtm", [128, 4096], BF16, offset=addr(qsub))
    ustr = nc.alloc_sbuf_tensor_at("ustr", [128, HID], U8, offset=addr(vsub))
    hdown = nc.alloc_sbuf_tensor_at("hdown", [128, 1024], BF16, offset=addr(soar))
    zztmp = nc.alloc_sbuf_tensor_at("zztmp", [128, 2112], F32, offset=addr(wstr))
    lnrow = nc.alloc_sbuf_tensor_at("lnrow", [128, 768], F32, offset=addr(soar))
    lnbf = nc.alloc_sbuf_tensor_at("lnbf", [128, 512], BF16, offset=addr(soar) + 3072)
    stats = nc.alloc_sbuf_tensor_at("stats", [128, 192], F32, offset=addr(rstr))
    wts = nc.alloc_sbuf_tensor_at("wts", [128, 2048], BF16, offset=addr(rstr))
    vts = nc.alloc_sbuf_tensor_at("vts", [128, 2048], BF16, offset=addr(rstr) + 4096)
    x2f = nc.alloc_sbuf_tensor_at("x2f", [128, 1024], F32, offset=addr(soar) + 4096)
    cdnb = nc.alloc_sbuf_tensor_at("cdnb", [128, 512], BF16, offset=addr(vsub))
    x2sb = nc.alloc_sbuf_tensor_at("x2sb", [128, T], BF16, offset=addr(wsb2) + 4096)
    xsb = nc.alloc_sbuf_tensor_at("xsb", [128, T], BF16, offset=addr(wsb2) + 6144)
    cN = sb("cN", [128, 8], F32)
    nc.const_aps.register(F32, -8.0, cN[:, C_NEG8:C_NEG8 + 1])
    mq8 = sb("mq8", [128, 4], F32)
    mkv = sb("mkv", [128, 4], F32)
    idm = sb("idm", [128, 128], BF16)
    idm2 = sb("idm2", [128, 128], BF16)
    ones1 = sb("ones1", [128, 2], BF16)
    onesr = sb("onesr", [1, 128], BF16)
    onesr32 = sb("onesr32", [1, 128], F32)
    w32 = sb("w32", [128, 63], BF16)
    w86 = sb("w86", [128, 171], BF16)
    shf = sb("shf", [128, 1], U8)

    P = [psal(f"P{i}", [128, 512], F32) for i in range(7)]
    PT = psal("PT", [128, 1024], BF16)

    PSx = {}
    def PS(t_):
        k = id(t_)
        if k not in PSx:
            PSx[k] = t_.ap().ap[0][0]
        return PSx[k]

    s = Sched(nc)
    for name in ("sx", "sw", "sr", "se", "sv", "stn", "sh"):
        s.sem(name)

    QOF, KOF, VOF, OOF = 0, 8192, 16384, 24576

    for dst, src in ((cN, consts), (idm, idmat), (idm2, idmat2), (ones1, ones128),
                     (w32, wide32), (w86, wide86), (shf, shiftpp)):
        s.dma("gpsimd", dst[:, :], src[:, :], inc="sx")
    s.dma("gpsimd", onesr[:, :], onesrow[:, :], inc="sx")
    s.dma("gpsimd", onesr32[:, :], onesrow32[:, :], inc="sx")
    s.dma("gpsimd", mq8[:, 0:3], maskq[:, :], inc="sx")
    s.dma("gpsimd", mkv[:, 0:3], maskkv[:, :], inc="sx")
    pro_ld = s.at("sx")

    def mk_masks(v):
        a = v.tensor_tensor(mq8[:, 3:4], mq8[:, 2:3], mq8[:, 1:2], ALU.mult)
        b = v.tensor_tensor(mkv[:, 3:4], mkv[:, 2:3], mkv[:, 1:2], ALU.mult)
        v.drain()
        c = v.tensor_tensor(mq8[:, 3:4], mq8[:, 3:4], mq8[:, 0:1], ALU.add)
        d = v.tensor_tensor(mkv[:, 3:4], mkv[:, 3:4], mkv[:, 0:1], ALU.add)
        return [a, b, c, d]
    s.emit("vector", mk_masks, waits=[("sx", pro_ld)], inc="sv")

    def mk_sig(sc):
        return [sc.activation(mq8[:, 0:1], mq8[:, 3:4], AF.Sigmoid),
                sc.activation(mkv[:, 0:1], mkv[:, 3:4], AF.Sigmoid)]
    s.emit("scalar", mk_sig, waits=[("sv", s.at("sv"))], inc="se")

    def mk_mq8(v):
        return v.tensor_scalar(mq8[:, 0:1], mq8[:, 0:1], 0.125, None, ALU.mult)
    s.emit("vector", mk_mq8, waits=[("se", s.at("se"))], inc="sv")
    masks_sv = s.at("sv")
    masks_se = s.at("se")

    if "masks" in taps:
        tap_t["tmask"] = nc.dram_tensor("tmask", [128, 8], F32, kind="ExternalOutput")
        s.dma("sync", tap_t["tmask"][:, 0:4], mq8[:, 0:4],
              waits=[("sv", masks_sv), ("se", masks_se)], inc="sr")
        s.dma("sync", tap_t["tmask"][:, 4:8], mkv[:, 0:4],
              waits=[("sv", masks_sv), ("se", masks_se)], inc="sr")

    projs = [("q", wqt, QOF, True), ("k", wkt, KOF, False), ("v", wvt, VOF, False)]

    for ph in range(nph):
        t0 = ph * TP
        for g in range(2):
            s.dma("gpsimd", xtm[:, 0:4096], xin[t0 + g * 128: t0 + (g + 1) * 128, :],
                  waits=[("sr", s.at("sr")), ("se", s.at("se"))], inc="sx")
            ld = s.at("sx")
            for jc in range(NJC):
                dst = ap(xfm, jc * TP + g * 128, [[PS(xfm), 128], [1, 128]])
                def fn(e, dst=dst, jc=jc):
                    return e.dma_start(dst, xtm[:, jc * 128:(jc + 1) * 128], transpose=True)
                s.emit("sync", fn, waits=[("sx", ld)], inc="sr", amt=16)
        xfm_ready = s.at("sr")

        for (pname, wsrc, dof, isq) in projs:
            for ic in range(NJC):
                buf = ic % 2
                s.dma("gpsimd", wstr[:, buf * HID:(buf + 1) * HID],
                      wsrc[ic * 128:(ic + 1) * 128, :],
                      waits=[("stn", s.at("stn"))], inc="sw")
                wld = s.at("sw")
                pb = P[ic % 2]
                def mmfn(t_, buf=buf, pb=pb):
                    outs = []
                    for jc in range(NJC):
                        outs.append(t_.matmul(
                            pb[:, 0:TP],
                            wstr[:, buf * HID + jc * 128: buf * HID + (jc + 1) * 128],
                            ap(xfm, jc * TP, [[PS(xfm), 128], [1, TP]]),
                            start=(jc == 0), stop=(jc == NJC - 1)))
                    return outs
                s.emit("tensor", mmfn,
                       waits=[("sw", wld), ("sr", xfm_ready), ("se", s.at("se"))],
                       inc="stn")
                mmd = s.at("stn")
                scale_ap = mq8[:, 0:1] if isq else mkv[:, 0:1]
                dst = ap(Abig, dof + ic * TP, [[PS(Abig), 128], [1, TP]])
                def evfn(sc, dst=dst, pb=pb, scale_ap=scale_ap):
                    return sc.activation(dst, pb[:, 0:TP], AF.Copy, scale=scale_ap)
                s.emit("scalar", evfn,
                       waits=[("stn", mmd), ("sv", masks_sv), ("sr", s.at("sr"))],
                       inc="se")
        qkv_ready = s.at("sr")

        if "qkv" in taps and ph == 0:
            evall = s.at("se")
            s.dma("sync", tap_t["tq"][:, :], ap(Abig, QOF, [[PS(Abig), 128], [1, 8192]]),
                  waits=[("se", evall)], inc="sr")
            s.dma("sync", tap_t["tk"][:, :], ap(Abig, KOF, [[PS(Abig), 128], [1, 8192]]),
                  waits=[("se", evall)], inc="sr")
            s.dma("sync", tap_t["tv"][:, :], ap(Abig, VOF, [[PS(Abig), 128], [1, 8192]]),
                  waits=[("se", evall)], inc="sr")
            qkv_ready = s.at("sr")

        if stop_after == "qkv":
            continue

        for sub in range(8):
            sl0 = sub * 32
            for (dof, dstt) in ((QOF, qsub), (KOF, ksub), (VOF, vsub)):
                for hsel in range(2):
                    srcap = ap(Abig, dof + hsel * 64 * PS(Abig) + sl0,
                               [[PS(Abig), 64], [TP, 32], [1, 32]])
                    dstap = ap(dstt, hsel * 1024, [[PS(dstt), 64], [32, 32], [1, 32]])
                    def rfn(e, dstap=dstap, srcap=srcap):
                        return e.dma_start(dstap, srcap)
                    s.emit("sync", rfn,
                           waits=[("sr", qkv_ready), ("stn", s.at("stn")), ("se", s.at("se"))],
                           inc="sr", amt=16)
            sub_rdy = s.at("sr")
            if attn_stage == "rearr":
                continue
            for bk in range(4):
                pb = P[2 + (bk % 2)]
                def scfn(t_, pb=pb, bk=bk):
                    outs = []
                    for j in range(8):
                        tl = bk * 8 + j
                        lh = ap(qsub, tl, [[PS(qsub), 64], [32, 64]])
                        rh = ap(ksub, tl, [[PS(ksub), 64], [32, 64]])
                        outs.append(t_.matmul(pb[0:64, j * 64:(j + 1) * 64], lh, rh,
                                              start=True, stop=True))
                    return outs
                s.emit("tensor", scfn,
                       waits=[("sr", sub_rdy), ("se", s.at("se")), ("sv", s.at("sv"))],
                       inc="stn")
                scd = s.at("stn")
                if attn_stage == "mm":
                    continue
                srcp = ap(pb, 0, [[PS(pb), 64], [64, 8], [1, 64]])
                dstw = ap(wsb2, bk * 8 * 64, [[PS(wsb2), 64], [64, 8], [1, 64]])
                def exfn(sc, dstw=dstw, srcp=srcp):
                    return sc.activation(dstw, srcp, AF.Exp, bias=cN[0:64, C_EXPB:C_EXPB + 1])
                s.emit("scalar", exfn, waits=[("stn", scd)], inc="se")
                exd = s.at("se")
                def zfn(v, bk=bk):
                    srcw = ap(wsb2, bk * 8 * 64, [[PS(wsb2), 64], [64, 8], [1, 64]])
                    return v.reduce_sum(zsb[0:64, bk * 8:(bk + 1) * 8], srcw, axis=AX.X)
                s.emit("vector", zfn, waits=[("se", exd)], inc="sv")
            if attn_stage == "mm":
                continue
            expd = s.at("se")
            zred = s.at("sv")
            def rzfn(v):
                a_ = v.reciprocal(zsb[0:64, 0:32], zsb[0:64, 0:32])
                v.drain()
                b_ = v.tensor_copy(rzbf[0:64, 0:32], zsb[0:64, 0:32])
                return [a_, b_]
            s.emit("vector", rzfn, waits=[("sv", zred)], inc="sv")
            def w2fn(v):
                i0 = ap(wsb2, 0, [[PS(wsb2), 64], [64, 32], [1, 64]])
                i1 = ap(rzbf, 0, [[PS(rzbf), 64], [1, 32], [0, 64]])
                o = ap(w2, 0, [[PS(w2), 64], [64, 32], [1, 64]])
                return v.tensor_tensor(o, i0, i1, ALU.mult)
            s.emit("vector", w2fn, waits=[("sv", s.at("sv")), ("se", expd)], inc="sv")
            w2d = s.at("sv")
            if attn_stage == "scores":
                continue

            for grp in range(4):
                def trfn(t_, grp=grp):
                    outs = []
                    for i in range(8):
                        tl = grp * 8 + i
                        win = ap(w2, tl * 64, [[PS(w2), 64], [1, 64]])
                        wout = ap(PT, i * 64, [[PS(PT), 64], [1, 64]])
                        outs.append(t_.transpose(wout, win, idm[0:64, 0:64]))
                        vin = ap(vsub, tl, [[PS(vsub), 64], [32, 64]])
                        vout = ap(PT, 512 + i * 64, [[PS(PT), 64], [1, 64]])
                        outs.append(t_.transpose(vout, vin, idm[0:64, 0:64]))
                    return outs
                s.emit("tensor", trfn, waits=[("sv", w2d), ("se", s.at("se"))], inc="stn")
                trd = s.at("stn")
                def tefn(sc, grp=grp):
                    outs = []
                    wsrc = ap(PT, 0, [[PS(PT), 64], [64, 8], [1, 64]])
                    wdst = ap(wts, grp * 8, [[PS(wts), 64], [1, 8], [32, 64]])
                    outs.append(sc.copy(wdst, wsrc))
                    vsrc = ap(PT, 512, [[PS(PT), 64], [64, 8], [1, 64]])
                    vdst = ap(vts, grp * 8, [[PS(vts), 64], [1, 8], [32, 64]])
                    outs.append(sc.copy(vdst, vsrc))
                    return outs
                s.emit("scalar", tefn, waits=[("stn", trd)], inc="se")
            trevd = s.at("se")
            if attn_stage == "trans":
                continue

            for bko in range(4):
                pb = P[4 + (bko % 2)]
                def omfn(t_, pb=pb, bko=bko):
                    outs = []
                    for j in range(8):
                        tl = bko * 8 + j
                        lh = ap(vts, tl, [[PS(vts), 64], [32, 64]])
                        rh = ap(wts, tl, [[PS(wts), 64], [32, 64]])
                        outs.append(t_.matmul(pb[0:64, j * 64:(j + 1) * 64], lh, rh,
                                              start=True, stop=True))
                    return outs
                s.emit("tensor", omfn, waits=[("se", s.at("se"))], inc="stn")
                omd = s.at("stn")
                srcp = ap(pb, 0, [[PS(pb), 64], [64, 8], [1, 64]])
                dsto = ap(soar, bko * 8, [[PS(soar), 64], [1, 8], [32, 64]])
                def oefn(sc, dsto=dsto, srcp=srcp):
                    return sc.copy(dsto, srcp)
                s.emit("scalar", oefn, waits=[("stn", omd)], inc="se")
            oevd = s.at("se")
            for par in range(2):
                srcap = ap(soar, par * 32, [[PS(soar), 64], [64, 32], [1, 32]])
                dstap = ap(Abig, OOF + par * 64 * PS(Abig) + sl0, [[PS(Abig), 64], [TP, 32], [1, 32]])
                def rofn(e, dstap=dstap, srcap=srcap):
                    return e.dma_start(dstap, srcap)
                s.emit("sync", rofn, waits=[("se", oevd)], inc="sr", amt=16)
        attn_done = s.at("sr")

        if "attn" in taps and ph == 0:
            s.dma("sync", tap_t["to"][:, :], ap(Abig, OOF, [[PS(Abig), 128], [1, 8192]]),
                  waits=[("sr", attn_done)], inc="sr")
            attn_done = s.at("sr")

        if stop_after == "attn":
            continue

        for mc in range(NJC):
            buf = mc % 2
            s.dma("gpsimd", wstr[:, buf * HID:(buf + 1) * HID],
                  wot[mc * 128:(mc + 1) * 128, :],
                  waits=[("stn", s.at("stn"))], inc="sw")
            wld = s.at("sw")
            pb = P[mc % 2]
            def wofn(t_, buf=buf, pb=pb):
                outs = []
                for ac in range(NJC):
                    outs.append(t_.matmul(
                        pb[:, 0:TP],
                        wstr[:, buf * HID + ac * 128: buf * HID + (ac + 1) * 128],
                        ap(Abig, OOF + ac * TP, [[PS(Abig), 128], [1, TP]]),
                        start=(ac == 0), stop=(ac == NJC - 1)))
                return outs
            s.emit("tensor", wofn,
                   waits=[("sw", wld), ("sr", attn_done), ("sv", s.at("sv"))], inc="stn")
            wod = s.at("stn")
            def refn(v, mc=mc, pb=pb):
                return v.tensor_tensor(
                    sum32[:, mc * TP:(mc + 1) * TP],
                    pb[:, 0:TP],
                    ap(xfm, mc * TP, [[PS(xfm), 128], [1, TP]]),
                    ALU.add)
            s.emit("vector", refn, waits=[("stn", wod)], inc="sv")
        sumd = s.at("sv")

        for g8 in range(4):
            def castf(sc, g8=g8):
                outs = []
                for i in range(8):
                    mc = g8 * 8 + i
                    outs.append(sc.copy(qsub[:, i * 256:(i + 1) * 256],
                                        sum32[:, mc * TP:(mc + 1) * TP]))
                    outs.append(sc.activation(ksub[:, i * 256:(i + 1) * 256],
                                              sum32[:, mc * TP:(mc + 1) * TP], AF.Square))
                return outs
            s.emit("scalar", castf, waits=[("sv", sumd), ("stn", s.at("stn")), ("sr", s.at("sr"))], inc="se")
            cd_ = s.at("se")
            def stf2(t_, g8=g8):
                outs = []
                for i in range(8):
                    mc = g8 * 8 + i
                    outs.append(t_.matmul(P[2][0:1, 0:TP], ones1[:, 0:1],
                                          qsub[:, i * 256:(i + 1) * 256],
                                          start=(mc == 0), stop=(mc == NJC - 1)))
                    outs.append(t_.matmul(P[3][0:1, 0:TP], ones1[:, 0:1],
                                          ksub[:, i * 256:(i + 1) * 256],
                                          start=(mc == 0), stop=(mc == NJC - 1)))
                return outs
            s.emit("tensor", stf2, waits=[("se", cd_), ("sv", sumd)], inc="stn")
        std_ = s.at("stn")
        def rowfn(v):
            o = [v.tensor_scalar(lnrow[0:1, 0:256], P[2][0:1, 0:TP],
                                 cN[0:1, C_INV4096:C_INV4096 + 1], None, ALU.mult),
                 v.tensor_scalar(lnrow[0:1, 256:512], P[3][0:1, 0:TP],
                                 cN[0:1, C_INV4096:C_INV4096 + 1], None, ALU.mult)]
            v.drain()
            o.append(v.tensor_tensor(lnrow[0:1, 512:768], lnrow[0:1, 0:256], lnrow[0:1, 0:256], ALU.mult))
            v.drain()
            o.append(v.tensor_tensor(lnrow[0:1, 512:768], lnrow[0:1, 256:512], lnrow[0:1, 512:768], ALU.subtract))
            v.drain()
            o.append(v.tensor_scalar(lnrow[0:1, 512:768], lnrow[0:1, 512:768],
                                     cN[0:1, C_EPS:C_EPS + 1], None, ALU.add))
            return o
        s.emit("vector", rowfn, waits=[("stn", std_)], inc="sv")
        def rsq(sc):
            return sc.activation(lnrow[0:1, 256:512], lnrow[0:1, 512:768], AF.Sqrt)
        s.emit("scalar", rsq, waits=[("sv", s.at("sv"))], inc="se")
        def bfn(v):
            o = [v.reciprocal(lnrow[0:1, 256:512], lnrow[0:1, 256:512])]
            v.drain()
            o.append(v.tensor_tensor(lnrow[0:1, 0:256], lnrow[0:1, 0:256], lnrow[0:1, 256:512], ALU.mult))
            v.drain()
            o.append(v.tensor_scalar(lnrow[0:1, 0:256], lnrow[0:1, 0:256],
                                     cN[0:1, C_NEG1:C_NEG1 + 1], None, ALU.mult))
            return o
        s.emit("vector", bfn, waits=[("se", s.at("se"))], inc="sv")
        rowd = s.at("sv")
        def bcfn(t_):
            return [
                t_.matmul(P[4][:, 0:TP], onesr32[0:1, :], lnrow[0:1, 256:512], start=True, stop=True),
                t_.matmul(P[5][:, 0:TP], onesr32[0:1, :], lnrow[0:1, 0:256], start=True, stop=True),
            ]
        s.emit("tensor", bcfn, waits=[("sv", rowd), ("se", s.at("se"))], inc="stn")
        bcd = s.at("stn")
        def apfn(v, ph=ph):
            outs = []
            for mc2 in range(0, NJC, 2):
                t0 = x2f[:, 0:256]
                t1 = x2f[:, 256:512]
                outs.append(v.tensor_tensor(
                    t0, sum32[:, mc2 * TP:(mc2 + 1) * TP], P[4][:, 0:TP], ALU.mult))
                outs.append(v.tensor_tensor(
                    t1, sum32[:, (mc2 + 1) * TP:(mc2 + 2) * TP], P[4][:, 0:TP], ALU.mult))
                v.drain()
                outs.append(v.tensor_tensor(
                    ap(xn, mc2 * T + ph * TP, [[PS(xn), 128], [1, TP]]),
                    t0, P[5][:, 0:TP], ALU.add))
                outs.append(v.tensor_tensor(
                    ap(xn, (mc2 + 1) * T + ph * TP, [[PS(xn), 128], [1, TP]]),
                    t1, P[5][:, 0:TP], ALU.add))
                v.drain()
            return outs
        s.emit("vector", apfn, waits=[("stn", bcd)], inc="sv")
        lastap = s.at("sv")
        def nopf(e):
            return e.nop()
        s.emit("sync", nopf, waits=[("sv", lastap)], inc="sr", amt=1)

    xn_sv = s.at("sv")

    if stop_after == "qkv" or stop_after == "attn":
        fin = s.at("sr")
        def earlyw(e):
            return e.nop()
        s.emit("sync", earlyw, waits=[("sr", fin)])
        block = st.enter_context(nc.Block())
        s.build(block)
        st.close()
        s._stack.close()
        return nc, tap_t

    if "xn" in taps:
        s.dma("sync", tap_t["txn"][:, :], xn[:, :], waits=[("sv", xn_sv)], inc="sr")

    if stop_after == "xn":
        fin = s.at("sr")
        def earlyw2(e):
            return e.nop()
        s.emit("sync", earlyw2, waits=[("sr", fin)])
        block = st.enter_context(nc.Block())
        s.build(block)
        st.close()
        s._stack.close()
        return nc, tap_t

    PIECES = [(0, 512), (512, 512), (1024, 352)]
    for (zdr, sdr, cdr) in ((gzer, gsc_nat, cg_dram), (uzer, usc_nat, cu_dram)):
        for (p0, pw) in PIECES:
            s.dma("gpsimd", ap(zztmp, 0, [[PS(zztmp), 32], [1, pw]]).bitcast(I32),
                  zdr[:, p0:p0 + pw],
                  waits=[("sv", s.at("sv")), ("sx", s.at("sx")), ("sr", s.at("sr")),
                         ("se", s.at("se")), ("sh", s.at("sh")), ("stn", s.at("stn"))], inc="sx")
            s.dma("gpsimd", ap(xfm, 0, [[PS(xfm), 32], [1, pw * 8]]),
                  sdr[:, p0 * 8:(p0 + pw) * 8], inc="sx")
            zl = s.at("sx")
            for jj in range(8):
                def plfn(v, jj=jj, pw=pw):
                    o1 = v.tensor_scalar(ap(zztmp, 512, [[PS(zztmp), 32], [1, pw]]).bitcast(I32),
                                         ap(zztmp, 0, [[PS(zztmp), 32], [1, pw]]).bitcast(I32),
                                         4 * jj, 15, ALU.logical_shift_right, ALU.bitwise_and)
                    v.drain()
                    o2 = v.tensor_copy(ap(zztmp, 1024, [[PS(zztmp), 32], [1, pw]]),
                                       ap(zztmp, 512, [[PS(zztmp), 32], [1, pw]]).bitcast(I32))
                    v.drain()
                    o3 = v.tensor_scalar(ap(zztmp, 1024, [[PS(zztmp), 32], [1, pw]]),
                                         ap(zztmp, 1024, [[PS(zztmp), 32], [1, pw]]),
                                         -7.0, -1.0, ALU.add, ALU.mult)
                    v.drain()
                    o4 = v.tensor_copy(ap(w2, 0, [[PS(w2), 32], [1, pw]]),
                                       ap(zztmp, 1024, [[PS(zztmp), 32], [1, pw]]))
                    v.drain()
                    o5 = v.tensor_tensor(ap(w2, 1024, [[PS(w2), 32], [1, pw]]),
                                         ap(w2, 0, [[PS(w2), 32], [1, pw]]),
                                         ap(xfm, jj, [[PS(xfm), 32], [8, pw]]),
                                         ALU.mult)
                    return [o1, o2, o3, o4, o5]
                s.emit("vector", plfn, waits=[("sx", zl), ("sr", s.at("sr"))], inc="sv")
                pld = s.at("sv")
                dst = ap(cdr, p0 * 8 + jj, [[INTER, 32], [8, pw]])
                srcw = ap(w2, 1024, [[PS(w2), 32], [1, pw]])
                def cst_dma(e, dst=dst, srcw=srcw):
                    return e.dma_start(dst, srcw, allow_slow_non_contiguous=True)
                s.emit("sync", cst_dma, waits=[("sv", pld)], inc="sr", amt=16)
    s.dma("gpsimd", ap(xfm, 0, [[PS(xfm), 86], [1, HID]]), dscale[:, :],
          waits=[("sr", s.at("sr")), ("sv", s.at("sv")), ("stn", s.at("stn"))], inc="sx")
    s.dma("gpsimd", ap(zztmp, 0, [[PS(zztmp), 86], [1, 512]]).bitcast(I32), dzer[:, :],
          waits=[("sx", s.at("sx"))], inc="sx")
    dzl = s.at("sx")
    for jj in range(8):
        def pl2fn(v, jj=jj):
            o1 = v.tensor_scalar(ap(zztmp, 512, [[PS(zztmp), 86], [1, 512]]).bitcast(I32),
                                 ap(zztmp, 0, [[PS(zztmp), 86], [1, 512]]).bitcast(I32),
                                 4 * jj, 15, ALU.logical_shift_right, ALU.bitwise_and)
            v.drain()
            o2 = v.tensor_copy(ap(zztmp, 1024, [[PS(zztmp), 86], [1, 512]]),
                               ap(zztmp, 512, [[PS(zztmp), 86], [1, 512]]).bitcast(I32))
            v.drain()
            o3 = v.tensor_scalar(ap(zztmp, 1024, [[PS(zztmp), 86], [1, 512]]),
                                 ap(zztmp, 1024, [[PS(zztmp), 86], [1, 512]]),
                                 -7.0, -1.0, ALU.add, ALU.mult)
            v.drain()
            o4 = v.tensor_copy(ap(w2, 2048, [[PS(w2), 86], [1, 512]]),
                               ap(zztmp, 1024, [[PS(zztmp), 86], [1, 512]]))
            v.drain()
            o5 = v.tensor_tensor(ap(w2, 3072, [[PS(w2), 86], [1, 512]]),
                                 ap(w2, 2048, [[PS(w2), 86], [1, 512]]),
                                 ap(xfm, jj, [[PS(xfm), 86], [8, 512]]),
                                 ALU.mult)
            return [o1, o2, o3, o4, o5]
        s.emit("vector", pl2fn, waits=[("sx", dzl), ("sr", s.at("sr"))], inc="sv")
        pl2d = s.at("sv")
        dstcd = ap(cd_dram, jj, [[HID, 86], [8, 512]])
        srccd = ap(w2, 3072, [[PS(w2), 86], [1, 512]])
        def cddma(e, dstcd=dstcd, srccd=srccd):
            return e.dma_start(dstcd, srccd, allow_slow_non_contiguous=True)
        s.emit("sync", cddma, waits=[("sv", pl2d)], inc="sr", amt=16)
    cdone_sv = s.at("sv")
    cg_done_sr = s.at("sr")

    for tc in range(2):
        def xsfn(t_, tc=tc):
            outs = []
            for kc in range(NJC):
                outs.append(t_.matmul(
                    P[6][0:32, 0:512],
                    w32[:, 31 - kc:63 - kc],
                    ap(xn, kc * T + tc * 512, [[PS(xn), 128], [1, 512]]),
                    start=(kc == 0), stop=(kc == NJC - 1)))
            return outs
        s.emit("tensor", xsfn, waits=[("sv", xn_sv), ("se", s.at("se")), ("stn", s.at("stn"))], inc="stn")
        xsd = s.at("stn")
        def xsev(sc, tc=tc):
            return sc.copy(ap(xsb, tc * 512, [[PS(xsb), 32], [1, 512]]),
                           ap(P[6], 0, [[PS(P[6]), 32], [1, 512]]))
        s.emit("scalar", xsev, waits=[("stn", xsd)], inc="se")
    xs_done = s.at("se")

    for ncx in range(NGI):
        buf = ncx % 2
        if ncx % 8 == 0:
            cw = min(1024, (NGI - ncx) * 128)
            s.dma("gpsimd", cbuf[0:32, 0:cw], cg_dram[:, ncx * 128: ncx * 128 + cw],
                  waits=[("stn", s.at("stn")), ("sr", cg_done_sr), ("sv", cdone_sv)], inc="sx")
            s.dma("gpsimd", cbuf[0:32, 1024:1024 + cw], cu_dram[:, ncx * 128: ncx * 128 + cw],
                  inc="sx")
        cb_ld = s.at("sx")
        s.dma("gpsimd", bstr[:, buf * HID:(buf + 1) * HID],
              gbytes[ncx * 128:(ncx + 1) * 128, :],
              waits=[("sv", s.at("sv"))], inc="sx")
        s.dma("gpsimd", rstr[:, 0:HID],
              ap(gscale, ncx * NJC * 128, [[0, 128], [1, HID]]),
              waits=[("sh", s.at("sh"))], inc="sx")
        gld = s.at("sx")
        def dqg(v, buf=buf, gld=gld):
            return v.tensor_scalar(ustr[:, :], bstr[:, buf * HID:(buf + 1) * HID],
                                   shf[:, 0:1], 15, ALU.logical_shift_right, ALU.bitwise_and)
        s.emit("vector", dqg, waits=[("sx", gld), ("se", s.at("se")), ("stn", s.at("stn"))], inc="sv")
        uq = s.at("sv")
        def cst(sc):
            return sc.activation(cstr[:, :], ustr[:, :], AF.Copy, bias=-8.0)
        s.emit("scalar", cst, waits=[("sv", uq), ("sh", s.at("sh"))], inc="se")
        csd = s.at("se")
        def sclm(g, buf=buf, csd=csd):
            return g.tensor_tensor(wstr[:, buf * HID:(buf + 1) * HID], cstr[:, :],
                                   rstr[:, 0:HID], ALU.mult)
        s.emit("gpsimd", sclm, waits=[("se", csd)], inc="sh")
        wg = s.at("sh")
        for tc in range(2):
            def gmm(t_, buf=buf, tc=tc, ncx=ncx):
                outs = []
                for kc in range(NJC):
                    outs.append(t_.matmul(
                        P[tc][:, 0:512],
                        wstr[:, buf * HID + kc * 128: buf * HID + (kc + 1) * 128],
                        ap(xn, kc * T + tc * 512, [[PS(xn), 128], [1, 512]]),
                        start=(kc == 0), stop=False))
                outs.append(t_.matmul(
                    P[tc][:, 0:512],
                    ap(cbuf, (ncx % 8) * 128, [[PS(cbuf), 32], [1, 128]]),
                    ap(xsb, tc * 512, [[PS(xsb), 32], [1, 512]]),
                    start=False, stop=True))
                return outs
            s.emit("tensor", gmm, waits=[("sh", wg), ("se", xs_done), ("sv", s.at("sv")), ("sx", cb_ld)], inc="stn")
        gmd = s.at("stn")
        def silu(sc, gmd=gmd):
            return [sc.activation(qsub[:, 0:512], P[0][:, 0:512], AF.Silu),
                    sc.activation(qsub[:, 512:1024], P[1][:, 0:512], AF.Silu)]
        s.emit("scalar", silu, waits=[("stn", gmd), ("sv", s.at("sv")), ("sr", s.at("sr"))], inc="se")
        sild = s.at("se")
        s.dma("gpsimd", bstr[:, buf * HID:(buf + 1) * HID],
              ubytes[ncx * 128:(ncx + 1) * 128, :],
              waits=[("sh", wg)], inc="sx")
        s.dma("gpsimd", rstr[:, 0:HID],
              ap(uscale, ncx * NJC * 128, [[0, 128], [1, HID]]),
              waits=[("sh", wg)], inc="sx")
        uld = s.at("sx")
        def dqu(v, buf=buf, uld=uld):
            return v.tensor_scalar(ustr[:, :], bstr[:, buf * HID:(buf + 1) * HID],
                                   shf[:, 0:1], 15, ALU.logical_shift_right, ALU.bitwise_and)
        s.emit("vector", dqu, waits=[("sx", uld), ("se", sild)], inc="sv")
        uq2 = s.at("sv")
        def cst2(sc):
            return sc.activation(cstr[:, :], ustr[:, :], AF.Copy, bias=-8.0)
        s.emit("scalar", cst2, waits=[("sv", uq2), ("sh", s.at("sh"))], inc="se")
        csd2 = s.at("se")
        def sclm2(g, buf=buf, csd2=csd2):
            return g.tensor_tensor(wstr[:, buf * HID:(buf + 1) * HID], cstr[:, :],
                                   rstr[:, 0:HID], ALU.mult)
        s.emit("gpsimd", sclm2, waits=[("se", csd2)], inc="sh")
        wu = s.at("sh")
        for tc in range(2):
            def umm(t_, buf=buf, tc=tc, ncx=ncx):
                outs = []
                for kc in range(NJC):
                    outs.append(t_.matmul(
                        P[2 + tc][:, 0:512],
                        wstr[:, buf * HID + kc * 128: buf * HID + (kc + 1) * 128],
                        ap(xn, kc * T + tc * 512, [[PS(xn), 128], [1, 512]]),
                        start=(kc == 0), stop=False))
                outs.append(t_.matmul(
                    P[2 + tc][:, 0:512],
                    ap(cbuf, 1024 + (ncx % 8) * 128, [[PS(cbuf), 32], [1, 128]]),
                    ap(xsb, tc * 512, [[PS(xsb), 32], [1, 512]]),
                    start=False, stop=True))
                return outs
            s.emit("tensor", umm, waits=[("sh", wu), ("sv", s.at("sv")), ("se", s.at("se"))], inc="stn")
        umd = s.at("stn")
        def hfn(v, umd=umd):
            return [v.tensor_tensor(ksub[:, 0:512], qsub[:, 0:512], P[2][:, 0:512], ALU.mult),
                    v.tensor_tensor(ksub[:, 512:1024], qsub[:, 512:1024], P[3][:, 0:512], ALU.mult)]
        s.emit("vector", hfn, waits=[("stn", umd), ("sr", s.at("sr"))], inc="sv")
        hd = s.at("sv")
        def x2fn(t_, ncx=ncx):
            outs = []
            for tc in range(2):
                outs.append(t_.matmul(
                    P[4 + tc][0:86, 0:512],
                    w86[:, 85 - ncx:171 - ncx],
                    ksub[:, tc * 512:(tc + 1) * 512],
                    start=(ncx == 0), stop=(ncx == NGI - 1)))
            return outs
        s.emit("tensor", x2fn, waits=[("sv", hd)], inc="stn")
        s.dma("sync", h_dram[ncx * 128:(ncx + 1) * 128, :], ksub[:, 0:1024],
              waits=[("sv", hd)], inc="sr")
    gate_done_sr = s.at("sr")
    x2mm = s.at("stn")
    def x2ev(sc):
        return [sc.copy(ap(x2sb, 0, [[PS(x2sb), 86], [1, 512]]),
                        ap(P[4], 0, [[PS(P[4]), 86], [1, 512]])),
                sc.copy(ap(x2sb, 512, [[PS(x2sb), 86], [1, 512]]),
                        ap(P[5], 0, [[PS(P[5]), 86], [1, 512]]))]
    s.emit("scalar", x2ev, waits=[("stn", x2mm)], inc="se")
    x2d = s.at("se")

    if "h" in taps:
        s.dma("sync", tap_t["th"][:, :], h_dram[:, :], waits=[("sr", gate_done_sr)], inc="sr")

    if stop_after == "h":
        fin = s.at("sr")
        x2w = s.at("se")
        def earlyw3(e):
            return e.nop()
        s.emit("sync", earlyw3, waits=[("sr", fin), ("se", x2w)])
        block = st.enter_context(nc.Block())
        s.build(block)
        st.close()
        s._stack.close()
        return nc, tap_t

    for jc in range(NGI):
        s.dma("gpsimd", bstr[:, 0:HID],
              dbytes[jc * 128:(jc + 1) * 128, :],
              waits=[("sh", s.at("sh")), ("sr", s.at("sr")), ("sv", s.at("sv"))], inc="sx")
        s.dma("gpsimd", rstr[:, 0:HID],
              ap(dscale, jc * HID, [[0, 128], [1, HID]]),
              waits=[("sh", s.at("sh"))], inc="sx")
        dld = s.at("sx")
        def dqd(v, dld=dld):
            return v.tensor_scalar(ustr[:, :], bstr[:, 0:HID],
                                   shf[:, 0:1], 15, ALU.logical_shift_right, ALU.bitwise_and)
        s.emit("vector", dqd, waits=[("sx", dld), ("se", s.at("se"))], inc="sv")
        uq3 = s.at("sv")
        def cst3(sc):
            return sc.activation(cstr[:, :], ustr[:, :], AF.Copy, bias=-8.0)
        s.emit("scalar", cst3, waits=[("sv", uq3), ("sh", s.at("sh"))], inc="se")
        csd3 = s.at("se")
        def sclm3(g, csd3=csd3):
            return g.tensor_tensor(wstr[:, 0:HID], cstr[:, :], rstr[:, 0:HID], ALU.mult)
        s.emit("gpsimd", sclm3, waits=[("se", csd3), ("sr", s.at("sr"))], inc="sh")
        wdd = s.at("sh")
        s.dma("sync", wd_dram[jc * 128:(jc + 1) * 128, :], wstr[:, 0:HID],
              waits=[("sh", wdd)], inc="sr")
    wd_done = s.at("sr")

    if "wd" in taps:
        s.dma("sync", tap_t["twd"][:, :], wd_dram[0:256, :], waits=[("sr", wd_done)], inc="sr")

    for mb in range(8):
        s.dma("gpsimd", cdnb[0:86, 0:512], cd_dram[:, mb * 512:(mb + 1) * 512],
              waits=[("stn", s.at("stn")), ("sr", wd_done)], inc="sx")
        cdl = s.at("sx")
        for th in range(2):
            for jc in range(NGI):
                buf = jc % 2
                s.dma("gpsimd", wstr[:, buf * 512:(buf + 1) * 512],
                      wd_dram[jc * 128:(jc + 1) * 128, mb * 512:(mb + 1) * 512],
                      waits=[("stn", s.at("stn")), ("sr", wd_done), ("sh", s.at("sh"))], inc="sx")
                s.dma("gpsimd", hdown[:, buf * 512:(buf + 1) * 512],
                      h_dram[jc * 128:(jc + 1) * 128, th * 512:(th + 1) * 512],
                      waits=[("sr", gate_done_sr)], inc="sx")
                dl = s.at("sx")
                def dmm(t_, buf=buf, jc=jc):
                    outs = []
                    for tb4 in range(4):
                        outs.append(t_.matmul(
                            P[tb4][:, 0:512],
                            hdown[:, buf * 512 + tb4 * 128: buf * 512 + (tb4 + 1) * 128],
                            wstr[:, buf * 512:(buf + 1) * 512],
                            start=(jc == 0), stop=False))
                    return outs
                s.emit("tensor", dmm, waits=[("sx", dl), ("sv", s.at("sv")), ("se", x2d)], inc="stn")
            def dcor(t_, mb=mb, th=th):
                outs = []
                for tb4 in range(4):
                    tbg = th * 4 + tb4
                    outs.append(t_.matmul(
                        P[tb4][:, 0:512],
                        ap(x2sb, tbg * 128, [[PS(x2sb), 86], [1, 128]]),
                        ap(cdnb, 0, [[PS(cdnb), 86], [1, 512]]),
                        start=False, stop=True))
                return outs
            s.emit("tensor", dcor, waits=[("se", x2d), ("sx", cdl)], inc="stn")
            dmd = s.at("stn")
            if "dp0" in taps and mb == 0 and th == 0:
                tap_t["tdp"] = nc.dram_tensor("tdp", [128, 512], F32, kind="ExternalOutput")
                def dpcp(sc):
                    return sc.copy(x2f[:, 0:512], P[0][:, 0:512])
                s.emit("scalar", dpcp, waits=[("stn", dmd)], inc="se")
                s.dma("sync", tap_t["tdp"][:, :], x2f[:, 0:512], waits=[("se", s.at("se"))], inc="sr")
            for tb4 in range(4):
                tbg = th * 4 + tb4
                def xtf(t_, mb=mb, tbg=tbg):
                    outs = []
                    for mcc in range(4):
                        mc = mb * 4 + mcc
                        outs.append(t_.transpose(
                            ap(PT, mcc * 128, [[PS(PT), 128], [1, 128]]),
                            ap(xn, mc * T + tbg * 128, [[PS(xn), 128], [1, 128]]),
                            idm2[:, :]))
                    return outs
                s.emit("tensor", xtf, waits=[("se", s.at("se")), ("stn", dmd)], inc="stn")
                xtd = s.at("stn")
                def xte(sc):
                    return sc.copy(ap(wsb2, 0, [[PS(wsb2), 128], [1, 512]]),
                                   ap(PT, 0, [[PS(PT), 128], [1, 512]]))
                s.emit("scalar", xte, waits=[("stn", xtd)], inc="se")
                xted = s.at("se")
                def res2(v, mb=mb, tbg=tbg, tb4=tb4):
                    return v.tensor_tensor(
                        ap(Abig, tbg * 4096 + mb * 512, [[PS(Abig), 128], [1, 512]]),
                        P[tb4][:, 0:512],
                        ap(wsb2, 0, [[PS(wsb2), 128], [1, 512]]),
                        ALU.add)
                s.emit("vector", res2, waits=[("se", xted)], inc="sv")
                r2d = s.at("sv")
                def sq2(sc, tbg=tbg, mb=mb):
                    return sc.activation(ap(w2, 0, [[PS(w2), 128], [1, 512]]),
                                         ap(Abig, tbg * 4096 + mb * 512, [[PS(Abig), 128], [1, 512]]),
                                         AF.Square)
                s.emit("scalar", sq2, waits=[("sv", r2d)], inc="se")
                sq2d = s.at("se")
                def st2(v, tbg=tbg, mb=mb):
                    a = v.reduce_sum(stats[:, tbg * 8 + mb: tbg * 8 + mb + 1],
                                     ap(Abig, tbg * 4096 + mb * 512, [[PS(Abig), 128], [1, 512]]),
                                     axis=AX.X)
                    b = v.reduce_sum(stats[:, 64 + tbg * 8 + mb: 64 + tbg * 8 + mb + 1],
                                     ap(w2, 0, [[PS(w2), 128], [1, 512]]), axis=AX.X)
                    return [a, b]
                s.emit("vector", st2, waits=[("se", sq2d)], inc="sv")
    down_sv = s.at("sv")
    if "sum2" in taps:
        tap_t["ts2"] = nc.dram_tensor("ts2", [128, 32768], BF16, kind="ExternalOutput")
        s.dma("sync", tap_t["ts2"][:, :], ap(Abig, 0, [[PS(Abig), 128], [1, 32768]]),
              waits=[("sv", down_sv)], inc="sr")

    def fin1(v):
        return [
            v.reduce_sum(stats[:, 128:136], ap(stats, 0, [[PS(stats), 128], [8, 8], [1, 8]]), axis=AX.X),
            v.reduce_sum(stats[:, 136:144], ap(stats, 64, [[PS(stats), 128], [8, 8], [1, 8]]), axis=AX.X),
        ]
    s.emit("vector", fin1, waits=[("sv", down_sv)], inc="sv")
    def fin2(v):
        o = [v.tensor_scalar(stats[:, 128:136], stats[:, 128:136], cN[:, C_INV4096:C_INV4096 + 1], None, ALU.mult),
             v.tensor_scalar(stats[:, 136:144], stats[:, 136:144], cN[:, C_INV4096:C_INV4096 + 1], None, ALU.mult)]
        v.drain()
        o.append(v.tensor_tensor(stats[:, 160:168], stats[:, 128:136], stats[:, 128:136], ALU.mult))
        v.drain()
        o.append(v.tensor_tensor(stats[:, 160:168], stats[:, 136:144], stats[:, 160:168], ALU.subtract))
        v.drain()
        o.append(v.tensor_scalar(stats[:, 160:168], stats[:, 160:168], cN[:, C_EPS:C_EPS + 1], None, ALU.add))
        return o
    s.emit("vector", fin2, waits=[("sv", s.at("sv"))], inc="sv")
    def fin3(sc):
        return sc.activation(stats[:, 144:152], stats[:, 160:168], AF.Sqrt)
    s.emit("scalar", fin3, waits=[("sv", s.at("sv"))], inc="se")
    def fin4(v):
        r0 = v.reciprocal(stats[:, 144:152], stats[:, 144:152])
        v.drain()
        a = v.tensor_tensor(stats[:, 152:160], stats[:, 128:136], stats[:, 144:152], ALU.mult)
        v.drain()
        b = v.tensor_scalar(stats[:, 152:160], stats[:, 152:160], cN[:, C_NEG1:C_NEG1 + 1], None, ALU.mult)
        return [r0, a, b]
    s.emit("vector", fin4, waits=[("se", s.at("se"))], inc="sv")
    ln2d = s.at("sv")

    for tb in range(8):
        for piece in range(4):
            def outp(v, tb=tb, piece=piece):
                return v.tensor_scalar(
                    x2f[:, :],
                    ap(Abig, tb * 4096 + piece * 1024, [[PS(Abig), 128], [1, 1024]]),
                    stats[:, 144 + tb:145 + tb], stats[:, 152 + tb:153 + tb],
                    ALU.mult, ALU.add)
            s.emit("vector", outp, waits=[("sv", ln2d), ("sr", s.at("sr"))], inc="sv")
            od = s.at("sv")
            s.dma("sync", y[tb * 128:(tb + 1) * 128, piece * 1024:(piece + 1) * 1024],
                  x2f[:, :], waits=[("sv", od)], inc="sr")
    final_sr = s.at("sr")
    def finw(e):
        return e.nop()
    s.emit("sync", finw, waits=[("sr", final_sr)])

    block = st.enter_context(nc.Block())
    s.build(block)
    st.close()
    s._stack.close()
    return nc, tap_t


def _host_stage(inputs):
    d = {}
    f32 = np.float32

    def restage_w(w):
        wt = np.ascontiguousarray(np.asarray(w, f32).T)
        return np.ascontiguousarray(
            wt.reshape(NJC, 128, NJC, 128).transpose(2, 1, 0, 3).reshape(NJC * 128, HID))

    d["wqt"] = restage_w(inputs["L0_attn_Wq"])
    d["wkt"] = restage_w(inputs["L0_attn_Wk"])
    d["wvt"] = restage_w(inputs["L0_attn_Wv"])
    wot_nat = np.ascontiguousarray(np.asarray(inputs["L0_attn_Wo"], f32).T)
    hp = np.arange(4096) // 64
    dd = np.arange(4096) % 64
    imap = (2 * (hp % 32) + hp // 32) * 64 + dd
    wot_p = wot_nat[imap]
    d["wot"] = np.ascontiguousarray(
        wot_p.reshape(NJC, 128, NJC, 128).transpose(2, 1, 0, 3).reshape(NJC * 128, HID))

    def tile3(a, b, c):
        out = np.zeros((128, 3), f32)
        out[:, 0] = np.tile(np.asarray(a, f32), 2)
        out[:, 1] = np.tile(np.asarray(b, f32), 2)
        out[:, 2] = np.tile(np.asarray(c, f32), 2)
        return out
    d["maskq"] = tile3(inputs["L0_attn_mu_q"], inputs["L0_attn_sigma_q"], inputs["L0_attn_eps_q"])
    d["maskkv"] = tile3(inputs["L0_attn_mu_kv"], inputs["L0_attn_sigma_kv"], inputs["L0_attn_eps_kv"])

    cv = np.zeros((128, 8), f32)
    cv[:, C_EXPB] = -4.0
    cv[:, C_INV4096] = 1.0 / 4096.0
    cv[:, C_EPS] = 1e-5
    cv[:, C_NEG1] = -1.0
    cv[:, C_ONE] = 1.0
    cv[:, C_NEG8] = -8.0
    d["consts"] = cv

    idm = np.zeros((128, 128), bfdt)
    idm[0:64, 0:64] = np.eye(64)
    idm[64:128, 0:64] = np.eye(64)
    d["idmat"] = idm
    d["idmat2"] = np.eye(128).astype(bfdt)
    o1 = np.zeros((128, 2), bfdt)
    o1[:, 0] = 1
    d["ones128"] = o1
    d["onesrow"] = np.ones((1, 128), bfdt)
    d["onesrow32"] = np.ones((1, 128), f32)
    w32_ = np.zeros((128, 63), bfdt)
    w32_[:, 31] = 1
    d["wide32"] = w32_
    w86_ = np.zeros((128, 171), bfdt)
    w86_[:, 85] = 1
    d["wide86"] = w86_
    d["shiftpp"] = np.tile(np.array([0, 4], np.uint8), 64).reshape(128, 1)

    def bytes_rep(qw):
        K8, N = qw.shape
        b = np.ascontiguousarray(qw).view(np.uint8).reshape(K8, N, 4).transpose(0, 2, 1).reshape(K8 * 4, N)
        return np.repeat(b, 2, axis=0)

    gb = bytes_rep(np.asarray(inputs["L0_ffn_gate_qweight"], np.int32))
    ub = bytes_rep(np.asarray(inputs["L0_ffn_up_qweight"], np.int32))
    db = bytes_rep(np.asarray(inputs["L0_ffn_down_qweight"], np.int32))
    d["gbytes"] = np.ascontiguousarray(
        gb.reshape(NJC, 128, NGI, 128).transpose(2, 1, 0, 3).reshape(NGI * 128, NJC * 128))
    d["ubytes"] = np.ascontiguousarray(
        ub.reshape(NJC, 128, NGI, 128).transpose(2, 1, 0, 3).reshape(NGI * 128, NJC * 128))
    d["dbytes"] = np.ascontiguousarray(db)

    gs = np.asarray(inputs["L0_ffn_gate_scales"], f32)
    us = np.asarray(inputs["L0_ffn_up_scales"], f32)
    ds = np.asarray(inputs["L0_ffn_down_scales"], f32)
    d["gscale"] = np.ascontiguousarray(gs.reshape(NJC, NGI, 128).transpose(1, 0, 2).reshape(NGI, NJC * 128))
    d["uscale"] = np.ascontiguousarray(us.reshape(NJC, NGI, 128).transpose(1, 0, 2).reshape(NGI, NJC * 128))
    d["dscale"] = np.ascontiguousarray(ds)
    d["gsc_nat"] = np.ascontiguousarray(gs)
    d["usc_nat"] = np.ascontiguousarray(us)
    d["gzer"] = np.ascontiguousarray(np.asarray(inputs["L0_ffn_gate_qzeros"], np.int32))
    d["uzer"] = np.ascontiguousarray(np.asarray(inputs["L0_ffn_up_qzeros"], np.int32))
    d["dzer"] = np.ascontiguousarray(np.asarray(inputs["L0_ffn_down_qzeros"], np.int32))
    return d


def kernel(**inputs):
    x = np.ascontiguousarray(np.asarray(inputs["L0_attn_hidden_states"], np.float32).reshape(-1, HID))
    shared = _host_stage(inputs)
    nc, _ = build_nc()
    in_maps = []
    for c in range(NCORE):
        m = dict(shared)
        m["xin"] = np.ascontiguousarray(x[c * T:(c + 1) * T])
        in_maps.append(m)
    res = run_bass_kernel_spmd(nc, in_maps, core_ids=list(range(NCORE)))
    out = np.concatenate([r["y"] for r in res.results], axis=0)
    return np.ascontiguousarray(out.reshape(2, 4096, HID))


# version 03
# speedup vs baseline: 1.0325x; 1.0325x over previous
import numpy as np
import ml_dtypes
from contextlib import ExitStack

import concourse.bass as bass
import concourse.mybir as mybir
from concourse.bass_utils import run_bass_kernel_spmd

F32 = mybir.dt.float32
BF16 = mybir.dt.bfloat16
U8 = mybir.dt.uint8
I32 = mybir.dt.int32
AF = mybir.ActivationFunctionType
ALU = mybir.AluOpType
AX = mybir.AxisListType
bfdt = ml_dtypes.bfloat16

NCORE = 8
T = 1024
HID = 4096
INTER = 11008
NJC = 32
NGI = 86
TP = 256
NPH = 4
SUB = 64
NSUB = 4

C_EXPB, C_INV4096, C_EPS, C_NEG1, C_ONE, C_NEG8 = 0, 1, 2, 3, 4, 5

ENGINES = ("tensor", "vector", "scalar", "gpsimd", "sync")


class Sched:
    def __init__(self, nc):
        self.nc = nc
        self.q = {e: [] for e in ENGINES}
        self.cnt = {}
        self.sems = {}
        self._stack = ExitStack()

    def sem(self, name):
        self.sems[name] = self._stack.enter_context(self.nc.semaphore(name))
        self.cnt[name] = 0

    def at(self, name):
        return self.cnt[name]

    def emit(self, eng, fn, waits=(), inc=None, amt=1):
        if inc is not None:
            self.cnt[inc] += amt
        self.q[eng].append((list(waits), fn, inc, amt))

    def dma(self, eng, out, in_, waits=(), inc=None):
        def fn(e, out=out, in_=in_):
            return e.dma_start(out, in_)
        self.emit(eng, fn, waits=waits, inc=inc, amt=16)

    def build(self, block):
        for eng in ENGINES:
            items = self.q[eng]
            if not items:
                continue

            def body(e, items=items):
                last_wait = {}
                for waits, fn, inc, amt in items:
                    for sname, v in waits:
                        if v > 0 and last_wait.get(sname, -1) < v:
                            e.wait_ge(self.sems[sname], v)
                            last_wait[sname] = v
                    ins = fn(e)
                    if inc is not None:
                        if isinstance(ins, list):
                            ins = ins[-1]
                        ins.then_inc(self.sems[inc], amt)
            getattr(block, eng)(body)


def ap(t, offset, dims):
    return bass.AP(t, offset, dims)


def build_nc(taps=(), stop_after="full", nph=NPH, attn_stage="full"):
    nc = bass.Bass("TRN2", target_bir_lowering=False)

    xin = nc.dram_tensor("xin", [T, HID], F32, kind="ExternalInput")
    wqt = nc.dram_tensor("wqt", [NJC * 128, HID], F32, kind="ExternalInput")
    wkt = nc.dram_tensor("wkt", [NJC * 128, HID], F32, kind="ExternalInput")
    wvt = nc.dram_tensor("wvt", [NJC * 128, HID], F32, kind="ExternalInput")
    wot = nc.dram_tensor("wot", [NJC * 128, HID], F32, kind="ExternalInput")
    maskq = nc.dram_tensor("maskq", [128, 3], F32, kind="ExternalInput")
    maskkv = nc.dram_tensor("maskkv", [128, 3], F32, kind="ExternalInput")
    consts = nc.dram_tensor("consts", [128, 8], F32, kind="ExternalInput")
    idmat = nc.dram_tensor("idmat", [128, 128], BF16, kind="ExternalInput")
    idmat2 = nc.dram_tensor("idmat2", [128, 128], BF16, kind="ExternalInput")
    ones128 = nc.dram_tensor("ones128", [128, 2], BF16, kind="ExternalInput")
    onesrow = nc.dram_tensor("onesrow", [1, 128], BF16, kind="ExternalInput")
    onesrow32 = nc.dram_tensor("onesrow32", [1, 128], F32, kind="ExternalInput")
    wide32 = nc.dram_tensor("wide32", [128, 63], BF16, kind="ExternalInput")
    wide86 = nc.dram_tensor("wide86", [128, 171], BF16, kind="ExternalInput")
    gbytes = nc.dram_tensor("gbytes", [NGI * 128, NJC * 128], U8, kind="ExternalInput")
    ubytes = nc.dram_tensor("ubytes", [NGI * 128, NJC * 128], U8, kind="ExternalInput")
    dbytes = nc.dram_tensor("dbytes", [NGI * 128, HID], U8, kind="ExternalInput")
    gscale = nc.dram_tensor("gscale", [NGI, NJC * 128], F32, kind="ExternalInput")
    uscale = nc.dram_tensor("uscale", [NGI, NJC * 128], F32, kind="ExternalInput")
    dscale = nc.dram_tensor("dscale", [NGI, HID], F32, kind="ExternalInput")
    gsc_nat = nc.dram_tensor("gsc_nat", [NJC, INTER], F32, kind="ExternalInput")
    usc_nat = nc.dram_tensor("usc_nat", [NJC, INTER], F32, kind="ExternalInput")
    gzer = nc.dram_tensor("gzer", [NJC, INTER // 8], I32, kind="ExternalInput")
    uzer = nc.dram_tensor("uzer", [NJC, INTER // 8], I32, kind="ExternalInput")
    dzer = nc.dram_tensor("dzer", [NGI, HID // 8], I32, kind="ExternalInput")
    shiftpp = nc.dram_tensor("shiftpp", [128, 1], U8, kind="ExternalInput")

    y = nc.dram_tensor("y", [T, HID], F32, kind="ExternalOutput")

    h_dram = nc.dram_tensor("h_scr", [NGI * 128, T], BF16, kind="Internal")
    wd_dram = nc.dram_tensor("wd_scr", [NGI * 128, HID], BF16, kind="Internal")
    cg_dram = nc.dram_tensor("cg_scr", [32, INTER], BF16, kind="Internal")
    cu_dram = nc.dram_tensor("cu_scr", [32, INTER], BF16, kind="Internal")
    cd_dram = nc.dram_tensor("cd_scr", [NGI, HID], BF16, kind="Internal")

    tap_t = {}
    if "qkv" in taps:
        for n in ("tq", "tk", "tv"):
            tap_t[n] = nc.dram_tensor(n, [128, 8192], BF16, kind="ExternalOutput")
    if "attn" in taps:
        tap_t["to"] = nc.dram_tensor("to", [128, 8192], BF16, kind="ExternalOutput")
    if "xn" in taps:
        tap_t["txn"] = nc.dram_tensor("txn", [128, NJC * T], BF16, kind="ExternalOutput")
    if "h" in taps:
        tap_t["th"] = nc.dram_tensor("th", [NGI * 128, T], BF16, kind="ExternalOutput")
    if "wd" in taps:
        tap_t["twd"] = nc.dram_tensor("twd", [256, HID], BF16, kind="ExternalOutput")

    st = ExitStack()
    sb = lambda name, shape, dt: st.enter_context(nc.sbuf_tensor(name, shape, dt))
    psal = lambda name, shape, dt: st.enter_context(nc.psum_tensor(name, shape, dt))

    xn = sb("xn", [128, NJC * T], BF16)
    xfm = sb("xfm", [128, NJC * TP], BF16)
    Abig = sb("Abig", [128, 32768], BF16)
    wstr = sb("wstr", [128, 2 * HID], BF16)
    bstr = sb("bstr", [128, 2 * HID], U8)
    rstr = sb("rstr", [128, HID], BF16)
    soar = sb("soar", [128, 4096], BF16)
    wsb2 = sb("wsb2", [128, 4096], BF16)
    qsub = sb("qsub", [128, 2048], BF16)
    ksub = sb("ksub", [128, 2048], BF16)
    vsub = sb("vsub", [128, 2048], BF16)
    zsb = sb("zsb", [128, 32], F32)
    rzbf = sb("rzbf", [128, 32], BF16)
    addr = lambda t_: nc.lookup_mloc(t_).addr
    w2 = nc.alloc_sbuf_tensor_at("w2", [128, 4096], BF16, offset=addr(bstr))
    cstr = nc.alloc_sbuf_tensor_at("cstr", [128, HID], BF16, offset=addr(Abig))
    sum32 = nc.alloc_sbuf_tensor_at("sum32", [128, 8192], F32, offset=addr(Abig))
    cbuf = nc.alloc_sbuf_tensor_at("cbuf", [128, 2048], BF16, offset=addr(xfm))
    assert addr(ksub) == addr(qsub) + 4096, (addr(qsub), addr(ksub))
    xtm = nc.alloc_sbuf_tensor_at("xtm", [128, 4096], BF16, offset=addr(qsub))
    ustr = nc.alloc_sbuf_tensor_at("ustr", [128, HID], U8, offset=addr(vsub))
    hdown = nc.alloc_sbuf_tensor_at("hdown", [128, 1024], BF16, offset=addr(soar))
    zztmp = nc.alloc_sbuf_tensor_at("zztmp", [128, 2112], F32, offset=addr(wstr))
    lnrow = nc.alloc_sbuf_tensor_at("lnrow", [128, 768], F32, offset=addr(soar))
    lnbf = nc.alloc_sbuf_tensor_at("lnbf", [128, 512], BF16, offset=addr(soar) + 3072)
    stats = nc.alloc_sbuf_tensor_at("stats", [128, 192], F32, offset=addr(rstr))
    wts = nc.alloc_sbuf_tensor_at("wts", [128, 2048], BF16, offset=addr(rstr))
    vts = nc.alloc_sbuf_tensor_at("vts", [128, 2048], BF16, offset=addr(rstr) + 4096)
    x2f = nc.alloc_sbuf_tensor_at("x2f", [128, 1024], F32, offset=addr(soar) + 4096)
    cdnb = nc.alloc_sbuf_tensor_at("cdnb", [128, 512], BF16, offset=addr(vsub))
    x2sb = nc.alloc_sbuf_tensor_at("x2sb", [128, T], BF16, offset=addr(wsb2) + 4096)
    xsb = nc.alloc_sbuf_tensor_at("xsb", [128, T], BF16, offset=addr(wsb2) + 6144)
    cN = sb("cN", [128, 8], F32)
    nc.const_aps.register(F32, -8.0, cN[:, C_NEG8:C_NEG8 + 1])
    mq8 = sb("mq8", [128, 4], F32)
    mkv = sb("mkv", [128, 4], F32)
    idm = sb("idm", [128, 128], BF16)
    idm2 = sb("idm2", [128, 128], BF16)
    ones1 = sb("ones1", [128, 2], BF16)
    onesr = sb("onesr", [1, 128], BF16)
    onesr32 = sb("onesr32", [1, 128], F32)
    w32 = sb("w32", [128, 63], BF16)
    w86 = sb("w86", [128, 171], BF16)
    shf = sb("shf", [128, 1], U8)

    P = [psal(f"P{i}", [128, 512], F32) for i in range(7)]
    PT = psal("PT", [128, 1024], BF16)

    PSx = {}
    def PS(t_):
        k = id(t_)
        if k not in PSx:
            PSx[k] = t_.ap().ap[0][0]
        return PSx[k]

    s = Sched(nc)
    for name in ("sx", "sw", "sr", "se", "sv", "stn", "sh"):
        s.sem(name)

    QOF, KOF, VOF, OOF = 0, 8192, 16384, 24576

    for dst, src in ((cN, consts), (idm, idmat), (idm2, idmat2), (ones1, ones128),
                     (w32, wide32), (w86, wide86), (shf, shiftpp)):
        s.dma("gpsimd", dst[:, :], src[:, :], inc="sx")
    s.dma("gpsimd", onesr[:, :], onesrow[:, :], inc="sx")
    s.dma("gpsimd", onesr32[:, :], onesrow32[:, :], inc="sx")
    s.dma("gpsimd", mq8[:, 0:3], maskq[:, :], inc="sx")
    s.dma("gpsimd", mkv[:, 0:3], maskkv[:, :], inc="sx")
    pro_ld = s.at("sx")

    def mk_masks(v):
        a = v.tensor_tensor(mq8[:, 3:4], mq8[:, 2:3], mq8[:, 1:2], ALU.mult)
        b = v.tensor_tensor(mkv[:, 3:4], mkv[:, 2:3], mkv[:, 1:2], ALU.mult)
        v.drain()
        c = v.tensor_tensor(mq8[:, 3:4], mq8[:, 3:4], mq8[:, 0:1], ALU.add)
        d = v.tensor_tensor(mkv[:, 3:4], mkv[:, 3:4], mkv[:, 0:1], ALU.add)
        return [a, b, c, d]
    s.emit("vector", mk_masks, waits=[("sx", pro_ld)], inc="sv")

    def mk_sig(sc):
        return [sc.activation(mq8[:, 0:1], mq8[:, 3:4], AF.Sigmoid),
                sc.activation(mkv[:, 0:1], mkv[:, 3:4], AF.Sigmoid)]
    s.emit("scalar", mk_sig, waits=[("sv", s.at("sv"))], inc="se")

    def mk_mq8(v):
        return v.tensor_scalar(mq8[:, 0:1], mq8[:, 0:1], 0.125, None, ALU.mult)
    s.emit("vector", mk_mq8, waits=[("se", s.at("se"))], inc="sv")
    masks_sv = s.at("sv")
    masks_se = s.at("se")

    if "masks" in taps:
        tap_t["tmask"] = nc.dram_tensor("tmask", [128, 8], F32, kind="ExternalOutput")
        s.dma("sync", tap_t["tmask"][:, 0:4], mq8[:, 0:4],
              waits=[("sv", masks_sv), ("se", masks_se)], inc="sr")
        s.dma("sync", tap_t["tmask"][:, 4:8], mkv[:, 0:4],
              waits=[("sv", masks_sv), ("se", masks_se)], inc="sr")

    projs = [("q", wqt, QOF, True), ("k", wkt, KOF, False), ("v", wvt, VOF, False)]

    for ph in range(nph):
        t0 = ph * TP
        for g in range(2):
            s.dma("gpsimd", xtm[:, 0:4096], xin[t0 + g * 128: t0 + (g + 1) * 128, :],
                  waits=[("sr", s.at("sr")), ("se", s.at("se"))], inc="sx")
            ld = s.at("sx")
            for jc in range(NJC):
                dst = ap(xfm, jc * TP + g * 128, [[PS(xfm), 128], [1, 128]])
                def fn(e, dst=dst, jc=jc):
                    return e.dma_start(dst, xtm[:, jc * 128:(jc + 1) * 128], transpose=True)
                s.emit("sync", fn, waits=[("sx", ld)], inc="sr", amt=16)
        xfm_ready = s.at("sr")

        for (pname, wsrc, dof, isq) in projs:
            for ic in range(NJC):
                buf = ic % 2
                s.dma("gpsimd", wstr[:, buf * HID:(buf + 1) * HID],
                      wsrc[ic * 128:(ic + 1) * 128, :],
                      waits=[("stn", s.at("stn"))], inc="sw")
                wld = s.at("sw")
                pb = P[ic % 2]
                def mmfn(t_, buf=buf, pb=pb):
                    outs = []
                    for jc in range(NJC):
                        outs.append(t_.matmul(
                            pb[:, 0:TP],
                            wstr[:, buf * HID + jc * 128: buf * HID + (jc + 1) * 128],
                            ap(xfm, jc * TP, [[PS(xfm), 128], [1, TP]]),
                            start=(jc == 0), stop=(jc == NJC - 1)))
                    return outs
                s.emit("tensor", mmfn,
                       waits=[("sw", wld), ("sr", xfm_ready), ("se", s.at("se"))],
                       inc="stn")
                mmd = s.at("stn")
                scale_ap = mq8[:, 0:1] if isq else mkv[:, 0:1]
                dst = ap(Abig, dof + ic * TP, [[PS(Abig), 128], [1, TP]])
                def evfn(sc, dst=dst, pb=pb, scale_ap=scale_ap):
                    return sc.activation(dst, pb[:, 0:TP], AF.Copy, scale=scale_ap)
                s.emit("scalar", evfn,
                       waits=[("stn", mmd), ("sv", masks_sv), ("sr", s.at("sr"))],
                       inc="se")
        qkv_ready = s.at("sr")

        if "qkv" in taps and ph == 0:
            evall = s.at("se")
            s.dma("sync", tap_t["tq"][:, :], ap(Abig, QOF, [[PS(Abig), 128], [1, 8192]]),
                  waits=[("se", evall)], inc="sr")
            s.dma("sync", tap_t["tk"][:, :], ap(Abig, KOF, [[PS(Abig), 128], [1, 8192]]),
                  waits=[("se", evall)], inc="sr")
            s.dma("sync", tap_t["tv"][:, :], ap(Abig, VOF, [[PS(Abig), 128], [1, 8192]]),
                  waits=[("se", evall)], inc="sr")
            qkv_ready = s.at("sr")

        if stop_after == "qkv":
            continue

        for sub in range(8):
            sl0 = sub * 32
            for (dof, dstt) in ((QOF, qsub), (KOF, ksub), (VOF, vsub)):
                for hsel in range(2):
                    srcap = ap(Abig, dof + hsel * 64 * PS(Abig) + sl0,
                               [[PS(Abig), 64], [TP, 32], [1, 32]])
                    dstap = ap(dstt, hsel * 1024, [[PS(dstt), 64], [32, 32], [1, 32]])
                    def rfn(e, dstap=dstap, srcap=srcap):
                        return e.dma_start(dstap, srcap)
                    s.emit("sync", rfn,
                           waits=[("sr", qkv_ready), ("stn", s.at("stn")), ("se", s.at("se"))],
                           inc="sr", amt=16)
            sub_rdy = s.at("sr")
            if attn_stage == "rearr":
                continue
            for bk in range(4):
                pb = P[2 + (bk % 2)]
                def scfn(t_, pb=pb, bk=bk):
                    outs = []
                    for j in range(8):
                        tl = bk * 8 + j
                        lh = ap(qsub, tl, [[PS(qsub), 64], [32, 64]])
                        rh = ap(ksub, tl, [[PS(ksub), 64], [32, 64]])
                        outs.append(t_.matmul(pb[0:64, j * 64:(j + 1) * 64], lh, rh,
                                              start=True, stop=True))
                    return outs
                s.emit("tensor", scfn,
                       waits=[("sr", sub_rdy), ("se", s.at("se")), ("sv", s.at("sv"))],
                       inc="stn")
                scd = s.at("stn")
                if attn_stage == "mm":
                    continue
                srcp = ap(pb, 0, [[PS(pb), 64], [64, 8], [1, 64]])
                dstw = ap(wsb2, bk * 8 * 64, [[PS(wsb2), 64], [64, 8], [1, 64]])
                def exfn(sc, dstw=dstw, srcp=srcp):
                    return sc.activation(dstw, srcp, AF.Exp, bias=cN[0:64, C_EXPB:C_EXPB + 1])
                s.emit("scalar", exfn, waits=[("stn", scd)], inc="se")
                exd = s.at("se")
                def zfn(v, bk=bk):
                    srcw = ap(wsb2, bk * 8 * 64, [[PS(wsb2), 64], [64, 8], [1, 64]])
                    return v.reduce_sum(zsb[0:64, bk * 8:(bk + 1) * 8], srcw, axis=AX.X)
                s.emit("vector", zfn, waits=[("se", exd)], inc="sv")
            if attn_stage == "mm":
                continue
            expd = s.at("se")
            zred = s.at("sv")
            def rzfn(v):
                a_ = v.reciprocal(zsb[0:64, 0:32], zsb[0:64, 0:32])
                v.drain()
                b_ = v.tensor_copy(rzbf[0:64, 0:32], zsb[0:64, 0:32])
                return [a_, b_]
            s.emit("vector", rzfn, waits=[("sv", zred)], inc="sv")
            def w2fn(v):
                i0 = ap(wsb2, 0, [[PS(wsb2), 64], [64, 32], [1, 64]])
                i1 = ap(rzbf, 0, [[PS(rzbf), 64], [1, 32], [0, 64]])
                o = ap(w2, 0, [[PS(w2), 64], [64, 32], [1, 64]])
                return v.tensor_tensor(o, i0, i1, ALU.mult)
            s.emit("vector", w2fn, waits=[("sv", s.at("sv")), ("se", expd)], inc="sv")
            w2d = s.at("sv")
            if attn_stage == "scores":
                continue

            for grp in range(4):
                def trfn(t_, grp=grp):
                    outs = []
                    for i in range(8):
                        tl = grp * 8 + i
                        win = ap(w2, tl * 64, [[PS(w2), 64], [1, 64]])
                        wout = ap(PT, i * 64, [[PS(PT), 64], [1, 64]])
                        outs.append(t_.transpose(wout, win, idm[0:64, 0:64]))
                        vin = ap(vsub, tl, [[PS(vsub), 64], [32, 64]])
                        vout = ap(PT, 512 + i * 64, [[PS(PT), 64], [1, 64]])
                        outs.append(t_.transpose(vout, vin, idm[0:64, 0:64]))
                    return outs
                s.emit("tensor", trfn, waits=[("sv", w2d), ("se", s.at("se"))], inc="stn")
                trd = s.at("stn")
                def tefn(sc, grp=grp):
                    outs = []
                    wsrc = ap(PT, 0, [[PS(PT), 64], [64, 8], [1, 64]])
                    wdst = ap(wts, grp * 8, [[PS(wts), 64], [1, 8], [32, 64]])
                    outs.append(sc.copy(wdst, wsrc))
                    vsrc = ap(PT, 512, [[PS(PT), 64], [64, 8], [1, 64]])
                    vdst = ap(vts, grp * 8, [[PS(vts), 64], [1, 8], [32, 64]])
                    outs.append(sc.copy(vdst, vsrc))
                    return outs
                s.emit("scalar", tefn, waits=[("stn", trd)], inc="se")
            trevd = s.at("se")
            if attn_stage == "trans":
                continue

            for bko in range(4):
                pb = P[4 + (bko % 2)]
                def omfn(t_, pb=pb, bko=bko):
                    outs = []
                    for j in range(8):
                        tl = bko * 8 + j
                        lh = ap(vts, tl, [[PS(vts), 64], [32, 64]])
                        rh = ap(wts, tl, [[PS(wts), 64], [32, 64]])
                        outs.append(t_.matmul(pb[0:64, j * 64:(j + 1) * 64], lh, rh,
                                              start=True, stop=True))
                    return outs
                s.emit("tensor", omfn, waits=[("se", s.at("se"))], inc="stn")
                omd = s.at("stn")
                srcp = ap(pb, 0, [[PS(pb), 64], [64, 8], [1, 64]])
                dsto = ap(soar, bko * 8, [[PS(soar), 64], [1, 8], [32, 64]])
                def oefn(sc, dsto=dsto, srcp=srcp):
                    return sc.copy(dsto, srcp)
                s.emit("scalar", oefn, waits=[("stn", omd)], inc="se")
            oevd = s.at("se")
            for par in range(2):
                srcap = ap(soar, par * 32, [[PS(soar), 64], [64, 32], [1, 32]])
                dstap = ap(Abig, OOF + par * 64 * PS(Abig) + sl0, [[PS(Abig), 64], [TP, 32], [1, 32]])
                def rofn(e, dstap=dstap, srcap=srcap):
                    return e.dma_start(dstap, srcap)
                s.emit("sync", rofn, waits=[("se", oevd)], inc="sr", amt=16)
        attn_done = s.at("sr")

        if "attn" in taps and ph == 0:
            s.dma("sync", tap_t["to"][:, :], ap(Abig, OOF, [[PS(Abig), 128], [1, 8192]]),
                  waits=[("sr", attn_done)], inc="sr")
            attn_done = s.at("sr")

        if stop_after == "attn":
            continue

        for mc in range(NJC):
            buf = mc % 2
            s.dma("gpsimd", wstr[:, buf * HID:(buf + 1) * HID],
                  wot[mc * 128:(mc + 1) * 128, :],
                  waits=[("stn", s.at("stn"))], inc="sw")
            wld = s.at("sw")
            pb = P[mc % 2]
            def wofn(t_, buf=buf, pb=pb):
                outs = []
                for ac in range(NJC):
                    outs.append(t_.matmul(
                        pb[:, 0:TP],
                        wstr[:, buf * HID + ac * 128: buf * HID + (ac + 1) * 128],
                        ap(Abig, OOF + ac * TP, [[PS(Abig), 128], [1, TP]]),
                        start=(ac == 0), stop=(ac == NJC - 1)))
                return outs
            s.emit("tensor", wofn,
                   waits=[("sw", wld), ("sr", attn_done), ("sv", s.at("sv"))], inc="stn")
            wod = s.at("stn")
            def refn(v, mc=mc, pb=pb):
                return v.tensor_tensor(
                    sum32[:, mc * TP:(mc + 1) * TP],
                    pb[:, 0:TP],
                    ap(xfm, mc * TP, [[PS(xfm), 128], [1, TP]]),
                    ALU.add)
            s.emit("vector", refn, waits=[("stn", wod)], inc="sv")
        sumd = s.at("sv")

        for g8 in range(4):
            def castf(sc, g8=g8):
                outs = []
                for i in range(8):
                    mc = g8 * 8 + i
                    outs.append(sc.copy(qsub[:, i * 256:(i + 1) * 256],
                                        sum32[:, mc * TP:(mc + 1) * TP]))
                    outs.append(sc.activation(ksub[:, i * 256:(i + 1) * 256],
                                              sum32[:, mc * TP:(mc + 1) * TP], AF.Square))
                return outs
            s.emit("scalar", castf, waits=[("sv", sumd), ("stn", s.at("stn")), ("sr", s.at("sr"))], inc="se")
            cd_ = s.at("se")
            def stf2(t_, g8=g8):
                outs = []
                for i in range(8):
                    mc = g8 * 8 + i
                    outs.append(t_.matmul(P[2][0:1, 0:TP], ones1[:, 0:1],
                                          qsub[:, i * 256:(i + 1) * 256],
                                          start=(mc == 0), stop=(mc == NJC - 1)))
                    outs.append(t_.matmul(P[3][0:1, 0:TP], ones1[:, 0:1],
                                          ksub[:, i * 256:(i + 1) * 256],
                                          start=(mc == 0), stop=(mc == NJC - 1)))
                return outs
            s.emit("tensor", stf2, waits=[("se", cd_), ("sv", sumd)], inc="stn")
        std_ = s.at("stn")
        def rowfn(v):
            o = [v.tensor_scalar(lnrow[0:1, 0:256], P[2][0:1, 0:TP],
                                 cN[0:1, C_INV4096:C_INV4096 + 1], None, ALU.mult),
                 v.tensor_scalar(lnrow[0:1, 256:512], P[3][0:1, 0:TP],
                                 cN[0:1, C_INV4096:C_INV4096 + 1], None, ALU.mult)]
            v.drain()
            o.append(v.tensor_tensor(lnrow[0:1, 512:768], lnrow[0:1, 0:256], lnrow[0:1, 0:256], ALU.mult))
            v.drain()
            o.append(v.tensor_tensor(lnrow[0:1, 512:768], lnrow[0:1, 256:512], lnrow[0:1, 512:768], ALU.subtract))
            v.drain()
            o.append(v.tensor_scalar(lnrow[0:1, 512:768], lnrow[0:1, 512:768],
                                     cN[0:1, C_EPS:C_EPS + 1], None, ALU.add))
            return o
        s.emit("vector", rowfn, waits=[("stn", std_)], inc="sv")
        def rsq(sc):
            return sc.activation(lnrow[0:1, 256:512], lnrow[0:1, 512:768], AF.Sqrt)
        s.emit("scalar", rsq, waits=[("sv", s.at("sv"))], inc="se")
        def bfn(v):
            o = [v.reciprocal(lnrow[0:1, 256:512], lnrow[0:1, 256:512])]
            v.drain()
            o.append(v.tensor_tensor(lnrow[0:1, 0:256], lnrow[0:1, 0:256], lnrow[0:1, 256:512], ALU.mult))
            v.drain()
            o.append(v.tensor_scalar(lnrow[0:1, 0:256], lnrow[0:1, 0:256],
                                     cN[0:1, C_NEG1:C_NEG1 + 1], None, ALU.mult))
            return o
        s.emit("vector", bfn, waits=[("se", s.at("se"))], inc="sv")
        rowd = s.at("sv")
        def bcfn(t_):
            return [
                t_.matmul(P[4][:, 0:TP], onesr32[0:1, :], lnrow[0:1, 256:512], start=True, stop=True),
                t_.matmul(P[5][:, 0:TP], onesr32[0:1, :], lnrow[0:1, 0:256], start=True, stop=True),
            ]
        s.emit("tensor", bcfn, waits=[("sv", rowd), ("se", s.at("se"))], inc="stn")
        bcd = s.at("stn")
        def apfn(v, ph=ph):
            outs = []
            for mc2 in range(0, NJC, 2):
                t0 = x2f[:, 0:256]
                t1 = x2f[:, 256:512]
                outs.append(v.tensor_tensor(
                    t0, sum32[:, mc2 * TP:(mc2 + 1) * TP], P[4][:, 0:TP], ALU.mult))
                outs.append(v.tensor_tensor(
                    t1, sum32[:, (mc2 + 1) * TP:(mc2 + 2) * TP], P[4][:, 0:TP], ALU.mult))
                v.drain()
                outs.append(v.tensor_tensor(
                    ap(xn, mc2 * T + ph * TP, [[PS(xn), 128], [1, TP]]),
                    t0, P[5][:, 0:TP], ALU.add))
                outs.append(v.tensor_tensor(
                    ap(xn, (mc2 + 1) * T + ph * TP, [[PS(xn), 128], [1, TP]]),
                    t1, P[5][:, 0:TP], ALU.add))
                v.drain()
            return outs
        s.emit("vector", apfn, waits=[("stn", bcd)], inc="sv")
        lastap = s.at("sv")
        def nopf(e):
            return e.nop()
        s.emit("sync", nopf, waits=[("sv", lastap)], inc="sr", amt=1)

    xn_sv = s.at("sv")

    if stop_after == "qkv" or stop_after == "attn":
        fin = s.at("sr")
        def earlyw(e):
            return e.nop()
        s.emit("sync", earlyw, waits=[("sr", fin)])
        block = st.enter_context(nc.Block())
        s.build(block)
        st.close()
        s._stack.close()
        return nc, tap_t

    if "xn" in taps:
        s.dma("sync", tap_t["txn"][:, :], xn[:, :], waits=[("sv", xn_sv)], inc="sr")

    if stop_after == "xn":
        fin = s.at("sr")
        def earlyw2(e):
            return e.nop()
        s.emit("sync", earlyw2, waits=[("sr", fin)])
        block = st.enter_context(nc.Block())
        s.build(block)
        st.close()
        s._stack.close()
        return nc, tap_t

    PIECES = [(0, 512), (512, 512), (1024, 352)]
    for (zdr, sdr, cdr) in ((gzer, gsc_nat, cg_dram), (uzer, usc_nat, cu_dram)):
        for (p0, pw) in PIECES:
            s.dma("gpsimd", ap(zztmp, 0, [[PS(zztmp), 32], [1, pw]]).bitcast(I32),
                  zdr[:, p0:p0 + pw],
                  waits=[("sv", s.at("sv")), ("sx", s.at("sx")), ("sr", s.at("sr")),
                         ("se", s.at("se")), ("sh", s.at("sh")), ("stn", s.at("stn"))], inc="sx")
            s.dma("gpsimd", ap(xfm, 0, [[PS(xfm), 32], [1, pw * 8]]),
                  sdr[:, p0 * 8:(p0 + pw) * 8], inc="sx")
            zl = s.at("sx")
            for jj in range(8):
                def plfn(v, jj=jj, pw=pw):
                    o1 = v.tensor_scalar(ap(zztmp, 512, [[PS(zztmp), 32], [1, pw]]).bitcast(I32),
                                         ap(zztmp, 0, [[PS(zztmp), 32], [1, pw]]).bitcast(I32),
                                         4 * jj, 15, ALU.logical_shift_right, ALU.bitwise_and)
                    v.drain()
                    o2 = v.tensor_copy(ap(zztmp, 1024, [[PS(zztmp), 32], [1, pw]]),
                                       ap(zztmp, 512, [[PS(zztmp), 32], [1, pw]]).bitcast(I32))
                    v.drain()
                    o3 = v.tensor_scalar(ap(zztmp, 1024, [[PS(zztmp), 32], [1, pw]]),
                                         ap(zztmp, 1024, [[PS(zztmp), 32], [1, pw]]),
                                         -7.0, -1.0, ALU.add, ALU.mult)
                    v.drain()
                    o4 = v.tensor_copy(ap(w2, 0, [[PS(w2), 32], [1, pw]]),
                                       ap(zztmp, 1024, [[PS(zztmp), 32], [1, pw]]))
                    v.drain()
                    o5 = v.tensor_tensor(ap(w2, 1024, [[PS(w2), 32], [1, pw]]),
                                         ap(w2, 0, [[PS(w2), 32], [1, pw]]),
                                         ap(xfm, jj, [[PS(xfm), 32], [8, pw]]),
                                         ALU.mult)
                    return [o1, o2, o3, o4, o5]
                s.emit("vector", plfn, waits=[("sx", zl), ("sr", s.at("sr"))], inc="sv")
                pld = s.at("sv")
                dst = ap(cdr, p0 * 8 + jj, [[INTER, 32], [8, pw]])
                srcw = ap(w2, 1024, [[PS(w2), 32], [1, pw]])
                def cst_dma(e, dst=dst, srcw=srcw):
                    return e.dma_start(dst, srcw, allow_slow_non_contiguous=True)
                s.emit("sync", cst_dma, waits=[("sv", pld)], inc="sr", amt=16)
    s.dma("gpsimd", ap(xfm, 0, [[PS(xfm), 86], [1, HID]]), dscale[:, :],
          waits=[("sr", s.at("sr")), ("sv", s.at("sv")), ("stn", s.at("stn"))], inc="sx")
    s.dma("gpsimd", ap(zztmp, 0, [[PS(zztmp), 86], [1, 512]]).bitcast(I32), dzer[:, :],
          waits=[("sx", s.at("sx"))], inc="sx")
    dzl = s.at("sx")
    for jj in range(8):
        def pl2fn(v, jj=jj):
            o1 = v.tensor_scalar(ap(zztmp, 512, [[PS(zztmp), 86], [1, 512]]).bitcast(I32),
                                 ap(zztmp, 0, [[PS(zztmp), 86], [1, 512]]).bitcast(I32),
                                 4 * jj, 15, ALU.logical_shift_right, ALU.bitwise_and)
            v.drain()
            o2 = v.tensor_copy(ap(zztmp, 1024, [[PS(zztmp), 86], [1, 512]]),
                               ap(zztmp, 512, [[PS(zztmp), 86], [1, 512]]).bitcast(I32))
            v.drain()
            o3 = v.tensor_scalar(ap(zztmp, 1024, [[PS(zztmp), 86], [1, 512]]),
                                 ap(zztmp, 1024, [[PS(zztmp), 86], [1, 512]]),
                                 -7.0, -1.0, ALU.add, ALU.mult)
            v.drain()
            o4 = v.tensor_copy(ap(w2, 2048, [[PS(w2), 86], [1, 512]]),
                               ap(zztmp, 1024, [[PS(zztmp), 86], [1, 512]]))
            v.drain()
            o5 = v.tensor_tensor(ap(w2, 3072, [[PS(w2), 86], [1, 512]]),
                                 ap(w2, 2048, [[PS(w2), 86], [1, 512]]),
                                 ap(xfm, jj, [[PS(xfm), 86], [8, 512]]),
                                 ALU.mult)
            return [o1, o2, o3, o4, o5]
        s.emit("vector", pl2fn, waits=[("sx", dzl), ("sr", s.at("sr"))], inc="sv")
        pl2d = s.at("sv")
        dstcd = ap(cd_dram, jj, [[HID, 86], [8, 512]])
        srccd = ap(w2, 3072, [[PS(w2), 86], [1, 512]])
        def cddma(e, dstcd=dstcd, srccd=srccd):
            return e.dma_start(dstcd, srccd, allow_slow_non_contiguous=True)
        s.emit("sync", cddma, waits=[("sv", pl2d)], inc="sr", amt=16)
    cdone_sv = s.at("sv")
    cg_done_sr = s.at("sr")

    for tc in range(2):
        def xsfn(t_, tc=tc):
            outs = []
            for kc in range(NJC):
                outs.append(t_.matmul(
                    P[6][0:32, 0:512],
                    w32[:, 31 - kc:63 - kc],
                    ap(xn, kc * T + tc * 512, [[PS(xn), 128], [1, 512]]),
                    start=(kc == 0), stop=(kc == NJC - 1)))
            return outs
        s.emit("tensor", xsfn, waits=[("sv", xn_sv), ("se", s.at("se")), ("stn", s.at("stn"))], inc="stn")
        xsd = s.at("stn")
        def xsev(sc, tc=tc):
            return sc.copy(ap(xsb, tc * 512, [[PS(xsb), 32], [1, 512]]),
                           ap(P[6], 0, [[PS(P[6]), 32], [1, 512]]))
        s.emit("scalar", xsev, waits=[("stn", xsd)], inc="se")
    xs_done = s.at("se")

    for ncx in range(NGI):
        buf = ncx % 2
        if ncx % 8 == 0:
            cw = min(1024, (NGI - ncx) * 128)
            s.dma("sync", cbuf[0:32, 0:cw], cg_dram[:, ncx * 128: ncx * 128 + cw],
                  waits=[("stn", s.at("stn")), ("sr", cg_done_sr), ("sv", cdone_sv)], inc="sx")
            s.dma("sync", cbuf[0:32, 1024:1024 + cw], cu_dram[:, ncx * 128: ncx * 128 + cw],
                  inc="sx")
        cb_ld = s.at("sx")
        s.dma("sync", bstr[:, buf * HID:(buf + 1) * HID],
              gbytes[ncx * 128:(ncx + 1) * 128, :],
              waits=[("sv", s.at("sv"))], inc="sx")
        s.dma("gpsimd", rstr[:, 0:HID],
              ap(gscale, ncx * NJC * 128, [[0, 128], [1, HID]]),
              waits=[("sh", s.at("sh"))], inc="sx")
        gld = s.at("sx")
        def dqg(v, buf=buf, gld=gld):
            return v.tensor_scalar(ustr[:, :], bstr[:, buf * HID:(buf + 1) * HID],
                                   shf[:, 0:1], 15, ALU.logical_shift_right, ALU.bitwise_and)
        s.emit("vector", dqg, waits=[("sx", gld), ("se", s.at("se")), ("stn", s.at("stn"))], inc="sv")
        uq = s.at("sv")
        def cst(sc):
            return sc.activation(cstr[:, :], ustr[:, :], AF.Copy, bias=-8.0)
        s.emit("scalar", cst, waits=[("sv", uq), ("sh", s.at("sh"))], inc="se")
        csd = s.at("se")
        def sclm(g, buf=buf, csd=csd):
            return g.tensor_tensor(wstr[:, buf * HID:(buf + 1) * HID], cstr[:, :],
                                   rstr[:, 0:HID], ALU.mult)
        s.emit("vector", sclm, waits=[("se", csd)], inc="sh")
        wg = s.at("sh")
        for tc in range(2):
            def gmm(t_, buf=buf, tc=tc, ncx=ncx):
                outs = []
                for kc in range(NJC):
                    outs.append(t_.matmul(
                        P[tc][:, 0:512],
                        wstr[:, buf * HID + kc * 128: buf * HID + (kc + 1) * 128],
                        ap(xn, kc * T + tc * 512, [[PS(xn), 128], [1, 512]]),
                        start=(kc == 0), stop=False))
                outs.append(t_.matmul(
                    P[tc][:, 0:512],
                    ap(cbuf, (ncx % 8) * 128, [[PS(cbuf), 32], [1, 128]]),
                    ap(xsb, tc * 512, [[PS(xsb), 32], [1, 512]]),
                    start=False, stop=True))
                return outs
            s.emit("tensor", gmm, waits=[("sh", wg), ("se", xs_done), ("sv", s.at("sv")), ("sx", cb_ld)], inc="stn")
        gmd = s.at("stn")
        def silu(sc, gmd=gmd):
            return [sc.activation(qsub[:, 0:512], P[0][:, 0:512], AF.Silu),
                    sc.activation(qsub[:, 512:1024], P[1][:, 0:512], AF.Silu)]
        s.emit("scalar", silu, waits=[("stn", gmd), ("sv", s.at("sv")), ("sr", s.at("sr"))], inc="se")
        sild = s.at("se")
        s.dma("sync", bstr[:, buf * HID:(buf + 1) * HID],
              ubytes[ncx * 128:(ncx + 1) * 128, :],
              waits=[("sh", wg)], inc="sx")
        s.dma("gpsimd", rstr[:, 0:HID],
              ap(uscale, ncx * NJC * 128, [[0, 128], [1, HID]]),
              waits=[("sh", wg)], inc="sx")
        uld = s.at("sx")
        def dqu(v, buf=buf, uld=uld):
            return v.tensor_scalar(ustr[:, :], bstr[:, buf * HID:(buf + 1) * HID],
                                   shf[:, 0:1], 15, ALU.logical_shift_right, ALU.bitwise_and)
        s.emit("vector", dqu, waits=[("sx", uld), ("se", sild)], inc="sv")
        uq2 = s.at("sv")
        def cst2(sc):
            return sc.activation(cstr[:, :], ustr[:, :], AF.Copy, bias=-8.0)
        s.emit("scalar", cst2, waits=[("sv", uq2), ("sh", s.at("sh"))], inc="se")
        csd2 = s.at("se")
        def sclm2(g, buf=buf, csd2=csd2):
            return g.tensor_tensor(wstr[:, buf * HID:(buf + 1) * HID], cstr[:, :],
                                   rstr[:, 0:HID], ALU.mult)
        s.emit("vector", sclm2, waits=[("se", csd2)], inc="sh")
        wu = s.at("sh")
        for tc in range(2):
            def umm(t_, buf=buf, tc=tc, ncx=ncx):
                outs = []
                for kc in range(NJC):
                    outs.append(t_.matmul(
                        P[2 + tc][:, 0:512],
                        wstr[:, buf * HID + kc * 128: buf * HID + (kc + 1) * 128],
                        ap(xn, kc * T + tc * 512, [[PS(xn), 128], [1, 512]]),
                        start=(kc == 0), stop=False))
                outs.append(t_.matmul(
                    P[2 + tc][:, 0:512],
                    ap(cbuf, 1024 + (ncx % 8) * 128, [[PS(cbuf), 32], [1, 128]]),
                    ap(xsb, tc * 512, [[PS(xsb), 32], [1, 512]]),
                    start=False, stop=True))
                return outs
            s.emit("tensor", umm, waits=[("sh", wu), ("sv", s.at("sv")), ("se", s.at("se"))], inc="stn")
        umd = s.at("stn")
        def hfn(v, umd=umd):
            return [v.tensor_tensor(ksub[:, 0:512], qsub[:, 0:512], P[2][:, 0:512], ALU.mult),
                    v.tensor_tensor(ksub[:, 512:1024], qsub[:, 512:1024], P[3][:, 0:512], ALU.mult)]
        s.emit("vector", hfn, waits=[("stn", umd), ("sr", s.at("sr"))], inc="sv")
        hd = s.at("sv")
        def x2fn(t_, ncx=ncx):
            outs = []
            for tc in range(2):
                outs.append(t_.matmul(
                    P[4 + tc][0:86, 0:512],
                    w86[:, 85 - ncx:171 - ncx],
                    ksub[:, tc * 512:(tc + 1) * 512],
                    start=(ncx == 0), stop=(ncx == NGI - 1)))
            return outs
        s.emit("tensor", x2fn, waits=[("sv", hd)], inc="stn")
        s.dma("sync", h_dram[ncx * 128:(ncx + 1) * 128, :], ksub[:, 0:1024],
              waits=[("sv", hd)], inc="sr")
    gate_done_sr = s.at("sr")
    x2mm = s.at("stn")
    def x2ev(sc):
        return [sc.copy(ap(x2sb, 0, [[PS(x2sb), 86], [1, 512]]),
                        ap(P[4], 0, [[PS(P[4]), 86], [1, 512]])),
                sc.copy(ap(x2sb, 512, [[PS(x2sb), 86], [1, 512]]),
                        ap(P[5], 0, [[PS(P[5]), 86], [1, 512]]))]
    s.emit("scalar", x2ev, waits=[("stn", x2mm)], inc="se")
    x2d = s.at("se")

    if "h" in taps:
        s.dma("sync", tap_t["th"][:, :], h_dram[:, :], waits=[("sr", gate_done_sr)], inc="sr")

    if stop_after == "h":
        fin = s.at("sr")
        x2w = s.at("se")
        def earlyw3(e):
            return e.nop()
        s.emit("sync", earlyw3, waits=[("sr", fin), ("se", x2w)])
        block = st.enter_context(nc.Block())
        s.build(block)
        st.close()
        s._stack.close()
        return nc, tap_t

    for jc in range(NGI):
        s.dma("sync", bstr[:, 0:HID],
              dbytes[jc * 128:(jc + 1) * 128, :],
              waits=[("sh", s.at("sh")), ("sr", s.at("sr")), ("sv", s.at("sv"))], inc="sx")
        s.dma("gpsimd", rstr[:, 0:HID],
              ap(dscale, jc * HID, [[0, 128], [1, HID]]),
              waits=[("sh", s.at("sh"))], inc="sx")
        dld = s.at("sx")
        def dqd(v, dld=dld):
            return v.tensor_scalar(ustr[:, :], bstr[:, 0:HID],
                                   shf[:, 0:1], 15, ALU.logical_shift_right, ALU.bitwise_and)
        s.emit("vector", dqd, waits=[("sx", dld), ("se", s.at("se"))], inc="sv")
        uq3 = s.at("sv")
        def cst3(sc):
            return sc.activation(cstr[:, :], ustr[:, :], AF.Copy, bias=-8.0)
        s.emit("scalar", cst3, waits=[("sv", uq3), ("sh", s.at("sh"))], inc="se")
        csd3 = s.at("se")
        def sclm3(g, csd3=csd3):
            return g.tensor_tensor(wstr[:, 0:HID], cstr[:, :], rstr[:, 0:HID], ALU.mult)
        s.emit("vector", sclm3, waits=[("se", csd3), ("sr", s.at("sr"))], inc="sh")
        wdd = s.at("sh")
        s.dma("sync", wd_dram[jc * 128:(jc + 1) * 128, :], wstr[:, 0:HID],
              waits=[("sh", wdd)], inc="sr")
    wd_done = s.at("sr")

    if "wd" in taps:
        s.dma("sync", tap_t["twd"][:, :], wd_dram[0:256, :], waits=[("sr", wd_done)], inc="sr")

    for mb in range(8):
        s.dma("sync", cdnb[0:86, 0:512], cd_dram[:, mb * 512:(mb + 1) * 512],
              waits=[("stn", s.at("stn")), ("sr", wd_done)], inc="sx")
        cdl = s.at("sx")
        for th in range(2):
            for jc in range(NGI):
                buf = jc % 2
                s.dma("sync", wstr[:, buf * 512:(buf + 1) * 512],
                      wd_dram[jc * 128:(jc + 1) * 128, mb * 512:(mb + 1) * 512],
                      waits=[("stn", s.at("stn")), ("sr", wd_done), ("sh", s.at("sh"))], inc="sx")
                s.dma("sync", hdown[:, buf * 512:(buf + 1) * 512],
                      h_dram[jc * 128:(jc + 1) * 128, th * 512:(th + 1) * 512],
                      waits=[("sr", gate_done_sr)], inc="sx")
                dl = s.at("sx")
                def dmm(t_, buf=buf, jc=jc):
                    outs = []
                    for tb4 in range(4):
                        outs.append(t_.matmul(
                            P[tb4][:, 0:512],
                            hdown[:, buf * 512 + tb4 * 128: buf * 512 + (tb4 + 1) * 128],
                            wstr[:, buf * 512:(buf + 1) * 512],
                            start=(jc == 0), stop=False))
                    return outs
                s.emit("tensor", dmm, waits=[("sx", dl), ("sv", s.at("sv")), ("se", x2d)], inc="stn")
            def dcor(t_, mb=mb, th=th):
                outs = []
                for tb4 in range(4):
                    tbg = th * 4 + tb4
                    outs.append(t_.matmul(
                        P[tb4][:, 0:512],
                        ap(x2sb, tbg * 128, [[PS(x2sb), 86], [1, 128]]),
                        ap(cdnb, 0, [[PS(cdnb), 86], [1, 512]]),
                        start=False, stop=True))
                return outs
            s.emit("tensor", dcor, waits=[("se", x2d), ("sx", cdl)], inc="stn")
            dmd = s.at("stn")
            if "dp0" in taps and mb == 0 and th == 0:
                tap_t["tdp"] = nc.dram_tensor("tdp", [128, 512], F32, kind="ExternalOutput")
                def dpcp(sc):
                    return sc.copy(x2f[:, 0:512], P[0][:, 0:512])
                s.emit("scalar", dpcp, waits=[("stn", dmd)], inc="se")
                s.dma("sync", tap_t["tdp"][:, :], x2f[:, 0:512], waits=[("se", s.at("se"))], inc="sr")
            for tb4 in range(4):
                tbg = th * 4 + tb4
                def xtf(t_, mb=mb, tbg=tbg):
                    outs = []
                    for mcc in range(4):
                        mc = mb * 4 + mcc
                        outs.append(t_.transpose(
                            ap(PT, mcc * 128, [[PS(PT), 128], [1, 128]]),
                            ap(xn, mc * T + tbg * 128, [[PS(xn), 128], [1, 128]]),
                            idm2[:, :]))
                    return outs
                s.emit("tensor", xtf, waits=[("se", s.at("se")), ("stn", dmd)], inc="stn")
                xtd = s.at("stn")
                def xte(sc):
                    return sc.copy(ap(wsb2, 0, [[PS(wsb2), 128], [1, 512]]),
                                   ap(PT, 0, [[PS(PT), 128], [1, 512]]))
                s.emit("scalar", xte, waits=[("stn", xtd)], inc="se")
                xted = s.at("se")
                def res2(v, mb=mb, tbg=tbg, tb4=tb4):
                    return v.tensor_tensor(
                        ap(Abig, tbg * 4096 + mb * 512, [[PS(Abig), 128], [1, 512]]),
                        P[tb4][:, 0:512],
                        ap(wsb2, 0, [[PS(wsb2), 128], [1, 512]]),
                        ALU.add)
                s.emit("vector", res2, waits=[("se", xted)], inc="sv")
                r2d = s.at("sv")
                def sq2(sc, tbg=tbg, mb=mb):
                    return sc.activation(ap(w2, 0, [[PS(w2), 128], [1, 512]]),
                                         ap(Abig, tbg * 4096 + mb * 512, [[PS(Abig), 128], [1, 512]]),
                                         AF.Square)
                s.emit("scalar", sq2, waits=[("sv", r2d)], inc="se")
                sq2d = s.at("se")
                def st2(v, tbg=tbg, mb=mb):
                    a = v.reduce_sum(stats[:, tbg * 8 + mb: tbg * 8 + mb + 1],
                                     ap(Abig, tbg * 4096 + mb * 512, [[PS(Abig), 128], [1, 512]]),
                                     axis=AX.X)
                    b = v.reduce_sum(stats[:, 64 + tbg * 8 + mb: 64 + tbg * 8 + mb + 1],
                                     ap(w2, 0, [[PS(w2), 128], [1, 512]]), axis=AX.X)
                    return [a, b]
                s.emit("vector", st2, waits=[("se", sq2d)], inc="sv")
    down_sv = s.at("sv")
    if "sum2" in taps:
        tap_t["ts2"] = nc.dram_tensor("ts2", [128, 32768], BF16, kind="ExternalOutput")
        s.dma("sync", tap_t["ts2"][:, :], ap(Abig, 0, [[PS(Abig), 128], [1, 32768]]),
              waits=[("sv", down_sv)], inc="sr")

    def fin1(v):
        return [
            v.reduce_sum(stats[:, 128:136], ap(stats, 0, [[PS(stats), 128], [8, 8], [1, 8]]), axis=AX.X),
            v.reduce_sum(stats[:, 136:144], ap(stats, 64, [[PS(stats), 128], [8, 8], [1, 8]]), axis=AX.X),
        ]
    s.emit("vector", fin1, waits=[("sv", down_sv)], inc="sv")
    def fin2(v):
        o = [v.tensor_scalar(stats[:, 128:136], stats[:, 128:136], cN[:, C_INV4096:C_INV4096 + 1], None, ALU.mult),
             v.tensor_scalar(stats[:, 136:144], stats[:, 136:144], cN[:, C_INV4096:C_INV4096 + 1], None, ALU.mult)]
        v.drain()
        o.append(v.tensor_tensor(stats[:, 160:168], stats[:, 128:136], stats[:, 128:136], ALU.mult))
        v.drain()
        o.append(v.tensor_tensor(stats[:, 160:168], stats[:, 136:144], stats[:, 160:168], ALU.subtract))
        v.drain()
        o.append(v.tensor_scalar(stats[:, 160:168], stats[:, 160:168], cN[:, C_EPS:C_EPS + 1], None, ALU.add))
        return o
    s.emit("vector", fin2, waits=[("sv", s.at("sv"))], inc="sv")
    def fin3(sc):
        return sc.activation(stats[:, 144:152], stats[:, 160:168], AF.Sqrt)
    s.emit("scalar", fin3, waits=[("sv", s.at("sv"))], inc="se")
    def fin4(v):
        r0 = v.reciprocal(stats[:, 144:152], stats[:, 144:152])
        v.drain()
        a = v.tensor_tensor(stats[:, 152:160], stats[:, 128:136], stats[:, 144:152], ALU.mult)
        v.drain()
        b = v.tensor_scalar(stats[:, 152:160], stats[:, 152:160], cN[:, C_NEG1:C_NEG1 + 1], None, ALU.mult)
        return [r0, a, b]
    s.emit("vector", fin4, waits=[("se", s.at("se"))], inc="sv")
    ln2d = s.at("sv")

    for tb in range(8):
        for piece in range(4):
            def outp(v, tb=tb, piece=piece):
                return v.tensor_scalar(
                    x2f[:, :],
                    ap(Abig, tb * 4096 + piece * 1024, [[PS(Abig), 128], [1, 1024]]),
                    stats[:, 144 + tb:145 + tb], stats[:, 152 + tb:153 + tb],
                    ALU.mult, ALU.add)
            s.emit("vector", outp, waits=[("sv", ln2d), ("sr", s.at("sr"))], inc="sv")
            od = s.at("sv")
            s.dma("sync", y[tb * 128:(tb + 1) * 128, piece * 1024:(piece + 1) * 1024],
                  x2f[:, :], waits=[("sv", od)], inc="sr")
    final_sr = s.at("sr")
    def finw(e):
        return e.nop()
    s.emit("sync", finw, waits=[("sr", final_sr)])

    block = st.enter_context(nc.Block())
    s.build(block)
    st.close()
    s._stack.close()
    return nc, tap_t


def _host_stage(inputs):
    d = {}
    f32 = np.float32

    def restage_w(w):
        wt = np.ascontiguousarray(np.asarray(w, f32).T)
        return np.ascontiguousarray(
            wt.reshape(NJC, 128, NJC, 128).transpose(2, 1, 0, 3).reshape(NJC * 128, HID))

    d["wqt"] = restage_w(inputs["L0_attn_Wq"])
    d["wkt"] = restage_w(inputs["L0_attn_Wk"])
    d["wvt"] = restage_w(inputs["L0_attn_Wv"])
    wot_nat = np.ascontiguousarray(np.asarray(inputs["L0_attn_Wo"], f32).T)
    hp = np.arange(4096) // 64
    dd = np.arange(4096) % 64
    imap = (2 * (hp % 32) + hp // 32) * 64 + dd
    wot_p = wot_nat[imap]
    d["wot"] = np.ascontiguousarray(
        wot_p.reshape(NJC, 128, NJC, 128).transpose(2, 1, 0, 3).reshape(NJC * 128, HID))

    def tile3(a, b, c):
        out = np.zeros((128, 3), f32)
        out[:, 0] = np.tile(np.asarray(a, f32), 2)
        out[:, 1] = np.tile(np.asarray(b, f32), 2)
        out[:, 2] = np.tile(np.asarray(c, f32), 2)
        return out
    d["maskq"] = tile3(inputs["L0_attn_mu_q"], inputs["L0_attn_sigma_q"], inputs["L0_attn_eps_q"])
    d["maskkv"] = tile3(inputs["L0_attn_mu_kv"], inputs["L0_attn_sigma_kv"], inputs["L0_attn_eps_kv"])

    cv = np.zeros((128, 8), f32)
    cv[:, C_EXPB] = -4.0
    cv[:, C_INV4096] = 1.0 / 4096.0
    cv[:, C_EPS] = 1e-5
    cv[:, C_NEG1] = -1.0
    cv[:, C_ONE] = 1.0
    cv[:, C_NEG8] = -8.0
    d["consts"] = cv

    idm = np.zeros((128, 128), bfdt)
    idm[0:64, 0:64] = np.eye(64)
    idm[64:128, 0:64] = np.eye(64)
    d["idmat"] = idm
    d["idmat2"] = np.eye(128).astype(bfdt)
    o1 = np.zeros((128, 2), bfdt)
    o1[:, 0] = 1
    d["ones128"] = o1
    d["onesrow"] = np.ones((1, 128), bfdt)
    d["onesrow32"] = np.ones((1, 128), f32)
    w32_ = np.zeros((128, 63), bfdt)
    w32_[:, 31] = 1
    d["wide32"] = w32_
    w86_ = np.zeros((128, 171), bfdt)
    w86_[:, 85] = 1
    d["wide86"] = w86_
    d["shiftpp"] = np.tile(np.array([0, 4], np.uint8), 64).reshape(128, 1)

    def bytes_rep(qw):
        K8, N = qw.shape
        b = np.ascontiguousarray(qw).view(np.uint8).reshape(K8, N, 4).transpose(0, 2, 1).reshape(K8 * 4, N)
        return np.repeat(b, 2, axis=0)

    gb = bytes_rep(np.asarray(inputs["L0_ffn_gate_qweight"], np.int32))
    ub = bytes_rep(np.asarray(inputs["L0_ffn_up_qweight"], np.int32))
    db = bytes_rep(np.asarray(inputs["L0_ffn_down_qweight"], np.int32))
    d["gbytes"] = np.ascontiguousarray(
        gb.reshape(NJC, 128, NGI, 128).transpose(2, 1, 0, 3).reshape(NGI * 128, NJC * 128))
    d["ubytes"] = np.ascontiguousarray(
        ub.reshape(NJC, 128, NGI, 128).transpose(2, 1, 0, 3).reshape(NGI * 128, NJC * 128))
    d["dbytes"] = np.ascontiguousarray(db)

    gs = np.asarray(inputs["L0_ffn_gate_scales"], f32)
    us = np.asarray(inputs["L0_ffn_up_scales"], f32)
    ds = np.asarray(inputs["L0_ffn_down_scales"], f32)
    d["gscale"] = np.ascontiguousarray(gs.reshape(NJC, NGI, 128).transpose(1, 0, 2).reshape(NGI, NJC * 128))
    d["uscale"] = np.ascontiguousarray(us.reshape(NJC, NGI, 128).transpose(1, 0, 2).reshape(NGI, NJC * 128))
    d["dscale"] = np.ascontiguousarray(ds)
    d["gsc_nat"] = np.ascontiguousarray(gs)
    d["usc_nat"] = np.ascontiguousarray(us)
    d["gzer"] = np.ascontiguousarray(np.asarray(inputs["L0_ffn_gate_qzeros"], np.int32))
    d["uzer"] = np.ascontiguousarray(np.asarray(inputs["L0_ffn_up_qzeros"], np.int32))
    d["dzer"] = np.ascontiguousarray(np.asarray(inputs["L0_ffn_down_qzeros"], np.int32))
    return d


def kernel(**inputs):
    x = np.ascontiguousarray(np.asarray(inputs["L0_attn_hidden_states"], np.float32).reshape(-1, HID))
    shared = _host_stage(inputs)
    nc, _ = build_nc()
    in_maps = []
    for c in range(NCORE):
        m = dict(shared)
        m["xin"] = np.ascontiguousarray(x[c * T:(c + 1) * T])
        in_maps.append(m)
    res = run_bass_kernel_spmd(nc, in_maps, core_ids=list(range(NCORE)))
    out = np.concatenate([r["y"] for r in res.results], axis=0)
    return np.ascontiguousarray(out.reshape(2, 4096, HID))


# version 04
# speedup vs baseline: 1.1253x; 1.0899x over previous
import numpy as np
import ml_dtypes
from contextlib import ExitStack

import concourse.bass as bass
import concourse.mybir as mybir
from concourse.bass_utils import run_bass_kernel_spmd

F32 = mybir.dt.float32
BF16 = mybir.dt.bfloat16
U8 = mybir.dt.uint8
I32 = mybir.dt.int32
AF = mybir.ActivationFunctionType
ALU = mybir.AluOpType
AX = mybir.AxisListType
bfdt = ml_dtypes.bfloat16

NCORE = 8
T = 1024
HID = 4096
INTER = 11008
NJC = 32
NGI = 86
TP = 256
NPH = 4
SUB = 64
NSUB = 4

C_EXPB, C_INV4096, C_EPS, C_NEG1, C_ONE, C_NEG8 = 0, 1, 2, 3, 4, 5

ENGINES = ("tensor", "vector", "scalar", "gpsimd", "sync")


class Sched:
    def __init__(self, nc):
        self.nc = nc
        self.q = {e: [] for e in ENGINES}
        self.cnt = {}
        self.sems = {}
        self._stack = ExitStack()

    def sem(self, name):
        self.sems[name] = self._stack.enter_context(self.nc.semaphore(name))
        self.cnt[name] = 0

    def at(self, name):
        return self.cnt[name]

    def emit(self, eng, fn, waits=(), inc=None, amt=1):
        if inc is not None:
            self.cnt[inc] += amt
        self.q[eng].append((list(waits), fn, inc, amt))

    def dma(self, eng, out, in_, waits=(), inc=None):
        def fn(e, out=out, in_=in_):
            return e.dma_start(out, in_)
        self.emit(eng, fn, waits=waits, inc=inc, amt=16)

    def build(self, block):
        for eng in ENGINES:
            items = self.q[eng]
            if not items:
                continue

            def body(e, items=items):
                last_wait = {}
                for waits, fn, inc, amt in items:
                    for sname, v in waits:
                        if v > 0 and last_wait.get(sname, -1) < v:
                            e.wait_ge(self.sems[sname], v)
                            last_wait[sname] = v
                    ins = fn(e)
                    if inc is not None:
                        if isinstance(ins, list):
                            ins = ins[-1]
                        ins.then_inc(self.sems[inc], amt)
            getattr(block, eng)(body)


def ap(t, offset, dims):
    return bass.AP(t, offset, dims)


def build_nc(taps=(), stop_after="full", nph=NPH, attn_stage="full"):
    nc = bass.Bass("TRN2", target_bir_lowering=False)

    xin = nc.dram_tensor("xin", [T, HID], F32, kind="ExternalInput")
    wqt = nc.dram_tensor("wqt", [NJC * 128, HID], F32, kind="ExternalInput")
    wkt = nc.dram_tensor("wkt", [NJC * 128, HID], F32, kind="ExternalInput")
    wvt = nc.dram_tensor("wvt", [NJC * 128, HID], F32, kind="ExternalInput")
    wot = nc.dram_tensor("wot", [NJC * 128, HID], F32, kind="ExternalInput")
    maskq = nc.dram_tensor("maskq", [128, 3], F32, kind="ExternalInput")
    maskkv = nc.dram_tensor("maskkv", [128, 3], F32, kind="ExternalInput")
    consts = nc.dram_tensor("consts", [128, 8], F32, kind="ExternalInput")
    idmat = nc.dram_tensor("idmat", [128, 128], BF16, kind="ExternalInput")
    idmat2 = nc.dram_tensor("idmat2", [128, 128], BF16, kind="ExternalInput")
    ones128 = nc.dram_tensor("ones128", [128, 2], BF16, kind="ExternalInput")
    onesrow = nc.dram_tensor("onesrow", [1, 128], BF16, kind="ExternalInput")
    onesrow32 = nc.dram_tensor("onesrow32", [1, 128], F32, kind="ExternalInput")
    wide32 = nc.dram_tensor("wide32", [128, 63], BF16, kind="ExternalInput")
    wide86 = nc.dram_tensor("wide86", [128, 171], BF16, kind="ExternalInput")
    gbytes = nc.dram_tensor("gbytes", [NGI * 128, NJC * 128], U8, kind="ExternalInput")
    ubytes = nc.dram_tensor("ubytes", [NGI * 128, NJC * 128], U8, kind="ExternalInput")
    dbytes = nc.dram_tensor("dbytes", [NGI * 128, HID], U8, kind="ExternalInput")
    gscale = nc.dram_tensor("gscale", [NGI, NJC * 128], F32, kind="ExternalInput")
    uscale = nc.dram_tensor("uscale", [NGI, NJC * 128], F32, kind="ExternalInput")
    dscale = nc.dram_tensor("dscale", [NGI, HID], F32, kind="ExternalInput")
    gsc_nat = nc.dram_tensor("gsc_nat", [NJC, INTER], F32, kind="ExternalInput")
    usc_nat = nc.dram_tensor("usc_nat", [NJC, INTER], F32, kind="ExternalInput")
    gzer = nc.dram_tensor("gzer", [NJC, INTER // 8], I32, kind="ExternalInput")
    uzer = nc.dram_tensor("uzer", [NJC, INTER // 8], I32, kind="ExternalInput")
    dzer = nc.dram_tensor("dzer", [NGI, HID // 8], I32, kind="ExternalInput")
    shiftpp = nc.dram_tensor("shiftpp", [128, 1], U8, kind="ExternalInput")

    y = nc.dram_tensor("y", [T, HID], F32, kind="ExternalOutput")

    h_dram = nc.dram_tensor("h_scr", [NGI * 128, T], BF16, kind="Internal")
    wd_dram = nc.dram_tensor("wd_scr", [NGI * 128, HID], BF16, kind="Internal")
    cg_dram = nc.dram_tensor("cg_scr", [32, INTER], BF16, kind="Internal")
    cu_dram = nc.dram_tensor("cu_scr", [32, INTER], BF16, kind="Internal")
    cd_dram = nc.dram_tensor("cd_scr", [NGI, HID], BF16, kind="Internal")
    gs16 = nc.dram_tensor("gs16_scr", [NGI, NJC * 128], BF16, kind="Internal")
    us16 = nc.dram_tensor("us16_scr", [NGI, NJC * 128], BF16, kind="Internal")
    ds16 = nc.dram_tensor("ds16_scr", [NGI, HID], BF16, kind="Internal")

    tap_t = {}
    if "qkv" in taps:
        for n in ("tq", "tk", "tv"):
            tap_t[n] = nc.dram_tensor(n, [128, 8192], BF16, kind="ExternalOutput")
    if "attn" in taps:
        tap_t["to"] = nc.dram_tensor("to", [128, 8192], BF16, kind="ExternalOutput")
    if "xn" in taps:
        tap_t["txn"] = nc.dram_tensor("txn", [128, NJC * T], BF16, kind="ExternalOutput")
    if "h" in taps:
        tap_t["th"] = nc.dram_tensor("th", [NGI * 128, T], BF16, kind="ExternalOutput")
    if "wd" in taps:
        tap_t["twd"] = nc.dram_tensor("twd", [256, HID], BF16, kind="ExternalOutput")

    st = ExitStack()
    sb = lambda name, shape, dt: st.enter_context(nc.sbuf_tensor(name, shape, dt))
    psal = lambda name, shape, dt: st.enter_context(nc.psum_tensor(name, shape, dt))

    xn = sb("xn", [128, NJC * T], BF16)
    xfm = sb("xfm", [128, NJC * TP], BF16)
    Abig = sb("Abig", [128, 32768], BF16)
    wstr = sb("wstr", [128, 2 * HID], BF16)
    bstr = sb("bstr", [128, 2 * HID], U8)
    rstr = sb("rstr", [128, HID], BF16)
    soar = sb("soar", [128, 4096], BF16)
    wsb2 = sb("wsb2", [128, 4096], BF16)
    qsub = sb("qsub", [128, 2048], BF16)
    ksub = sb("ksub", [128, 2048], BF16)
    vsub = sb("vsub", [128, 2048], BF16)
    zsb = sb("zsb", [128, 32], F32)
    rzbf = sb("rzbf", [128, 32], BF16)
    addr = lambda t_: nc.lookup_mloc(t_).addr
    w2 = nc.alloc_sbuf_tensor_at("w2", [128, 4096], BF16, offset=addr(bstr))
    cstr = nc.alloc_sbuf_tensor_at("cstr", [128, HID], BF16, offset=addr(Abig))
    sum32 = nc.alloc_sbuf_tensor_at("sum32", [128, 8192], F32, offset=addr(Abig))
    cbuf = nc.alloc_sbuf_tensor_at("cbuf", [128, 2048], BF16, offset=addr(xfm))
    assert addr(ksub) == addr(qsub) + 4096, (addr(qsub), addr(ksub))
    xtm = nc.alloc_sbuf_tensor_at("xtm", [128, 4096], BF16, offset=addr(qsub))
    ustr = nc.alloc_sbuf_tensor_at("ustr", [128, HID], U8, offset=addr(vsub))
    hdown = nc.alloc_sbuf_tensor_at("hdown", [128, 4096], BF16, offset=addr(soar))
    zztmp = nc.alloc_sbuf_tensor_at("zztmp", [128, 2112], F32, offset=addr(wstr))
    lnrow = nc.alloc_sbuf_tensor_at("lnrow", [128, 768], F32, offset=addr(soar))
    lnbf = nc.alloc_sbuf_tensor_at("lnbf", [128, 512], BF16, offset=addr(soar) + 3072)
    stats = nc.alloc_sbuf_tensor_at("stats", [128, 192], F32, offset=addr(rstr))
    wts = nc.alloc_sbuf_tensor_at("wts", [128, 2048], BF16, offset=addr(rstr))
    vts = nc.alloc_sbuf_tensor_at("vts", [128, 2048], BF16, offset=addr(rstr) + 4096)
    x2f = nc.alloc_sbuf_tensor_at("x2f", [128, 1024], F32, offset=addr(soar) + 4096)
    cdnb = nc.alloc_sbuf_tensor_at("cdnb", [128, 512], BF16, offset=addr(vsub))
    x2sb = nc.alloc_sbuf_tensor_at("x2sb", [128, T], BF16, offset=addr(wsb2) + 4096)
    xsb = nc.alloc_sbuf_tensor_at("xsb", [128, T], BF16, offset=addr(wsb2) + 6144)
    cN = sb("cN", [128, 8], F32)
    nc.const_aps.register(F32, -8.0, cN[:, C_NEG8:C_NEG8 + 1])
    mq8 = sb("mq8", [128, 4], F32)
    mkv = sb("mkv", [128, 4], F32)
    idm = sb("idm", [128, 128], BF16)
    idm2 = sb("idm2", [128, 128], BF16)
    ones1 = sb("ones1", [128, 2], BF16)
    onesr = sb("onesr", [1, 128], BF16)
    onesr32 = sb("onesr32", [1, 128], F32)
    w32 = sb("w32", [128, 63], BF16)
    w86 = sb("w86", [128, 171], BF16)
    shf = sb("shf", [128, 1], U8)

    P = [psal(f"P{i}", [128, 512], F32) for i in range(7)]
    PT = psal("PT", [128, 1024], BF16)

    PSx = {}
    def PS(t_):
        k = id(t_)
        if k not in PSx:
            PSx[k] = t_.ap().ap[0][0]
        return PSx[k]

    s = Sched(nc)
    for name in ("sx", "sw", "sr", "se", "sv", "stn", "sh"):
        s.sem(name)

    QOF, KOF, VOF, OOF = 0, 8192, 16384, 24576

    for dst, src in ((cN, consts), (idm, idmat), (idm2, idmat2), (ones1, ones128),
                     (w32, wide32), (w86, wide86), (shf, shiftpp)):
        s.dma("gpsimd", dst[:, :], src[:, :], inc="sx")
    s.dma("gpsimd", onesr[:, :], onesrow[:, :], inc="sx")
    s.dma("gpsimd", onesr32[:, :], onesrow32[:, :], inc="sx")
    s.dma("gpsimd", mq8[:, 0:3], maskq[:, :], inc="sx")
    s.dma("gpsimd", mkv[:, 0:3], maskkv[:, :], inc="sx")
    pro_ld = s.at("sx")

    def mk_masks(v):
        a = v.tensor_tensor(mq8[:, 3:4], mq8[:, 2:3], mq8[:, 1:2], ALU.mult)
        b = v.tensor_tensor(mkv[:, 3:4], mkv[:, 2:3], mkv[:, 1:2], ALU.mult)
        v.drain()
        c = v.tensor_tensor(mq8[:, 3:4], mq8[:, 3:4], mq8[:, 0:1], ALU.add)
        d = v.tensor_tensor(mkv[:, 3:4], mkv[:, 3:4], mkv[:, 0:1], ALU.add)
        return [a, b, c, d]
    s.emit("vector", mk_masks, waits=[("sx", pro_ld)], inc="sv")

    def mk_sig(sc):
        return [sc.activation(mq8[:, 0:1], mq8[:, 3:4], AF.Sigmoid),
                sc.activation(mkv[:, 0:1], mkv[:, 3:4], AF.Sigmoid)]
    s.emit("scalar", mk_sig, waits=[("sv", s.at("sv"))], inc="se")

    def mk_mq8(v):
        return v.tensor_scalar(mq8[:, 0:1], mq8[:, 0:1], 0.125, None, ALU.mult)
    s.emit("vector", mk_mq8, waits=[("se", s.at("se"))], inc="sv")
    masks_sv = s.at("sv")
    masks_se = s.at("se")

    if "masks" in taps:
        tap_t["tmask"] = nc.dram_tensor("tmask", [128, 8], F32, kind="ExternalOutput")
        s.dma("sync", tap_t["tmask"][:, 0:4], mq8[:, 0:4],
              waits=[("sv", masks_sv), ("se", masks_se)], inc="sr")
        s.dma("sync", tap_t["tmask"][:, 4:8], mkv[:, 0:4],
              waits=[("sv", masks_sv), ("se", masks_se)], inc="sr")

    projs = [("q", wqt, QOF, True), ("k", wkt, KOF, False), ("v", wvt, VOF, False)]

    for ph in range(nph):
        t0 = ph * TP
        for g in range(2):
            s.dma("gpsimd", xtm[:, 0:4096], xin[t0 + g * 128: t0 + (g + 1) * 128, :],
                  waits=[("sr", s.at("sr")), ("se", s.at("se"))], inc="sx")
            ld = s.at("sx")
            for jc in range(NJC):
                dst = ap(xfm, jc * TP + g * 128, [[PS(xfm), 128], [1, 128]])
                def fn(e, dst=dst, jc=jc):
                    return e.dma_start(dst, xtm[:, jc * 128:(jc + 1) * 128], transpose=True)
                s.emit("sync", fn, waits=[("sx", ld)], inc="sr", amt=16)
        xfm_ready = s.at("sr")

        for (pname, wsrc, dof, isq) in projs:
            for ic in range(NJC):
                buf = ic % 2
                s.dma("gpsimd", wstr[:, buf * HID:(buf + 1) * HID],
                      wsrc[ic * 128:(ic + 1) * 128, :],
                      waits=[("stn", s.at("stn"))], inc="sw")
                wld = s.at("sw")
                pb = P[ic % 2]
                def mmfn(t_, buf=buf, pb=pb):
                    outs = []
                    for jc in range(NJC):
                        outs.append(t_.matmul(
                            pb[:, 0:TP],
                            wstr[:, buf * HID + jc * 128: buf * HID + (jc + 1) * 128],
                            ap(xfm, jc * TP, [[PS(xfm), 128], [1, TP]]),
                            start=(jc == 0), stop=(jc == NJC - 1)))
                    return outs
                s.emit("tensor", mmfn,
                       waits=[("sw", wld), ("sr", xfm_ready), ("se", s.at("se"))],
                       inc="stn")
                mmd = s.at("stn")
                scale_ap = mq8[:, 0:1] if isq else mkv[:, 0:1]
                dst = ap(Abig, dof + ic * TP, [[PS(Abig), 128], [1, TP]])
                def evfn(sc, dst=dst, pb=pb, scale_ap=scale_ap):
                    return sc.activation(dst, pb[:, 0:TP], AF.Copy, scale=scale_ap)
                s.emit("scalar", evfn,
                       waits=[("stn", mmd), ("sv", masks_sv), ("sr", s.at("sr"))],
                       inc="se")
        qkv_ready = s.at("sr")

        if "qkv" in taps and ph == 0:
            evall = s.at("se")
            s.dma("sync", tap_t["tq"][:, :], ap(Abig, QOF, [[PS(Abig), 128], [1, 8192]]),
                  waits=[("se", evall)], inc="sr")
            s.dma("sync", tap_t["tk"][:, :], ap(Abig, KOF, [[PS(Abig), 128], [1, 8192]]),
                  waits=[("se", evall)], inc="sr")
            s.dma("sync", tap_t["tv"][:, :], ap(Abig, VOF, [[PS(Abig), 128], [1, 8192]]),
                  waits=[("se", evall)], inc="sr")
            qkv_ready = s.at("sr")

        if stop_after == "qkv":
            continue

        for sub in range(8):
            sl0 = sub * 32
            for (dof, dstt) in ((QOF, qsub), (KOF, ksub), (VOF, vsub)):
                for hsel in range(2):
                    srcap = ap(Abig, dof + hsel * 64 * PS(Abig) + sl0,
                               [[PS(Abig), 64], [TP, 32], [1, 32]])
                    dstap = ap(dstt, hsel * 1024, [[PS(dstt), 64], [32, 32], [1, 32]])
                    def rfn(e, dstap=dstap, srcap=srcap):
                        return e.dma_start(dstap, srcap)
                    s.emit("sync", rfn,
                           waits=[("sr", qkv_ready), ("stn", s.at("stn")), ("se", s.at("se"))],
                           inc="sr", amt=16)
            sub_rdy = s.at("sr")
            if attn_stage == "rearr":
                continue
            for bk in range(4):
                pb = P[2 + (bk % 2)]
                def scfn(t_, pb=pb, bk=bk):
                    outs = []
                    for j in range(8):
                        tl = bk * 8 + j
                        lh = ap(qsub, tl, [[PS(qsub), 64], [32, 64]])
                        rh = ap(ksub, tl, [[PS(ksub), 64], [32, 64]])
                        outs.append(t_.matmul(pb[0:64, j * 64:(j + 1) * 64], lh, rh,
                                              start=True, stop=True))
                    return outs
                s.emit("tensor", scfn,
                       waits=[("sr", sub_rdy), ("se", s.at("se")), ("sv", s.at("sv"))],
                       inc="stn")
                scd = s.at("stn")
                if attn_stage == "mm":
                    continue
                srcp = ap(pb, 0, [[PS(pb), 64], [64, 8], [1, 64]])
                dstw = ap(wsb2, bk * 8 * 64, [[PS(wsb2), 64], [64, 8], [1, 64]])
                def exfn(sc, dstw=dstw, srcp=srcp):
                    return sc.activation(dstw, srcp, AF.Exp, bias=cN[0:64, C_EXPB:C_EXPB + 1])
                s.emit("scalar", exfn, waits=[("stn", scd)], inc="se")
                exd = s.at("se")
                def zfn(v, bk=bk):
                    srcw = ap(wsb2, bk * 8 * 64, [[PS(wsb2), 64], [64, 8], [1, 64]])
                    return v.reduce_sum(zsb[0:64, bk * 8:(bk + 1) * 8], srcw, axis=AX.X)
                s.emit("vector", zfn, waits=[("se", exd)], inc="sv")
            if attn_stage == "mm":
                continue
            expd = s.at("se")
            zred = s.at("sv")
            def rzfn(v):
                a_ = v.reciprocal(zsb[0:64, 0:32], zsb[0:64, 0:32])
                v.drain()
                b_ = v.tensor_copy(rzbf[0:64, 0:32], zsb[0:64, 0:32])
                return [a_, b_]
            s.emit("vector", rzfn, waits=[("sv", zred)], inc="sv")
            def w2fn(v):
                i0 = ap(wsb2, 0, [[PS(wsb2), 64], [64, 32], [1, 64]])
                i1 = ap(rzbf, 0, [[PS(rzbf), 64], [1, 32], [0, 64]])
                o = ap(w2, 0, [[PS(w2), 64], [64, 32], [1, 64]])
                return v.tensor_tensor(o, i0, i1, ALU.mult)
            s.emit("vector", w2fn, waits=[("sv", s.at("sv")), ("se", expd)], inc="sv")
            w2d = s.at("sv")
            if attn_stage == "scores":
                continue

            for grp in range(4):
                def trfn(t_, grp=grp):
                    outs = []
                    for i in range(8):
                        tl = grp * 8 + i
                        win = ap(w2, tl * 64, [[PS(w2), 64], [1, 64]])
                        wout = ap(PT, i * 64, [[PS(PT), 64], [1, 64]])
                        outs.append(t_.transpose(wout, win, idm[0:64, 0:64]))
                        vin = ap(vsub, tl, [[PS(vsub), 64], [32, 64]])
                        vout = ap(PT, 512 + i * 64, [[PS(PT), 64], [1, 64]])
                        outs.append(t_.transpose(vout, vin, idm[0:64, 0:64]))
                    return outs
                s.emit("tensor", trfn, waits=[("sv", w2d), ("se", s.at("se"))], inc="stn")
                trd = s.at("stn")
                def tefn(sc, grp=grp):
                    outs = []
                    wsrc = ap(PT, 0, [[PS(PT), 64], [64, 8], [1, 64]])
                    wdst = ap(wts, grp * 8, [[PS(wts), 64], [1, 8], [32, 64]])
                    outs.append(sc.copy(wdst, wsrc))
                    vsrc = ap(PT, 512, [[PS(PT), 64], [64, 8], [1, 64]])
                    vdst = ap(vts, grp * 8, [[PS(vts), 64], [1, 8], [32, 64]])
                    outs.append(sc.copy(vdst, vsrc))
                    return outs
                s.emit("scalar", tefn, waits=[("stn", trd)], inc="se")
            trevd = s.at("se")
            if attn_stage == "trans":
                continue

            for bko in range(4):
                pb = P[4 + (bko % 2)]
                def omfn(t_, pb=pb, bko=bko):
                    outs = []
                    for j in range(8):
                        tl = bko * 8 + j
                        lh = ap(vts, tl, [[PS(vts), 64], [32, 64]])
                        rh = ap(wts, tl, [[PS(wts), 64], [32, 64]])
                        outs.append(t_.matmul(pb[0:64, j * 64:(j + 1) * 64], lh, rh,
                                              start=True, stop=True))
                    return outs
                s.emit("tensor", omfn, waits=[("se", s.at("se"))], inc="stn")
                omd = s.at("stn")
                srcp = ap(pb, 0, [[PS(pb), 64], [64, 8], [1, 64]])
                dsto = ap(soar, bko * 8, [[PS(soar), 64], [1, 8], [32, 64]])
                def oefn(sc, dsto=dsto, srcp=srcp):
                    return sc.copy(dsto, srcp)
                s.emit("scalar", oefn, waits=[("stn", omd)], inc="se")
            oevd = s.at("se")
            for par in range(2):
                srcap = ap(soar, par * 32, [[PS(soar), 64], [64, 32], [1, 32]])
                dstap = ap(Abig, OOF + par * 64 * PS(Abig) + sl0, [[PS(Abig), 64], [TP, 32], [1, 32]])
                def rofn(e, dstap=dstap, srcap=srcap):
                    return e.dma_start(dstap, srcap)
                s.emit("sync", rofn, waits=[("se", oevd)], inc="sr", amt=16)
        attn_done = s.at("sr")

        if "attn" in taps and ph == 0:
            s.dma("sync", tap_t["to"][:, :], ap(Abig, OOF, [[PS(Abig), 128], [1, 8192]]),
                  waits=[("sr", attn_done)], inc="sr")
            attn_done = s.at("sr")

        if stop_after == "attn":
            continue

        for mc in range(NJC):
            buf = mc % 2
            s.dma("gpsimd", wstr[:, buf * HID:(buf + 1) * HID],
                  wot[mc * 128:(mc + 1) * 128, :],
                  waits=[("stn", s.at("stn"))], inc="sw")
            wld = s.at("sw")
            pb = P[mc % 2]
            def wofn(t_, buf=buf, pb=pb):
                outs = []
                for ac in range(NJC):
                    outs.append(t_.matmul(
                        pb[:, 0:TP],
                        wstr[:, buf * HID + ac * 128: buf * HID + (ac + 1) * 128],
                        ap(Abig, OOF + ac * TP, [[PS(Abig), 128], [1, TP]]),
                        start=(ac == 0), stop=(ac == NJC - 1)))
                return outs
            s.emit("tensor", wofn,
                   waits=[("sw", wld), ("sr", attn_done), ("sv", s.at("sv"))], inc="stn")
            wod = s.at("stn")
            def refn(v, mc=mc, pb=pb):
                return v.tensor_tensor(
                    sum32[:, mc * TP:(mc + 1) * TP],
                    pb[:, 0:TP],
                    ap(xfm, mc * TP, [[PS(xfm), 128], [1, TP]]),
                    ALU.add)
            s.emit("vector", refn, waits=[("stn", wod)], inc="sv")
        sumd = s.at("sv")

        for g8 in range(4):
            def castf(sc, g8=g8):
                outs = []
                for i in range(8):
                    mc = g8 * 8 + i
                    outs.append(sc.copy(qsub[:, i * 256:(i + 1) * 256],
                                        sum32[:, mc * TP:(mc + 1) * TP]))
                    outs.append(sc.activation(ksub[:, i * 256:(i + 1) * 256],
                                              sum32[:, mc * TP:(mc + 1) * TP], AF.Square))
                return outs
            s.emit("scalar", castf, waits=[("sv", sumd), ("stn", s.at("stn")), ("sr", s.at("sr"))], inc="se")
            cd_ = s.at("se")
            def stf2(t_, g8=g8):
                outs = []
                for i in range(8):
                    mc = g8 * 8 + i
                    outs.append(t_.matmul(P[2][0:1, 0:TP], ones1[:, 0:1],
                                          qsub[:, i * 256:(i + 1) * 256],
                                          start=(mc == 0), stop=(mc == NJC - 1)))
                    outs.append(t_.matmul(P[3][0:1, 0:TP], ones1[:, 0:1],
                                          ksub[:, i * 256:(i + 1) * 256],
                                          start=(mc == 0), stop=(mc == NJC - 1)))
                return outs
            s.emit("tensor", stf2, waits=[("se", cd_), ("sv", sumd)], inc="stn")
        std_ = s.at("stn")
        def rowfn(v):
            o = [v.tensor_scalar(lnrow[0:1, 0:256], P[2][0:1, 0:TP],
                                 cN[0:1, C_INV4096:C_INV4096 + 1], None, ALU.mult),
                 v.tensor_scalar(lnrow[0:1, 256:512], P[3][0:1, 0:TP],
                                 cN[0:1, C_INV4096:C_INV4096 + 1], None, ALU.mult)]
            v.drain()
            o.append(v.tensor_tensor(lnrow[0:1, 512:768], lnrow[0:1, 0:256], lnrow[0:1, 0:256], ALU.mult))
            v.drain()
            o.append(v.tensor_tensor(lnrow[0:1, 512:768], lnrow[0:1, 256:512], lnrow[0:1, 512:768], ALU.subtract))
            v.drain()
            o.append(v.tensor_scalar(lnrow[0:1, 512:768], lnrow[0:1, 512:768],
                                     cN[0:1, C_EPS:C_EPS + 1], None, ALU.add))
            return o
        s.emit("vector", rowfn, waits=[("stn", std_)], inc="sv")
        def rsq(sc):
            return sc.activation(lnrow[0:1, 256:512], lnrow[0:1, 512:768], AF.Sqrt)
        s.emit("scalar", rsq, waits=[("sv", s.at("sv"))], inc="se")
        def bfn(v):
            o = [v.reciprocal(lnrow[0:1, 256:512], lnrow[0:1, 256:512])]
            v.drain()
            o.append(v.tensor_tensor(lnrow[0:1, 0:256], lnrow[0:1, 0:256], lnrow[0:1, 256:512], ALU.mult))
            v.drain()
            o.append(v.tensor_scalar(lnrow[0:1, 0:256], lnrow[0:1, 0:256],
                                     cN[0:1, C_NEG1:C_NEG1 + 1], None, ALU.mult))
            return o
        s.emit("vector", bfn, waits=[("se", s.at("se"))], inc="sv")
        rowd = s.at("sv")
        def bcfn(t_):
            return [
                t_.matmul(P[4][:, 0:TP], onesr32[0:1, :], lnrow[0:1, 256:512], start=True, stop=True),
                t_.matmul(P[5][:, 0:TP], onesr32[0:1, :], lnrow[0:1, 0:256], start=True, stop=True),
            ]
        s.emit("tensor", bcfn, waits=[("sv", rowd), ("se", s.at("se"))], inc="stn")
        bcd = s.at("stn")
        def apfn(v, ph=ph):
            outs = []
            for mc2 in range(0, NJC, 2):
                t0 = x2f[:, 0:256]
                t1 = x2f[:, 256:512]
                outs.append(v.tensor_tensor(
                    t0, sum32[:, mc2 * TP:(mc2 + 1) * TP], P[4][:, 0:TP], ALU.mult))
                outs.append(v.tensor_tensor(
                    t1, sum32[:, (mc2 + 1) * TP:(mc2 + 2) * TP], P[4][:, 0:TP], ALU.mult))
                v.drain()
                outs.append(v.tensor_tensor(
                    ap(xn, mc2 * T + ph * TP, [[PS(xn), 128], [1, TP]]),
                    t0, P[5][:, 0:TP], ALU.add))
                outs.append(v.tensor_tensor(
                    ap(xn, (mc2 + 1) * T + ph * TP, [[PS(xn), 128], [1, TP]]),
                    t1, P[5][:, 0:TP], ALU.add))
                v.drain()
            return outs
        s.emit("vector", apfn, waits=[("stn", bcd)], inc="sv")
        lastap = s.at("sv")
        def nopf(e):
            return e.nop()
        s.emit("sync", nopf, waits=[("sv", lastap)], inc="sr", amt=1)

    xn_sv = s.at("sv")

    if stop_after == "qkv" or stop_after == "attn":
        fin = s.at("sr")
        def earlyw(e):
            return e.nop()
        s.emit("sync", earlyw, waits=[("sr", fin)])
        block = st.enter_context(nc.Block())
        s.build(block)
        st.close()
        s._stack.close()
        return nc, tap_t

    if "xn" in taps:
        s.dma("sync", tap_t["txn"][:, :], xn[:, :], waits=[("sv", xn_sv)], inc="sr")

    if stop_after == "xn":
        fin = s.at("sr")
        def earlyw2(e):
            return e.nop()
        s.emit("sync", earlyw2, waits=[("sr", fin)])
        block = st.enter_context(nc.Block())
        s.build(block)
        st.close()
        s._stack.close()
        return nc, tap_t

    for (sdr, s16) in ((gscale, gs16), (uscale, us16), (dscale, ds16)):
        s.dma("gpsimd", ap(cstr, 0, [[PS(cstr), NGI], [1, HID]]), sdr[:, :],
              waits=[("sv", xn_sv), ("sr", s.at("sr")), ("se", s.at("se")), ("sx", s.at("sx"))], inc="sx")
        s.dma("sync", s16[:, :], ap(cstr, 0, [[PS(cstr), NGI], [1, HID]]),
              waits=[("sx", s.at("sx"))], inc="sr")

    PIECES = [(0, 512), (512, 512), (1024, 352)]
    for (zdr, sdr, cdr) in ((gzer, gsc_nat, cg_dram), (uzer, usc_nat, cu_dram)):
        for (p0, pw) in PIECES:
            s.dma("gpsimd", ap(zztmp, 0, [[PS(zztmp), 32], [1, pw]]).bitcast(I32),
                  zdr[:, p0:p0 + pw],
                  waits=[("sv", s.at("sv")), ("sx", s.at("sx")), ("sr", s.at("sr")),
                         ("se", s.at("se")), ("sh", s.at("sh")), ("stn", s.at("stn"))], inc="sx")
            s.dma("gpsimd", ap(xfm, 0, [[PS(xfm), 32], [1, pw * 8]]),
                  sdr[:, p0 * 8:(p0 + pw) * 8], inc="sx")
            zl = s.at("sx")
            for jj in range(8):
                def plfn(v, jj=jj, pw=pw):
                    o1 = v.tensor_scalar(ap(zztmp, 512, [[PS(zztmp), 32], [1, pw]]).bitcast(I32),
                                         ap(zztmp, 0, [[PS(zztmp), 32], [1, pw]]).bitcast(I32),
                                         4 * jj, 15, ALU.logical_shift_right, ALU.bitwise_and)
                    v.drain()
                    o2 = v.tensor_copy(ap(zztmp, 1024, [[PS(zztmp), 32], [1, pw]]),
                                       ap(zztmp, 512, [[PS(zztmp), 32], [1, pw]]).bitcast(I32))
                    v.drain()
                    o3 = v.tensor_scalar(ap(zztmp, 1024, [[PS(zztmp), 32], [1, pw]]),
                                         ap(zztmp, 1024, [[PS(zztmp), 32], [1, pw]]),
                                         -7.0, -1.0, ALU.add, ALU.mult)
                    v.drain()
                    o4 = v.tensor_copy(ap(w2, 0, [[PS(w2), 32], [1, pw]]),
                                       ap(zztmp, 1024, [[PS(zztmp), 32], [1, pw]]))
                    v.drain()
                    o5 = v.tensor_tensor(ap(w2, 1024, [[PS(w2), 32], [1, pw]]),
                                         ap(w2, 0, [[PS(w2), 32], [1, pw]]),
                                         ap(xfm, jj, [[PS(xfm), 32], [8, pw]]),
                                         ALU.mult)
                    return [o1, o2, o3, o4, o5]
                s.emit("vector", plfn, waits=[("sx", zl), ("sr", s.at("sr"))], inc="sv")
                pld = s.at("sv")
                dst = ap(cdr, p0 * 8 + jj, [[INTER, 32], [8, pw]])
                srcw = ap(w2, 1024, [[PS(w2), 32], [1, pw]])
                def cst_dma(e, dst=dst, srcw=srcw):
                    return e.dma_start(dst, srcw, allow_slow_non_contiguous=True)
                s.emit("sync", cst_dma, waits=[("sv", pld)], inc="sr", amt=16)
    s.dma("gpsimd", ap(xfm, 0, [[PS(xfm), 86], [1, HID]]), dscale[:, :],
          waits=[("sr", s.at("sr")), ("sv", s.at("sv")), ("stn", s.at("stn"))], inc="sx")
    s.dma("gpsimd", ap(zztmp, 0, [[PS(zztmp), 86], [1, 512]]).bitcast(I32), dzer[:, :],
          waits=[("sx", s.at("sx"))], inc="sx")
    dzl = s.at("sx")
    for jj in range(8):
        def pl2fn(v, jj=jj):
            o1 = v.tensor_scalar(ap(zztmp, 512, [[PS(zztmp), 86], [1, 512]]).bitcast(I32),
                                 ap(zztmp, 0, [[PS(zztmp), 86], [1, 512]]).bitcast(I32),
                                 4 * jj, 15, ALU.logical_shift_right, ALU.bitwise_and)
            v.drain()
            o2 = v.tensor_copy(ap(zztmp, 1024, [[PS(zztmp), 86], [1, 512]]),
                               ap(zztmp, 512, [[PS(zztmp), 86], [1, 512]]).bitcast(I32))
            v.drain()
            o3 = v.tensor_scalar(ap(zztmp, 1024, [[PS(zztmp), 86], [1, 512]]),
                                 ap(zztmp, 1024, [[PS(zztmp), 86], [1, 512]]),
                                 -7.0, -1.0, ALU.add, ALU.mult)
            v.drain()
            o4 = v.tensor_copy(ap(w2, 2048, [[PS(w2), 86], [1, 512]]),
                               ap(zztmp, 1024, [[PS(zztmp), 86], [1, 512]]))
            v.drain()
            o5 = v.tensor_tensor(ap(w2, 3072, [[PS(w2), 86], [1, 512]]),
                                 ap(w2, 2048, [[PS(w2), 86], [1, 512]]),
                                 ap(xfm, jj, [[PS(xfm), 86], [8, 512]]),
                                 ALU.mult)
            return [o1, o2, o3, o4, o5]
        s.emit("vector", pl2fn, waits=[("sx", dzl), ("sr", s.at("sr"))], inc="sv")
        pl2d = s.at("sv")
        dstcd = ap(cd_dram, jj, [[HID, 86], [8, 512]])
        srccd = ap(w2, 3072, [[PS(w2), 86], [1, 512]])
        def cddma(e, dstcd=dstcd, srccd=srccd):
            return e.dma_start(dstcd, srccd, allow_slow_non_contiguous=True)
        s.emit("sync", cddma, waits=[("sv", pl2d)], inc="sr", amt=16)
    cdone_sv = s.at("sv")
    cg_done_sr = s.at("sr")

    for tc in range(2):
        def xsfn(t_, tc=tc):
            outs = []
            for kc in range(NJC):
                outs.append(t_.matmul(
                    P[6][0:32, 0:512],
                    w32[:, 31 - kc:63 - kc],
                    ap(xn, kc * T + tc * 512, [[PS(xn), 128], [1, 512]]),
                    start=(kc == 0), stop=(kc == NJC - 1)))
            return outs
        s.emit("tensor", xsfn, waits=[("sv", xn_sv), ("se", s.at("se")), ("stn", s.at("stn"))], inc="stn")
        xsd = s.at("stn")
        def xsev(sc, tc=tc):
            return sc.copy(ap(xsb, tc * 512, [[PS(xsb), 32], [1, 512]]),
                           ap(P[6], 0, [[PS(P[6]), 32], [1, 512]]))
        s.emit("scalar", xsev, waits=[("stn", xsd)], inc="se")
    xs_done = s.at("se")

    for ncx in range(NGI):
        buf = ncx % 2
        if ncx % 8 == 0:
            cw = min(1024, (NGI - ncx) * 128)
            s.dma("sync", cbuf[0:32, 0:cw], cg_dram[:, ncx * 128: ncx * 128 + cw],
                  waits=[("stn", s.at("stn")), ("sr", cg_done_sr), ("sv", cdone_sv)], inc="sx")
            s.dma("sync", cbuf[0:32, 1024:1024 + cw], cu_dram[:, ncx * 128: ncx * 128 + cw],
                  inc="sx")
        cb_ld = s.at("sx")
        s.dma("sync", bstr[:, buf * HID:(buf + 1) * HID],
              gbytes[ncx * 128:(ncx + 1) * 128, :],
              waits=[("sv", s.at("sv"))], inc="sx")
        s.dma("sync", rstr[:, 0:HID],
              ap(gs16, ncx * NJC * 128, [[0, 128], [1, HID]]),
              waits=[("sh", s.at("sh")), ("sr", s.at("sr"))], inc="sx")
        gld = s.at("sx")
        def dqg(v, buf=buf, gld=gld):
            return v.tensor_scalar(ustr[:, :], bstr[:, buf * HID:(buf + 1) * HID],
                                   shf[:, 0:1], 15, ALU.logical_shift_right, ALU.bitwise_and)
        s.emit("vector", dqg, waits=[("sx", gld), ("se", s.at("se")), ("stn", s.at("stn"))], inc="sv")
        uq = s.at("sv")
        def cst(sc):
            return sc.activation(cstr[:, :], ustr[:, :], AF.Copy, bias=-8.0)
        s.emit("scalar", cst, waits=[("sv", uq), ("sh", s.at("sh"))], inc="se")
        csd = s.at("se")
        def sclm(g, buf=buf, csd=csd):
            return g.tensor_tensor(wstr[:, buf * HID:(buf + 1) * HID], cstr[:, :],
                                   rstr[:, 0:HID], ALU.mult)
        s.emit("vector", sclm, waits=[("se", csd)], inc="sh")
        wg = s.at("sh")
        for tc in range(2):
            def gmm(t_, buf=buf, tc=tc, ncx=ncx):
                outs = []
                for kc in range(NJC):
                    outs.append(t_.matmul(
                        P[tc][:, 0:512],
                        wstr[:, buf * HID + kc * 128: buf * HID + (kc + 1) * 128],
                        ap(xn, kc * T + tc * 512, [[PS(xn), 128], [1, 512]]),
                        start=(kc == 0), stop=False))
                outs.append(t_.matmul(
                    P[tc][:, 0:512],
                    ap(cbuf, (ncx % 8) * 128, [[PS(cbuf), 32], [1, 128]]),
                    ap(xsb, tc * 512, [[PS(xsb), 32], [1, 512]]),
                    start=False, stop=True))
                return outs
            s.emit("tensor", gmm, waits=[("sh", wg), ("se", xs_done), ("sv", s.at("sv")), ("sx", cb_ld)], inc="stn")
        gmd = s.at("stn")
        def silu(sc, gmd=gmd):
            return [sc.activation(qsub[:, 0:512], P[0][:, 0:512], AF.Silu),
                    sc.activation(qsub[:, 512:1024], P[1][:, 0:512], AF.Silu)]
        s.emit("scalar", silu, waits=[("stn", gmd), ("sv", s.at("sv")), ("sr", s.at("sr"))], inc="se")
        sild = s.at("se")
        s.dma("sync", bstr[:, buf * HID:(buf + 1) * HID],
              ubytes[ncx * 128:(ncx + 1) * 128, :],
              waits=[("sh", wg)], inc="sx")
        s.dma("sync", rstr[:, 0:HID],
              ap(us16, ncx * NJC * 128, [[0, 128], [1, HID]]),
              waits=[("sh", wg)], inc="sx")
        uld = s.at("sx")
        def dqu(v, buf=buf, uld=uld):
            return v.tensor_scalar(ustr[:, :], bstr[:, buf * HID:(buf + 1) * HID],
                                   shf[:, 0:1], 15, ALU.logical_shift_right, ALU.bitwise_and)
        s.emit("vector", dqu, waits=[("sx", uld), ("se", sild)], inc="sv")
        uq2 = s.at("sv")
        def cst2(sc):
            return sc.activation(cstr[:, :], ustr[:, :], AF.Copy, bias=-8.0)
        s.emit("scalar", cst2, waits=[("sv", uq2), ("sh", s.at("sh"))], inc="se")
        csd2 = s.at("se")
        def sclm2(g, buf=buf, csd2=csd2):
            return g.tensor_tensor(wstr[:, buf * HID:(buf + 1) * HID], cstr[:, :],
                                   rstr[:, 0:HID], ALU.mult)
        s.emit("vector", sclm2, waits=[("se", csd2)], inc="sh")
        wu = s.at("sh")
        for tc in range(2):
            def umm(t_, buf=buf, tc=tc, ncx=ncx):
                outs = []
                for kc in range(NJC):
                    outs.append(t_.matmul(
                        P[2 + tc][:, 0:512],
                        wstr[:, buf * HID + kc * 128: buf * HID + (kc + 1) * 128],
                        ap(xn, kc * T + tc * 512, [[PS(xn), 128], [1, 512]]),
                        start=(kc == 0), stop=False))
                outs.append(t_.matmul(
                    P[2 + tc][:, 0:512],
                    ap(cbuf, 1024 + (ncx % 8) * 128, [[PS(cbuf), 32], [1, 128]]),
                    ap(xsb, tc * 512, [[PS(xsb), 32], [1, 512]]),
                    start=False, stop=True))
                return outs
            s.emit("tensor", umm, waits=[("sh", wu), ("sv", s.at("sv")), ("se", s.at("se"))], inc="stn")
        umd = s.at("stn")
        def hfn(v, umd=umd):
            return [v.tensor_tensor(ksub[:, 0:512], qsub[:, 0:512], P[2][:, 0:512], ALU.mult),
                    v.tensor_tensor(ksub[:, 512:1024], qsub[:, 512:1024], P[3][:, 0:512], ALU.mult)]
        s.emit("vector", hfn, waits=[("stn", umd), ("sr", s.at("sr"))], inc="sv")
        hd = s.at("sv")
        def x2fn(t_, ncx=ncx):
            outs = []
            for tc in range(2):
                outs.append(t_.matmul(
                    P[4 + tc][0:86, 0:512],
                    w86[:, 85 - ncx:171 - ncx],
                    ksub[:, tc * 512:(tc + 1) * 512],
                    start=(ncx == 0), stop=(ncx == NGI - 1)))
            return outs
        s.emit("tensor", x2fn, waits=[("sv", hd)], inc="stn")
        s.dma("sync", h_dram[ncx * 128:(ncx + 1) * 128, :], ksub[:, 0:1024],
              waits=[("sv", hd)], inc="sr")
    gate_done_sr = s.at("sr")
    x2mm = s.at("stn")
    def x2ev(sc):
        return [sc.copy(ap(x2sb, 0, [[PS(x2sb), 86], [1, 512]]),
                        ap(P[4], 0, [[PS(P[4]), 86], [1, 512]])),
                sc.copy(ap(x2sb, 512, [[PS(x2sb), 86], [1, 512]]),
                        ap(P[5], 0, [[PS(P[5]), 86], [1, 512]]))]
    s.emit("scalar", x2ev, waits=[("stn", x2mm)], inc="se")
    x2d = s.at("se")

    if "h" in taps:
        s.dma("sync", tap_t["th"][:, :], h_dram[:, :], waits=[("sr", gate_done_sr)], inc="sr")

    if stop_after == "h":
        fin = s.at("sr")
        x2w = s.at("se")
        def earlyw3(e):
            return e.nop()
        s.emit("sync", earlyw3, waits=[("sr", fin), ("se", x2w)])
        block = st.enter_context(nc.Block())
        s.build(block)
        st.close()
        s._stack.close()
        return nc, tap_t

    for jc in range(NGI):
        s.dma("sync", bstr[:, 0:HID],
              dbytes[jc * 128:(jc + 1) * 128, :],
              waits=[("sh", s.at("sh")), ("sr", s.at("sr")), ("sv", s.at("sv"))], inc="sx")
        s.dma("sync", rstr[:, 0:HID],
              ap(ds16, jc * HID, [[0, 128], [1, HID]]),
              waits=[("sh", s.at("sh"))], inc="sx")
        dld = s.at("sx")
        def dqd(v, dld=dld):
            return v.tensor_scalar(ustr[:, :], bstr[:, 0:HID],
                                   shf[:, 0:1], 15, ALU.logical_shift_right, ALU.bitwise_and)
        s.emit("vector", dqd, waits=[("sx", dld), ("se", s.at("se"))], inc="sv")
        uq3 = s.at("sv")
        def cst3(sc):
            return sc.activation(cstr[:, :], ustr[:, :], AF.Copy, bias=-8.0)
        s.emit("scalar", cst3, waits=[("sv", uq3), ("sh", s.at("sh"))], inc="se")
        csd3 = s.at("se")
        def sclm3(g, csd3=csd3):
            return g.tensor_tensor(wstr[:, 0:HID], cstr[:, :], rstr[:, 0:HID], ALU.mult)
        s.emit("vector", sclm3, waits=[("se", csd3), ("sr", s.at("sr"))], inc="sh")
        wdd = s.at("sh")
        s.dma("sync", wd_dram[jc * 128:(jc + 1) * 128, :], wstr[:, 0:HID],
              waits=[("sh", wdd)], inc="sr")
    wd_done = s.at("sr")

    if "wd" in taps:
        s.dma("sync", tap_t["twd"][:, :], wd_dram[0:256, :], waits=[("sr", wd_done)], inc="sr")

    for mb in range(8):
        s.dma("sync", cdnb[0:86, 0:512], cd_dram[:, mb * 512:(mb + 1) * 512],
              waits=[("stn", s.at("stn")), ("sr", wd_done)], inc="sx")
        cdl = s.at("sx")
        for th in range(2):
            for jg in range(0, NGI, 4):
                nj = min(4, NGI - jg)
                buf = (jg // 4) % 2
                wsrc = ap(wd_dram, jg * 128 * HID + mb * 512,
                          [[HID, 128], [128 * HID, nj], [1, 512]])
                s.dma("sync", ap(wstr, buf * 2048, [[PS(wstr), 128], [512, nj], [1, 512]]),
                      wsrc,
                      waits=[("stn", s.at("stn")), ("sr", wd_done), ("sh", s.at("sh"))], inc="sx")
                hsrc = ap(h_dram, jg * 128 * T + th * 512,
                          [[T, 128], [128 * T, nj], [1, 512]])
                s.dma("sync", ap(hdown, buf * 2048, [[PS(hdown), 128], [512, nj], [1, 512]]),
                      hsrc,
                      waits=[("sr", gate_done_sr)], inc="sx")
                dl = s.at("sx")
                def dmm(t_, buf=buf, jg=jg, nj=nj):
                    outs = []
                    for j2 in range(nj):
                        for tb4 in range(4):
                            outs.append(t_.matmul(
                                P[tb4][:, 0:512],
                                ap(hdown, buf * 2048 + j2 * 512 + tb4 * 128,
                                   [[PS(hdown), 128], [1, 128]]),
                                ap(wstr, buf * 2048 + j2 * 512, [[PS(wstr), 128], [1, 512]]),
                                start=(jg + j2 == 0), stop=False))
                    return outs
                s.emit("tensor", dmm, waits=[("sx", dl), ("sv", s.at("sv")), ("se", x2d)], inc="stn")
            def dcor(t_, mb=mb, th=th):
                outs = []
                for tb4 in range(4):
                    tbg = th * 4 + tb4
                    outs.append(t_.matmul(
                        P[tb4][:, 0:512],
                        ap(x2sb, tbg * 128, [[PS(x2sb), 86], [1, 128]]),
                        ap(cdnb, 0, [[PS(cdnb), 86], [1, 512]]),
                        start=False, stop=True))
                return outs
            s.emit("tensor", dcor, waits=[("se", x2d), ("sx", cdl)], inc="stn")
            dmd = s.at("stn")
            if "dp0" in taps and mb == 0 and th == 0:
                tap_t["tdp"] = nc.dram_tensor("tdp", [128, 512], F32, kind="ExternalOutput")
                def dpcp(sc):
                    return sc.copy(x2f[:, 0:512], P[0][:, 0:512])
                s.emit("scalar", dpcp, waits=[("stn", dmd)], inc="se")
                s.dma("sync", tap_t["tdp"][:, :], x2f[:, 0:512], waits=[("se", s.at("se"))], inc="sr")
            for tb4 in range(4):
                tbg = th * 4 + tb4
                def xtf(t_, mb=mb, tbg=tbg):
                    outs = []
                    for mcc in range(4):
                        mc = mb * 4 + mcc
                        outs.append(t_.transpose(
                            ap(PT, mcc * 128, [[PS(PT), 128], [1, 128]]),
                            ap(xn, mc * T + tbg * 128, [[PS(xn), 128], [1, 128]]),
                            idm2[:, :]))
                    return outs
                s.emit("tensor", xtf, waits=[("se", s.at("se")), ("stn", dmd)], inc="stn")
                xtd = s.at("stn")
                def xte(sc):
                    return sc.copy(ap(wsb2, 0, [[PS(wsb2), 128], [1, 512]]),
                                   ap(PT, 0, [[PS(PT), 128], [1, 512]]))
                s.emit("scalar", xte, waits=[("stn", xtd)], inc="se")
                xted = s.at("se")
                def res2(v, mb=mb, tbg=tbg, tb4=tb4):
                    return v.tensor_tensor(
                        ap(Abig, tbg * 4096 + mb * 512, [[PS(Abig), 128], [1, 512]]),
                        P[tb4][:, 0:512],
                        ap(wsb2, 0, [[PS(wsb2), 128], [1, 512]]),
                        ALU.add)
                s.emit("vector", res2, waits=[("se", xted)], inc="sv")
                r2d = s.at("sv")
                def sq2(sc, tbg=tbg, mb=mb):
                    return sc.activation(ap(w2, 0, [[PS(w2), 128], [1, 512]]),
                                         ap(Abig, tbg * 4096 + mb * 512, [[PS(Abig), 128], [1, 512]]),
                                         AF.Square)
                s.emit("scalar", sq2, waits=[("sv", r2d)], inc="se")
                sq2d = s.at("se")
                def st2(v, tbg=tbg, mb=mb):
                    a = v.reduce_sum(stats[:, tbg * 8 + mb: tbg * 8 + mb + 1],
                                     ap(Abig, tbg * 4096 + mb * 512, [[PS(Abig), 128], [1, 512]]),
                                     axis=AX.X)
                    b = v.reduce_sum(stats[:, 64 + tbg * 8 + mb: 64 + tbg * 8 + mb + 1],
                                     ap(w2, 0, [[PS(w2), 128], [1, 512]]), axis=AX.X)
                    return [a, b]
                s.emit("vector", st2, waits=[("se", sq2d)], inc="sv")
    down_sv = s.at("sv")
    if "sum2" in taps:
        tap_t["ts2"] = nc.dram_tensor("ts2", [128, 32768], BF16, kind="ExternalOutput")
        s.dma("sync", tap_t["ts2"][:, :], ap(Abig, 0, [[PS(Abig), 128], [1, 32768]]),
              waits=[("sv", down_sv)], inc="sr")

    def fin1(v):
        return [
            v.reduce_sum(stats[:, 128:136], ap(stats, 0, [[PS(stats), 128], [8, 8], [1, 8]]), axis=AX.X),
            v.reduce_sum(stats[:, 136:144], ap(stats, 64, [[PS(stats), 128], [8, 8], [1, 8]]), axis=AX.X),
        ]
    s.emit("vector", fin1, waits=[("sv", down_sv)], inc="sv")
    def fin2(v):
        o = [v.tensor_scalar(stats[:, 128:136], stats[:, 128:136], cN[:, C_INV4096:C_INV4096 + 1], None, ALU.mult),
             v.tensor_scalar(stats[:, 136:144], stats[:, 136:144], cN[:, C_INV4096:C_INV4096 + 1], None, ALU.mult)]
        v.drain()
        o.append(v.tensor_tensor(stats[:, 160:168], stats[:, 128:136], stats[:, 128:136], ALU.mult))
        v.drain()
        o.append(v.tensor_tensor(stats[:, 160:168], stats[:, 136:144], stats[:, 160:168], ALU.subtract))
        v.drain()
        o.append(v.tensor_scalar(stats[:, 160:168], stats[:, 160:168], cN[:, C_EPS:C_EPS + 1], None, ALU.add))
        return o
    s.emit("vector", fin2, waits=[("sv", s.at("sv"))], inc="sv")
    def fin3(sc):
        return sc.activation(stats[:, 144:152], stats[:, 160:168], AF.Sqrt)
    s.emit("scalar", fin3, waits=[("sv", s.at("sv"))], inc="se")
    def fin4(v):
        r0 = v.reciprocal(stats[:, 144:152], stats[:, 144:152])
        v.drain()
        a = v.tensor_tensor(stats[:, 152:160], stats[:, 128:136], stats[:, 144:152], ALU.mult)
        v.drain()
        b = v.tensor_scalar(stats[:, 152:160], stats[:, 152:160], cN[:, C_NEG1:C_NEG1 + 1], None, ALU.mult)
        return [r0, a, b]
    s.emit("vector", fin4, waits=[("se", s.at("se"))], inc="sv")
    ln2d = s.at("sv")

    for tb in range(8):
        for piece in range(4):
            def outp(v, tb=tb, piece=piece):
                return v.tensor_scalar(
                    x2f[:, :],
                    ap(Abig, tb * 4096 + piece * 1024, [[PS(Abig), 128], [1, 1024]]),
                    stats[:, 144 + tb:145 + tb], stats[:, 152 + tb:153 + tb],
                    ALU.mult, ALU.add)
            s.emit("vector", outp, waits=[("sv", ln2d), ("sr", s.at("sr"))], inc="sv")
            od = s.at("sv")
            s.dma("sync", y[tb * 128:(tb + 1) * 128, piece * 1024:(piece + 1) * 1024],
                  x2f[:, :], waits=[("sv", od)], inc="sr")
    final_sr = s.at("sr")
    def finw(e):
        return e.nop()
    s.emit("sync", finw, waits=[("sr", final_sr)])

    block = st.enter_context(nc.Block())
    s.build(block)
    st.close()
    s._stack.close()
    return nc, tap_t


def _host_stage(inputs):
    d = {}
    f32 = np.float32

    def restage_w(w):
        wt = np.ascontiguousarray(np.asarray(w, f32).T)
        return np.ascontiguousarray(
            wt.reshape(NJC, 128, NJC, 128).transpose(2, 1, 0, 3).reshape(NJC * 128, HID))

    d["wqt"] = restage_w(inputs["L0_attn_Wq"])
    d["wkt"] = restage_w(inputs["L0_attn_Wk"])
    d["wvt"] = restage_w(inputs["L0_attn_Wv"])
    wot_nat = np.ascontiguousarray(np.asarray(inputs["L0_attn_Wo"], f32).T)
    hp = np.arange(4096) // 64
    dd = np.arange(4096) % 64
    imap = (2 * (hp % 32) + hp // 32) * 64 + dd
    wot_p = wot_nat[imap]
    d["wot"] = np.ascontiguousarray(
        wot_p.reshape(NJC, 128, NJC, 128).transpose(2, 1, 0, 3).reshape(NJC * 128, HID))

    def tile3(a, b, c):
        out = np.zeros((128, 3), f32)
        out[:, 0] = np.tile(np.asarray(a, f32), 2)
        out[:, 1] = np.tile(np.asarray(b, f32), 2)
        out[:, 2] = np.tile(np.asarray(c, f32), 2)
        return out
    d["maskq"] = tile3(inputs["L0_attn_mu_q"], inputs["L0_attn_sigma_q"], inputs["L0_attn_eps_q"])
    d["maskkv"] = tile3(inputs["L0_attn_mu_kv"], inputs["L0_attn_sigma_kv"], inputs["L0_attn_eps_kv"])

    cv = np.zeros((128, 8), f32)
    cv[:, C_EXPB] = -4.0
    cv[:, C_INV4096] = 1.0 / 4096.0
    cv[:, C_EPS] = 1e-5
    cv[:, C_NEG1] = -1.0
    cv[:, C_ONE] = 1.0
    cv[:, C_NEG8] = -8.0
    d["consts"] = cv

    idm = np.zeros((128, 128), bfdt)
    idm[0:64, 0:64] = np.eye(64)
    idm[64:128, 0:64] = np.eye(64)
    d["idmat"] = idm
    d["idmat2"] = np.eye(128).astype(bfdt)
    o1 = np.zeros((128, 2), bfdt)
    o1[:, 0] = 1
    d["ones128"] = o1
    d["onesrow"] = np.ones((1, 128), bfdt)
    d["onesrow32"] = np.ones((1, 128), f32)
    w32_ = np.zeros((128, 63), bfdt)
    w32_[:, 31] = 1
    d["wide32"] = w32_
    w86_ = np.zeros((128, 171), bfdt)
    w86_[:, 85] = 1
    d["wide86"] = w86_
    d["shiftpp"] = np.tile(np.array([0, 4], np.uint8), 64).reshape(128, 1)

    def bytes_rep(qw):
        K8, N = qw.shape
        b = np.ascontiguousarray(qw).view(np.uint8).reshape(K8, N, 4).transpose(0, 2, 1).reshape(K8 * 4, N)
        return np.repeat(b, 2, axis=0)

    gb = bytes_rep(np.asarray(inputs["L0_ffn_gate_qweight"], np.int32))
    ub = bytes_rep(np.asarray(inputs["L0_ffn_up_qweight"], np.int32))
    db = bytes_rep(np.asarray(inputs["L0_ffn_down_qweight"], np.int32))
    d["gbytes"] = np.ascontiguousarray(
        gb.reshape(NJC, 128, NGI, 128).transpose(2, 1, 0, 3).reshape(NGI * 128, NJC * 128))
    d["ubytes"] = np.ascontiguousarray(
        ub.reshape(NJC, 128, NGI, 128).transpose(2, 1, 0, 3).reshape(NGI * 128, NJC * 128))
    d["dbytes"] = np.ascontiguousarray(db)

    gs = np.asarray(inputs["L0_ffn_gate_scales"], f32)
    us = np.asarray(inputs["L0_ffn_up_scales"], f32)
    ds = np.asarray(inputs["L0_ffn_down_scales"], f32)
    d["gscale"] = np.ascontiguousarray(gs.reshape(NJC, NGI, 128).transpose(1, 0, 2).reshape(NGI, NJC * 128))
    d["uscale"] = np.ascontiguousarray(us.reshape(NJC, NGI, 128).transpose(1, 0, 2).reshape(NGI, NJC * 128))
    d["dscale"] = np.ascontiguousarray(ds)
    d["gsc_nat"] = np.ascontiguousarray(gs)
    d["usc_nat"] = np.ascontiguousarray(us)
    d["gzer"] = np.ascontiguousarray(np.asarray(inputs["L0_ffn_gate_qzeros"], np.int32))
    d["uzer"] = np.ascontiguousarray(np.asarray(inputs["L0_ffn_up_qzeros"], np.int32))
    d["dzer"] = np.ascontiguousarray(np.asarray(inputs["L0_ffn_down_qzeros"], np.int32))
    return d


def kernel(**inputs):
    x = np.ascontiguousarray(np.asarray(inputs["L0_attn_hidden_states"], np.float32).reshape(-1, HID))
    shared = _host_stage(inputs)
    nc, _ = build_nc()
    in_maps = []
    for c in range(NCORE):
        m = dict(shared)
        m["xin"] = np.ascontiguousarray(x[c * T:(c + 1) * T])
        in_maps.append(m)
    res = run_bass_kernel_spmd(nc, in_maps, core_ids=list(range(NCORE)))
    out = np.concatenate([r["y"] for r in res.results], axis=0)
    return np.ascontiguousarray(out.reshape(2, 4096, HID))


# version 05 (final)
# speedup vs baseline: 1.1333x; 1.0072x over previous
import numpy as np
import ml_dtypes
from contextlib import ExitStack

import concourse.bass as bass
import concourse.mybir as mybir
from concourse.bass_utils import run_bass_kernel_spmd

F32 = mybir.dt.float32
BF16 = mybir.dt.bfloat16
U8 = mybir.dt.uint8
I32 = mybir.dt.int32
AF = mybir.ActivationFunctionType
ALU = mybir.AluOpType
AX = mybir.AxisListType
bfdt = ml_dtypes.bfloat16

NCORE = 8
T = 1024
HID = 4096
INTER = 11008
NJC = 32
NGI = 86
TP = 256
NPH = 4
SUB = 64
NSUB = 4

C_EXPB, C_INV4096, C_EPS, C_NEG1, C_ONE, C_NEG8 = 0, 1, 2, 3, 4, 5

ENGINES = ("tensor", "vector", "scalar", "gpsimd", "sync")


class Sched:
    def __init__(self, nc):
        self.nc = nc
        self.q = {e: [] for e in ENGINES}
        self.cnt = {}
        self.sems = {}
        self._stack = ExitStack()

    def sem(self, name):
        self.sems[name] = self._stack.enter_context(self.nc.semaphore(name))
        self.cnt[name] = 0

    def at(self, name):
        return self.cnt[name]

    def emit(self, eng, fn, waits=(), inc=None, amt=1):
        if inc is not None:
            self.cnt[inc] += amt
        self.q[eng].append((list(waits), fn, inc, amt))

    def dma(self, eng, out, in_, waits=(), inc=None):
        def fn(e, out=out, in_=in_):
            return e.dma_start(out, in_)
        self.emit(eng, fn, waits=waits, inc=inc, amt=16)

    def build(self, block):
        for eng in ENGINES:
            items = self.q[eng]
            if not items:
                continue

            def body(e, items=items):
                last_wait = {}
                for waits, fn, inc, amt in items:
                    for sname, v in waits:
                        if v > 0 and last_wait.get(sname, -1) < v:
                            e.wait_ge(self.sems[sname], v)
                            last_wait[sname] = v
                    ins = fn(e)
                    if inc is not None:
                        if isinstance(ins, list):
                            ins = ins[-1]
                        ins.then_inc(self.sems[inc], amt)
            getattr(block, eng)(body)


def ap(t, offset, dims):
    return bass.AP(t, offset, dims)


def build_nc(taps=(), stop_after="full", nph=NPH, attn_stage="full"):
    nc = bass.Bass("TRN2", target_bir_lowering=False)

    xin = nc.dram_tensor("xin", [T, HID], F32, kind="ExternalInput")
    wqt = nc.dram_tensor("wqt", [NJC * 128, HID], F32, kind="ExternalInput")
    wkt = nc.dram_tensor("wkt", [NJC * 128, HID], F32, kind="ExternalInput")
    wvt = nc.dram_tensor("wvt", [NJC * 128, HID], F32, kind="ExternalInput")
    wot = nc.dram_tensor("wot", [NJC * 128, HID], F32, kind="ExternalInput")
    maskq = nc.dram_tensor("maskq", [128, 3], F32, kind="ExternalInput")
    maskkv = nc.dram_tensor("maskkv", [128, 3], F32, kind="ExternalInput")
    consts = nc.dram_tensor("consts", [128, 8], F32, kind="ExternalInput")
    idmat = nc.dram_tensor("idmat", [128, 128], BF16, kind="ExternalInput")
    idmat2 = nc.dram_tensor("idmat2", [128, 128], BF16, kind="ExternalInput")
    ones128 = nc.dram_tensor("ones128", [128, 2], BF16, kind="ExternalInput")
    onesrow = nc.dram_tensor("onesrow", [1, 128], BF16, kind="ExternalInput")
    onesrow32 = nc.dram_tensor("onesrow32", [1, 128], F32, kind="ExternalInput")
    wide32 = nc.dram_tensor("wide32", [128, 63], BF16, kind="ExternalInput")
    wide86 = nc.dram_tensor("wide86", [128, 171], BF16, kind="ExternalInput")
    gbytes = nc.dram_tensor("gbytes", [NGI * 128, NJC * 128], U8, kind="ExternalInput")
    ubytes = nc.dram_tensor("ubytes", [NGI * 128, NJC * 128], U8, kind="ExternalInput")
    dbytes = nc.dram_tensor("dbytes", [NGI * 128, HID], U8, kind="ExternalInput")
    gscale = nc.dram_tensor("gscale", [NGI, NJC * 128], F32, kind="ExternalInput")
    uscale = nc.dram_tensor("uscale", [NGI, NJC * 128], F32, kind="ExternalInput")
    dscale = nc.dram_tensor("dscale", [NGI, HID], F32, kind="ExternalInput")
    gsc_nat = nc.dram_tensor("gsc_nat", [NJC, INTER], F32, kind="ExternalInput")
    usc_nat = nc.dram_tensor("usc_nat", [NJC, INTER], F32, kind="ExternalInput")
    gzer = nc.dram_tensor("gzer", [NJC, INTER // 8], I32, kind="ExternalInput")
    uzer = nc.dram_tensor("uzer", [NJC, INTER // 8], I32, kind="ExternalInput")
    dzer = nc.dram_tensor("dzer", [NGI, HID // 8], I32, kind="ExternalInput")
    shiftpp = nc.dram_tensor("shiftpp", [128, 1], U8, kind="ExternalInput")

    y = nc.dram_tensor("y", [T, HID], F32, kind="ExternalOutput")

    h_dram = nc.dram_tensor("h_scr", [NGI * 128, T], BF16, kind="Internal")
    wd_dram = nc.dram_tensor("wd_scr", [NGI * 128, HID], BF16, kind="Internal")
    cg_dram = nc.dram_tensor("cg_scr", [32, INTER], BF16, kind="Internal")
    cu_dram = nc.dram_tensor("cu_scr", [32, INTER], BF16, kind="Internal")
    cd_dram = nc.dram_tensor("cd_scr", [NGI, HID], BF16, kind="Internal")
    gs16 = nc.dram_tensor("gs16_scr", [NGI, NJC * 128], BF16, kind="Internal")
    us16 = nc.dram_tensor("us16_scr", [NGI, NJC * 128], BF16, kind="Internal")
    ds16 = nc.dram_tensor("ds16_scr", [NGI, HID], BF16, kind="Internal")

    tap_t = {}
    if "qkv" in taps:
        for n in ("tq", "tk", "tv"):
            tap_t[n] = nc.dram_tensor(n, [128, 8192], BF16, kind="ExternalOutput")
    if "attn" in taps:
        tap_t["to"] = nc.dram_tensor("to", [128, 8192], BF16, kind="ExternalOutput")
    if "xn" in taps:
        tap_t["txn"] = nc.dram_tensor("txn", [128, NJC * T], BF16, kind="ExternalOutput")
    if "h" in taps:
        tap_t["th"] = nc.dram_tensor("th", [NGI * 128, T], BF16, kind="ExternalOutput")
    if "wd" in taps:
        tap_t["twd"] = nc.dram_tensor("twd", [256, HID], BF16, kind="ExternalOutput")

    st = ExitStack()
    sb = lambda name, shape, dt: st.enter_context(nc.sbuf_tensor(name, shape, dt))
    psal = lambda name, shape, dt: st.enter_context(nc.psum_tensor(name, shape, dt))

    xn = sb("xn", [128, NJC * T], BF16)
    xfm = sb("xfm", [128, NJC * TP], BF16)
    Abig = sb("Abig", [128, 32768], BF16)
    wstr = sb("wstr", [128, 2 * HID], BF16)
    bstr = sb("bstr", [128, 2 * HID], U8)
    rstr = sb("rstr", [128, HID], BF16)
    soar = sb("soar", [128, 4096], BF16)
    wsb2 = sb("wsb2", [128, 4096], BF16)
    qsub = sb("qsub", [128, 2048], BF16)
    ksub = sb("ksub", [128, 2048], BF16)
    vsub = sb("vsub", [128, 2048], BF16)
    zsb = sb("zsb", [128, 32], F32)
    rzbf = sb("rzbf", [128, 32], BF16)
    addr = lambda t_: nc.lookup_mloc(t_).addr
    w2 = nc.alloc_sbuf_tensor_at("w2", [128, 4096], BF16, offset=addr(bstr))
    cstr = nc.alloc_sbuf_tensor_at("cstr", [128, HID], BF16, offset=addr(Abig))
    sum32 = nc.alloc_sbuf_tensor_at("sum32", [128, 8192], F32, offset=addr(Abig))
    cbuf = nc.alloc_sbuf_tensor_at("cbuf", [128, 2048], BF16, offset=addr(xfm))
    assert addr(ksub) == addr(qsub) + 4096, (addr(qsub), addr(ksub))
    xtm = nc.alloc_sbuf_tensor_at("xtm", [128, 4096], BF16, offset=addr(qsub))
    ustr = nc.alloc_sbuf_tensor_at("ustr", [128, HID], U8, offset=addr(vsub))
    hdown = nc.alloc_sbuf_tensor_at("hdown", [128, 4096], BF16, offset=addr(soar))
    zztmp = nc.alloc_sbuf_tensor_at("zztmp", [128, 2112], F32, offset=addr(wstr))
    lnrow = nc.alloc_sbuf_tensor_at("lnrow", [128, 768], F32, offset=addr(soar))
    lnbf = nc.alloc_sbuf_tensor_at("lnbf", [128, 512], BF16, offset=addr(soar) + 3072)
    stats = nc.alloc_sbuf_tensor_at("stats", [128, 192], F32, offset=addr(rstr))
    wts = nc.alloc_sbuf_tensor_at("wts", [128, 2048], BF16, offset=addr(rstr))
    vts = nc.alloc_sbuf_tensor_at("vts", [128, 2048], BF16, offset=addr(rstr) + 4096)
    x2f = nc.alloc_sbuf_tensor_at("x2f", [128, 1024], F32, offset=addr(soar) + 4096)
    cdnb = nc.alloc_sbuf_tensor_at("cdnb", [128, 512], BF16, offset=addr(vsub))
    x2sb = nc.alloc_sbuf_tensor_at("x2sb", [128, T], BF16, offset=addr(wsb2) + 4096)
    xsb = nc.alloc_sbuf_tensor_at("xsb", [128, T], BF16, offset=addr(wsb2) + 6144)
    cN = sb("cN", [128, 8], F32)
    nc.const_aps.register(F32, -8.0, cN[:, C_NEG8:C_NEG8 + 1])
    mq8 = sb("mq8", [128, 4], F32)
    mkv = sb("mkv", [128, 4], F32)
    idm = sb("idm", [128, 128], BF16)
    idm2 = sb("idm2", [128, 128], BF16)
    ones1 = sb("ones1", [128, 2], BF16)
    onesr = sb("onesr", [1, 128], BF16)
    onesr32 = sb("onesr32", [1, 128], F32)
    w32 = sb("w32", [128, 63], BF16)
    w86 = sb("w86", [128, 171], BF16)
    shf = sb("shf", [128, 1], U8)

    P = [psal(f"P{i}", [128, 512], F32) for i in range(7)]
    PT = psal("PT", [128, 1024], BF16)

    PSx = {}
    def PS(t_):
        k = id(t_)
        if k not in PSx:
            PSx[k] = t_.ap().ap[0][0]
        return PSx[k]

    s = Sched(nc)
    for name in ("sx", "sw", "sr", "se", "sv", "stn", "sh"):
        s.sem(name)

    QOF, KOF, VOF, OOF = 0, 8192, 16384, 24576

    for dst, src in ((cN, consts), (idm, idmat), (idm2, idmat2), (ones1, ones128),
                     (w32, wide32), (w86, wide86), (shf, shiftpp)):
        s.dma("gpsimd", dst[:, :], src[:, :], inc="sx")
    s.dma("gpsimd", onesr[:, :], onesrow[:, :], inc="sx")
    s.dma("gpsimd", onesr32[:, :], onesrow32[:, :], inc="sx")
    s.dma("gpsimd", mq8[:, 0:3], maskq[:, :], inc="sx")
    s.dma("gpsimd", mkv[:, 0:3], maskkv[:, :], inc="sx")
    pro_ld = s.at("sx")

    def mk_masks(v):
        a = v.tensor_tensor(mq8[:, 3:4], mq8[:, 2:3], mq8[:, 1:2], ALU.mult)
        b = v.tensor_tensor(mkv[:, 3:4], mkv[:, 2:3], mkv[:, 1:2], ALU.mult)
        v.drain()
        c = v.tensor_tensor(mq8[:, 3:4], mq8[:, 3:4], mq8[:, 0:1], ALU.add)
        d = v.tensor_tensor(mkv[:, 3:4], mkv[:, 3:4], mkv[:, 0:1], ALU.add)
        return [a, b, c, d]
    s.emit("vector", mk_masks, waits=[("sx", pro_ld)], inc="sv")

    def mk_sig(sc):
        return [sc.activation(mq8[:, 0:1], mq8[:, 3:4], AF.Sigmoid),
                sc.activation(mkv[:, 0:1], mkv[:, 3:4], AF.Sigmoid)]
    s.emit("scalar", mk_sig, waits=[("sv", s.at("sv"))], inc="se")

    def mk_mq8(v):
        return v.tensor_scalar(mq8[:, 0:1], mq8[:, 0:1], 0.125, None, ALU.mult)
    s.emit("vector", mk_mq8, waits=[("se", s.at("se"))], inc="sv")
    masks_sv = s.at("sv")
    masks_se = s.at("se")

    if "masks" in taps:
        tap_t["tmask"] = nc.dram_tensor("tmask", [128, 8], F32, kind="ExternalOutput")
        s.dma("sync", tap_t["tmask"][:, 0:4], mq8[:, 0:4],
              waits=[("sv", masks_sv), ("se", masks_se)], inc="sr")
        s.dma("sync", tap_t["tmask"][:, 4:8], mkv[:, 0:4],
              waits=[("sv", masks_sv), ("se", masks_se)], inc="sr")

    projs = [("q", wqt, QOF, True), ("k", wkt, KOF, False), ("v", wvt, VOF, False)]

    for ph in range(nph):
        t0 = ph * TP
        for g in range(2):
            s.dma("gpsimd", xtm[:, 0:4096], xin[t0 + g * 128: t0 + (g + 1) * 128, :],
                  waits=[("sr", s.at("sr")), ("se", s.at("se"))], inc="sx")
            ld = s.at("sx")
            for jc in range(NJC):
                dst = ap(xfm, jc * TP + g * 128, [[PS(xfm), 128], [1, 128]])
                def fn(e, dst=dst, jc=jc):
                    return e.dma_start(dst, xtm[:, jc * 128:(jc + 1) * 128], transpose=True)
                s.emit("sync", fn, waits=[("sx", ld)], inc="sr", amt=16)
        xfm_ready = s.at("sr")

        for (pname, wsrc, dof, isq) in projs:
            for ic in range(NJC):
                buf = ic % 2
                s.dma("gpsimd", wstr[:, buf * HID:(buf + 1) * HID],
                      wsrc[ic * 128:(ic + 1) * 128, :],
                      waits=[("stn", s.at("stn"))], inc="sw")
                wld = s.at("sw")
                pb = P[ic % 2]
                def mmfn(t_, buf=buf, pb=pb):
                    outs = []
                    for jc in range(NJC):
                        outs.append(t_.matmul(
                            pb[:, 0:TP],
                            wstr[:, buf * HID + jc * 128: buf * HID + (jc + 1) * 128],
                            ap(xfm, jc * TP, [[PS(xfm), 128], [1, TP]]),
                            start=(jc == 0), stop=(jc == NJC - 1)))
                    return outs
                s.emit("tensor", mmfn,
                       waits=[("sw", wld), ("sr", xfm_ready), ("se", s.at("se"))],
                       inc="stn")
                mmd = s.at("stn")
                scale_ap = mq8[:, 0:1] if isq else mkv[:, 0:1]
                dst = ap(Abig, dof + ic * TP, [[PS(Abig), 128], [1, TP]])
                def evfn(sc, dst=dst, pb=pb, scale_ap=scale_ap):
                    return sc.activation(dst, pb[:, 0:TP], AF.Copy, scale=scale_ap)
                s.emit("scalar", evfn,
                       waits=[("stn", mmd), ("sv", masks_sv), ("sr", s.at("sr"))],
                       inc="se")
        qkv_ready = s.at("sr")

        if "qkv" in taps and ph == 0:
            evall = s.at("se")
            s.dma("sync", tap_t["tq"][:, :], ap(Abig, QOF, [[PS(Abig), 128], [1, 8192]]),
                  waits=[("se", evall)], inc="sr")
            s.dma("sync", tap_t["tk"][:, :], ap(Abig, KOF, [[PS(Abig), 128], [1, 8192]]),
                  waits=[("se", evall)], inc="sr")
            s.dma("sync", tap_t["tv"][:, :], ap(Abig, VOF, [[PS(Abig), 128], [1, 8192]]),
                  waits=[("se", evall)], inc="sr")
            qkv_ready = s.at("sr")

        if stop_after == "qkv":
            continue

        for sub in range(8):
            sl0 = sub * 32
            for (dof, dstt) in ((QOF, qsub), (KOF, ksub), (VOF, vsub)):
                for hsel in range(2):
                    srcap = ap(Abig, dof + hsel * 64 * PS(Abig) + sl0,
                               [[PS(Abig), 64], [TP, 32], [1, 32]])
                    dstap = ap(dstt, hsel * 1024, [[PS(dstt), 64], [32, 32], [1, 32]])
                    def rfn(e, dstap=dstap, srcap=srcap):
                        return e.dma_start(dstap, srcap)
                    s.emit("sync", rfn,
                           waits=[("sr", qkv_ready), ("stn", s.at("stn")), ("se", s.at("se"))],
                           inc="sr", amt=16)
            sub_rdy = s.at("sr")
            if attn_stage == "rearr":
                continue
            for bk in range(4):
                pb = P[2 + (bk % 2)]
                def scfn(t_, pb=pb, bk=bk):
                    outs = []
                    for j in range(8):
                        tl = bk * 8 + j
                        lh = ap(qsub, tl, [[PS(qsub), 64], [32, 64]])
                        rh = ap(ksub, tl, [[PS(ksub), 64], [32, 64]])
                        outs.append(t_.matmul(pb[0:64, j * 64:(j + 1) * 64], lh, rh,
                                              start=True, stop=True))
                    return outs
                s.emit("tensor", scfn,
                       waits=[("sr", sub_rdy), ("se", s.at("se")), ("sv", s.at("sv"))],
                       inc="stn")
                scd = s.at("stn")
                if attn_stage == "mm":
                    continue
                srcp = ap(pb, 0, [[PS(pb), 64], [64, 8], [1, 64]])
                dstw = ap(wsb2, bk * 8 * 64, [[PS(wsb2), 64], [64, 8], [1, 64]])
                def exfn(sc, dstw=dstw, srcp=srcp):
                    return sc.activation(dstw, srcp, AF.Exp, bias=cN[0:64, C_EXPB:C_EXPB + 1])
                s.emit("scalar", exfn, waits=[("stn", scd)], inc="se")
                exd = s.at("se")
                def zfn(v, bk=bk):
                    srcw = ap(wsb2, bk * 8 * 64, [[PS(wsb2), 64], [64, 8], [1, 64]])
                    return v.reduce_sum(zsb[0:64, bk * 8:(bk + 1) * 8], srcw, axis=AX.X)
                s.emit("vector", zfn, waits=[("se", exd)], inc="sv")
            if attn_stage == "mm":
                continue
            expd = s.at("se")
            zred = s.at("sv")
            def rzfn(v):
                a_ = v.reciprocal(zsb[0:64, 0:32], zsb[0:64, 0:32])
                v.drain()
                b_ = v.tensor_copy(rzbf[0:64, 0:32], zsb[0:64, 0:32])
                return [a_, b_]
            s.emit("vector", rzfn, waits=[("sv", zred)], inc="sv")
            def w2fn(v):
                i0 = ap(wsb2, 0, [[PS(wsb2), 64], [64, 32], [1, 64]])
                i1 = ap(rzbf, 0, [[PS(rzbf), 64], [1, 32], [0, 64]])
                o = ap(w2, 0, [[PS(w2), 64], [64, 32], [1, 64]])
                return v.tensor_tensor(o, i0, i1, ALU.mult)
            s.emit("vector", w2fn, waits=[("sv", s.at("sv")), ("se", expd)], inc="sv")
            w2d = s.at("sv")
            if attn_stage == "scores":
                continue

            for grp in range(4):
                def trfn(t_, grp=grp):
                    outs = []
                    for i in range(8):
                        tl = grp * 8 + i
                        win = ap(w2, tl * 64, [[PS(w2), 64], [1, 64]])
                        wout = ap(PT, i * 64, [[PS(PT), 64], [1, 64]])
                        outs.append(t_.transpose(wout, win, idm[0:64, 0:64]))
                        vin = ap(vsub, tl, [[PS(vsub), 64], [32, 64]])
                        vout = ap(PT, 512 + i * 64, [[PS(PT), 64], [1, 64]])
                        outs.append(t_.transpose(vout, vin, idm[0:64, 0:64]))
                    return outs
                s.emit("tensor", trfn, waits=[("sv", w2d), ("se", s.at("se"))], inc="stn")
                trd = s.at("stn")
                def tefn(sc, grp=grp):
                    outs = []
                    wsrc = ap(PT, 0, [[PS(PT), 64], [64, 8], [1, 64]])
                    wdst = ap(wts, grp * 8, [[PS(wts), 64], [1, 8], [32, 64]])
                    outs.append(sc.copy(wdst, wsrc))
                    vsrc = ap(PT, 512, [[PS(PT), 64], [64, 8], [1, 64]])
                    vdst = ap(vts, grp * 8, [[PS(vts), 64], [1, 8], [32, 64]])
                    outs.append(sc.copy(vdst, vsrc))
                    return outs
                s.emit("scalar", tefn, waits=[("stn", trd)], inc="se")
            trevd = s.at("se")
            if attn_stage == "trans":
                continue

            for bko in range(4):
                pb = P[4 + (bko % 2)]
                def omfn(t_, pb=pb, bko=bko):
                    outs = []
                    for j in range(8):
                        tl = bko * 8 + j
                        lh = ap(vts, tl, [[PS(vts), 64], [32, 64]])
                        rh = ap(wts, tl, [[PS(wts), 64], [32, 64]])
                        outs.append(t_.matmul(pb[0:64, j * 64:(j + 1) * 64], lh, rh,
                                              start=True, stop=True))
                    return outs
                s.emit("tensor", omfn, waits=[("se", s.at("se"))], inc="stn")
                omd = s.at("stn")
                srcp = ap(pb, 0, [[PS(pb), 64], [64, 8], [1, 64]])
                dsto = ap(soar, bko * 8, [[PS(soar), 64], [1, 8], [32, 64]])
                def oefn(sc, dsto=dsto, srcp=srcp):
                    return sc.copy(dsto, srcp)
                s.emit("scalar", oefn, waits=[("stn", omd)], inc="se")
            oevd = s.at("se")
            for par in range(2):
                srcap = ap(soar, par * 32, [[PS(soar), 64], [64, 32], [1, 32]])
                dstap = ap(Abig, OOF + par * 64 * PS(Abig) + sl0, [[PS(Abig), 64], [TP, 32], [1, 32]])
                def rofn(e, dstap=dstap, srcap=srcap):
                    return e.dma_start(dstap, srcap)
                s.emit("sync", rofn, waits=[("se", oevd)], inc="sr", amt=16)
        attn_done = s.at("sr")

        if "attn" in taps and ph == 0:
            s.dma("sync", tap_t["to"][:, :], ap(Abig, OOF, [[PS(Abig), 128], [1, 8192]]),
                  waits=[("sr", attn_done)], inc="sr")
            attn_done = s.at("sr")

        if stop_after == "attn":
            continue

        for mc in range(NJC):
            buf = mc % 2
            s.dma("gpsimd", wstr[:, buf * HID:(buf + 1) * HID],
                  wot[mc * 128:(mc + 1) * 128, :],
                  waits=[("stn", s.at("stn"))], inc="sw")
            wld = s.at("sw")
            pb = P[mc % 2]
            def wofn(t_, buf=buf, pb=pb):
                outs = []
                for ac in range(NJC):
                    outs.append(t_.matmul(
                        pb[:, 0:TP],
                        wstr[:, buf * HID + ac * 128: buf * HID + (ac + 1) * 128],
                        ap(Abig, OOF + ac * TP, [[PS(Abig), 128], [1, TP]]),
                        start=(ac == 0), stop=(ac == NJC - 1)))
                return outs
            s.emit("tensor", wofn,
                   waits=[("sw", wld), ("sr", attn_done), ("sv", s.at("sv"))], inc="stn")
            wod = s.at("stn")
            def refn(v, mc=mc, pb=pb):
                return v.tensor_tensor(
                    sum32[:, mc * TP:(mc + 1) * TP],
                    pb[:, 0:TP],
                    ap(xfm, mc * TP, [[PS(xfm), 128], [1, TP]]),
                    ALU.add)
            s.emit("vector", refn, waits=[("stn", wod)], inc="sv")
        sumd = s.at("sv")

        for g8 in range(4):
            def castf(sc, g8=g8):
                outs = []
                for i in range(8):
                    mc = g8 * 8 + i
                    outs.append(sc.copy(qsub[:, i * 256:(i + 1) * 256],
                                        sum32[:, mc * TP:(mc + 1) * TP]))
                    outs.append(sc.activation(ksub[:, i * 256:(i + 1) * 256],
                                              sum32[:, mc * TP:(mc + 1) * TP], AF.Square))
                return outs
            s.emit("scalar", castf, waits=[("sv", sumd), ("stn", s.at("stn")), ("sr", s.at("sr"))], inc="se")
            cd_ = s.at("se")
            def stf2(t_, g8=g8):
                outs = []
                for i in range(8):
                    mc = g8 * 8 + i
                    outs.append(t_.matmul(P[2][0:1, 0:TP], ones1[:, 0:1],
                                          qsub[:, i * 256:(i + 1) * 256],
                                          start=(mc == 0), stop=(mc == NJC - 1)))
                    outs.append(t_.matmul(P[3][0:1, 0:TP], ones1[:, 0:1],
                                          ksub[:, i * 256:(i + 1) * 256],
                                          start=(mc == 0), stop=(mc == NJC - 1)))
                return outs
            s.emit("tensor", stf2, waits=[("se", cd_), ("sv", sumd)], inc="stn")
        std_ = s.at("stn")
        def rowfn(v):
            o = [v.tensor_scalar(lnrow[0:1, 0:256], P[2][0:1, 0:TP],
                                 cN[0:1, C_INV4096:C_INV4096 + 1], None, ALU.mult),
                 v.tensor_scalar(lnrow[0:1, 256:512], P[3][0:1, 0:TP],
                                 cN[0:1, C_INV4096:C_INV4096 + 1], None, ALU.mult)]
            v.drain()
            o.append(v.tensor_tensor(lnrow[0:1, 512:768], lnrow[0:1, 0:256], lnrow[0:1, 0:256], ALU.mult))
            v.drain()
            o.append(v.tensor_tensor(lnrow[0:1, 512:768], lnrow[0:1, 256:512], lnrow[0:1, 512:768], ALU.subtract))
            v.drain()
            o.append(v.tensor_scalar(lnrow[0:1, 512:768], lnrow[0:1, 512:768],
                                     cN[0:1, C_EPS:C_EPS + 1], None, ALU.add))
            return o
        s.emit("vector", rowfn, waits=[("stn", std_)], inc="sv")
        def rsq(sc):
            return sc.activation(lnrow[0:1, 256:512], lnrow[0:1, 512:768], AF.Sqrt)
        s.emit("scalar", rsq, waits=[("sv", s.at("sv"))], inc="se")
        def bfn(v):
            o = [v.reciprocal(lnrow[0:1, 256:512], lnrow[0:1, 256:512])]
            v.drain()
            o.append(v.tensor_tensor(lnrow[0:1, 0:256], lnrow[0:1, 0:256], lnrow[0:1, 256:512], ALU.mult))
            v.drain()
            o.append(v.tensor_scalar(lnrow[0:1, 0:256], lnrow[0:1, 0:256],
                                     cN[0:1, C_NEG1:C_NEG1 + 1], None, ALU.mult))
            return o
        s.emit("vector", bfn, waits=[("se", s.at("se"))], inc="sv")
        rowd = s.at("sv")
        def bcfn(t_):
            return [
                t_.matmul(P[4][:, 0:TP], onesr32[0:1, :], lnrow[0:1, 256:512], start=True, stop=True),
                t_.matmul(P[5][:, 0:TP], onesr32[0:1, :], lnrow[0:1, 0:256], start=True, stop=True),
            ]
        s.emit("tensor", bcfn, waits=[("sv", rowd), ("se", s.at("se"))], inc="stn")
        bcd = s.at("stn")
        def apfn(v, ph=ph):
            outs = []
            for mc2 in range(0, NJC, 2):
                t0 = x2f[:, 0:256]
                t1 = x2f[:, 256:512]
                outs.append(v.tensor_tensor(
                    t0, sum32[:, mc2 * TP:(mc2 + 1) * TP], P[4][:, 0:TP], ALU.mult))
                outs.append(v.tensor_tensor(
                    t1, sum32[:, (mc2 + 1) * TP:(mc2 + 2) * TP], P[4][:, 0:TP], ALU.mult))
                v.drain()
                outs.append(v.tensor_tensor(
                    ap(xn, mc2 * T + ph * TP, [[PS(xn), 128], [1, TP]]),
                    t0, P[5][:, 0:TP], ALU.add))
                outs.append(v.tensor_tensor(
                    ap(xn, (mc2 + 1) * T + ph * TP, [[PS(xn), 128], [1, TP]]),
                    t1, P[5][:, 0:TP], ALU.add))
                v.drain()
            return outs
        s.emit("vector", apfn, waits=[("stn", bcd)], inc="sv")
        lastap = s.at("sv")
        def nopf(e):
            return e.nop()
        s.emit("sync", nopf, waits=[("sv", lastap)], inc="sr", amt=1)

    xn_sv = s.at("sv")

    if stop_after == "qkv" or stop_after == "attn":
        fin = s.at("sr")
        def earlyw(e):
            return e.nop()
        s.emit("sync", earlyw, waits=[("sr", fin)])
        block = st.enter_context(nc.Block())
        s.build(block)
        st.close()
        s._stack.close()
        return nc, tap_t

    if "xn" in taps:
        s.dma("sync", tap_t["txn"][:, :], xn[:, :], waits=[("sv", xn_sv)], inc="sr")

    if stop_after == "xn":
        fin = s.at("sr")
        def earlyw2(e):
            return e.nop()
        s.emit("sync", earlyw2, waits=[("sr", fin)])
        block = st.enter_context(nc.Block())
        s.build(block)
        st.close()
        s._stack.close()
        return nc, tap_t

    for (sdr, s16) in ((gscale, gs16), (uscale, us16), (dscale, ds16)):
        s.dma("gpsimd", ap(cstr, 0, [[PS(cstr), NGI], [1, HID]]), sdr[:, :],
              waits=[("sv", xn_sv), ("sr", s.at("sr")), ("se", s.at("se")), ("sx", s.at("sx"))], inc="sx")
        s.dma("sync", s16[:, :], ap(cstr, 0, [[PS(cstr), NGI], [1, HID]]),
              waits=[("sx", s.at("sx"))], inc="sr")

    PIECES = [(0, 512), (512, 512), (1024, 352)]
    for (zdr, sdr, cdr) in ((gzer, gsc_nat, cg_dram), (uzer, usc_nat, cu_dram)):
        for (p0, pw) in PIECES:
            s.dma("gpsimd", ap(zztmp, 0, [[PS(zztmp), 32], [1, pw]]).bitcast(I32),
                  zdr[:, p0:p0 + pw],
                  waits=[("sv", s.at("sv")), ("sx", s.at("sx")), ("sr", s.at("sr")),
                         ("se", s.at("se")), ("sh", s.at("sh")), ("stn", s.at("stn"))], inc="sx")
            s.dma("gpsimd", ap(xfm, 0, [[PS(xfm), 32], [1, pw * 8]]),
                  sdr[:, p0 * 8:(p0 + pw) * 8], inc="sx")
            zl = s.at("sx")
            for jj in range(8):
                def plfn(v, jj=jj, pw=pw):
                    o1 = v.tensor_scalar(ap(zztmp, 512, [[PS(zztmp), 32], [1, pw]]).bitcast(I32),
                                         ap(zztmp, 0, [[PS(zztmp), 32], [1, pw]]).bitcast(I32),
                                         4 * jj, 15, ALU.logical_shift_right, ALU.bitwise_and)
                    v.drain()
                    o2 = v.tensor_copy(ap(zztmp, 1024, [[PS(zztmp), 32], [1, pw]]),
                                       ap(zztmp, 512, [[PS(zztmp), 32], [1, pw]]).bitcast(I32))
                    v.drain()
                    o3 = v.tensor_scalar(ap(zztmp, 1024, [[PS(zztmp), 32], [1, pw]]),
                                         ap(zztmp, 1024, [[PS(zztmp), 32], [1, pw]]),
                                         -7.0, -1.0, ALU.add, ALU.mult)
                    v.drain()
                    o4 = v.tensor_copy(ap(w2, 0, [[PS(w2), 32], [1, pw]]),
                                       ap(zztmp, 1024, [[PS(zztmp), 32], [1, pw]]))
                    v.drain()
                    o5 = v.tensor_tensor(ap(w2, 1024, [[PS(w2), 32], [1, pw]]),
                                         ap(w2, 0, [[PS(w2), 32], [1, pw]]),
                                         ap(xfm, jj, [[PS(xfm), 32], [8, pw]]),
                                         ALU.mult)
                    return [o1, o2, o3, o4, o5]
                s.emit("vector", plfn, waits=[("sx", zl), ("sr", s.at("sr"))], inc="sv")
                pld = s.at("sv")
                dst = ap(cdr, p0 * 8 + jj, [[INTER, 32], [8, pw]])
                srcw = ap(w2, 1024, [[PS(w2), 32], [1, pw]])
                def cst_dma(e, dst=dst, srcw=srcw):
                    return e.dma_start(dst, srcw, allow_slow_non_contiguous=True)
                s.emit("sync", cst_dma, waits=[("sv", pld)], inc="sr", amt=16)
    s.dma("gpsimd", ap(xfm, 0, [[PS(xfm), 86], [1, HID]]), dscale[:, :],
          waits=[("sr", s.at("sr")), ("sv", s.at("sv")), ("stn", s.at("stn"))], inc="sx")
    s.dma("gpsimd", ap(zztmp, 0, [[PS(zztmp), 86], [1, 512]]).bitcast(I32), dzer[:, :],
          waits=[("sx", s.at("sx"))], inc="sx")
    dzl = s.at("sx")
    for jj in range(8):
        def pl2fn(v, jj=jj):
            o1 = v.tensor_scalar(ap(zztmp, 512, [[PS(zztmp), 86], [1, 512]]).bitcast(I32),
                                 ap(zztmp, 0, [[PS(zztmp), 86], [1, 512]]).bitcast(I32),
                                 4 * jj, 15, ALU.logical_shift_right, ALU.bitwise_and)
            v.drain()
            o2 = v.tensor_copy(ap(zztmp, 1024, [[PS(zztmp), 86], [1, 512]]),
                               ap(zztmp, 512, [[PS(zztmp), 86], [1, 512]]).bitcast(I32))
            v.drain()
            o3 = v.tensor_scalar(ap(zztmp, 1024, [[PS(zztmp), 86], [1, 512]]),
                                 ap(zztmp, 1024, [[PS(zztmp), 86], [1, 512]]),
                                 -7.0, -1.0, ALU.add, ALU.mult)
            v.drain()
            o4 = v.tensor_copy(ap(w2, 2048, [[PS(w2), 86], [1, 512]]),
                               ap(zztmp, 1024, [[PS(zztmp), 86], [1, 512]]))
            v.drain()
            o5 = v.tensor_tensor(ap(w2, 3072, [[PS(w2), 86], [1, 512]]),
                                 ap(w2, 2048, [[PS(w2), 86], [1, 512]]),
                                 ap(xfm, jj, [[PS(xfm), 86], [8, 512]]),
                                 ALU.mult)
            return [o1, o2, o3, o4, o5]
        s.emit("vector", pl2fn, waits=[("sx", dzl), ("sr", s.at("sr"))], inc="sv")
        pl2d = s.at("sv")
        dstcd = ap(cd_dram, jj, [[HID, 86], [8, 512]])
        srccd = ap(w2, 3072, [[PS(w2), 86], [1, 512]])
        def cddma(e, dstcd=dstcd, srccd=srccd):
            return e.dma_start(dstcd, srccd, allow_slow_non_contiguous=True)
        s.emit("sync", cddma, waits=[("sv", pl2d)], inc="sr", amt=16)
    cdone_sv = s.at("sv")
    cg_done_sr = s.at("sr")

    for tc in range(2):
        def xsfn(t_, tc=tc):
            outs = []
            for kc in range(NJC):
                outs.append(t_.matmul(
                    P[6][0:32, 0:512],
                    w32[:, 31 - kc:63 - kc],
                    ap(xn, kc * T + tc * 512, [[PS(xn), 128], [1, 512]]),
                    start=(kc == 0), stop=(kc == NJC - 1)))
            return outs
        s.emit("tensor", xsfn, waits=[("sv", xn_sv), ("se", s.at("se")), ("stn", s.at("stn"))], inc="stn")
        xsd = s.at("stn")
        def xsev(sc, tc=tc):
            return sc.copy(ap(xsb, tc * 512, [[PS(xsb), 32], [1, 512]]),
                           ap(P[6], 0, [[PS(P[6]), 32], [1, 512]]))
        s.emit("scalar", xsev, waits=[("stn", xsd)], inc="se")
    xs_done = s.at("se")

    for ncx in range(NGI):
        buf = ncx % 2
        if ncx % 8 == 0:
            cw = min(1024, (NGI - ncx) * 128)
            s.dma("sync", cbuf[0:32, 0:cw], cg_dram[:, ncx * 128: ncx * 128 + cw],
                  waits=[("stn", s.at("stn")), ("sr", cg_done_sr), ("sv", cdone_sv)], inc="sx")
            s.dma("sync", cbuf[0:32, 1024:1024 + cw], cu_dram[:, ncx * 128: ncx * 128 + cw],
                  inc="sx")
        cb_ld = s.at("sx")
        s.dma("sync", bstr[:, buf * HID:(buf + 1) * HID],
              gbytes[ncx * 128:(ncx + 1) * 128, :],
              waits=[("sv", s.at("sv"))], inc="sx")
        s.dma("sync", rstr[:, 0:HID],
              ap(gs16, ncx * NJC * 128, [[0, 128], [1, HID]]),
              waits=[("sh", s.at("sh")), ("sr", s.at("sr"))], inc="sx")
        gld = s.at("sx")
        def dqg(v, buf=buf, gld=gld):
            return v.tensor_scalar(ustr[:, :], bstr[:, buf * HID:(buf + 1) * HID],
                                   shf[:, 0:1], 15, ALU.logical_shift_right, ALU.bitwise_and)
        s.emit("vector", dqg, waits=[("sx", gld), ("se", s.at("se")), ("stn", s.at("stn"))], inc="sv")
        uq = s.at("sv")
        def cst(sc):
            return sc.activation(cstr[:, :], ustr[:, :], AF.Copy, bias=-8.0)
        s.emit("scalar", cst, waits=[("sv", uq), ("sh", s.at("sh"))], inc="se")
        csd = s.at("se")
        def sclm(g, buf=buf, csd=csd):
            return g.tensor_tensor(wstr[:, buf * HID:(buf + 1) * HID], cstr[:, :],
                                   rstr[:, 0:HID], ALU.mult)
        s.emit("vector", sclm, waits=[("se", csd)], inc="sh")
        wg = s.at("sh")
        for tc in range(2):
            def gmm(t_, buf=buf, tc=tc, ncx=ncx):
                outs = []
                for kc in range(NJC):
                    outs.append(t_.matmul(
                        P[tc][:, 0:512],
                        wstr[:, buf * HID + kc * 128: buf * HID + (kc + 1) * 128],
                        ap(xn, kc * T + tc * 512, [[PS(xn), 128], [1, 512]]),
                        start=(kc == 0), stop=False))
                outs.append(t_.matmul(
                    P[tc][:, 0:512],
                    ap(cbuf, (ncx % 8) * 128, [[PS(cbuf), 32], [1, 128]]),
                    ap(xsb, tc * 512, [[PS(xsb), 32], [1, 512]]),
                    start=False, stop=True))
                return outs
            s.emit("tensor", gmm, waits=[("sh", wg), ("se", xs_done), ("sv", s.at("sv")), ("sx", cb_ld)], inc="stn")
        gmd = s.at("stn")
        def silu(sc, gmd=gmd):
            return [sc.activation(qsub[:, 0:512], P[0][:, 0:512], AF.Silu),
                    sc.activation(qsub[:, 512:1024], P[1][:, 0:512], AF.Silu)]
        s.emit("scalar", silu, waits=[("stn", gmd), ("sv", s.at("sv")), ("sr", s.at("sr"))], inc="se")
        sild = s.at("se")
        s.dma("sync", bstr[:, buf * HID:(buf + 1) * HID],
              ubytes[ncx * 128:(ncx + 1) * 128, :],
              waits=[("sh", wg)], inc="sx")
        s.dma("sync", rstr[:, 0:HID],
              ap(us16, ncx * NJC * 128, [[0, 128], [1, HID]]),
              waits=[("sh", wg)], inc="sx")
        uld = s.at("sx")
        def dqu(v, buf=buf, uld=uld):
            return v.tensor_scalar(ustr[:, :], bstr[:, buf * HID:(buf + 1) * HID],
                                   shf[:, 0:1], 15, ALU.logical_shift_right, ALU.bitwise_and)
        s.emit("vector", dqu, waits=[("sx", uld), ("se", sild)], inc="sv")
        uq2 = s.at("sv")
        def cst2(sc):
            return sc.activation(cstr[:, :], ustr[:, :], AF.Copy, bias=-8.0)
        s.emit("scalar", cst2, waits=[("sv", uq2), ("sh", s.at("sh"))], inc="se")
        csd2 = s.at("se")
        def sclm2(g, buf=buf, csd2=csd2):
            return g.tensor_tensor(wstr[:, buf * HID:(buf + 1) * HID], cstr[:, :],
                                   rstr[:, 0:HID], ALU.mult)
        s.emit("vector", sclm2, waits=[("se", csd2)], inc="sh")
        wu = s.at("sh")
        for tc in range(2):
            def umm(t_, buf=buf, tc=tc, ncx=ncx):
                outs = []
                for kc in range(NJC):
                    outs.append(t_.matmul(
                        P[2 + tc][:, 0:512],
                        wstr[:, buf * HID + kc * 128: buf * HID + (kc + 1) * 128],
                        ap(xn, kc * T + tc * 512, [[PS(xn), 128], [1, 512]]),
                        start=(kc == 0), stop=False))
                outs.append(t_.matmul(
                    P[2 + tc][:, 0:512],
                    ap(cbuf, 1024 + (ncx % 8) * 128, [[PS(cbuf), 32], [1, 128]]),
                    ap(xsb, tc * 512, [[PS(xsb), 32], [1, 512]]),
                    start=False, stop=True))
                return outs
            s.emit("tensor", umm, waits=[("sh", wu), ("sv", s.at("sv")), ("se", s.at("se"))], inc="stn")
        umd = s.at("stn")
        def hfn(v, umd=umd):
            return [v.tensor_tensor(ksub[:, 0:512], qsub[:, 0:512], P[2][:, 0:512], ALU.mult),
                    v.tensor_tensor(ksub[:, 512:1024], qsub[:, 512:1024], P[3][:, 0:512], ALU.mult)]
        s.emit("vector", hfn, waits=[("stn", umd), ("sr", s.at("sr"))], inc="sv")
        hd = s.at("sv")
        def x2fn(t_, ncx=ncx):
            outs = []
            for tc in range(2):
                outs.append(t_.matmul(
                    P[4 + tc][0:86, 0:512],
                    w86[:, 85 - ncx:171 - ncx],
                    ksub[:, tc * 512:(tc + 1) * 512],
                    start=(ncx == 0), stop=(ncx == NGI - 1)))
            return outs
        s.emit("tensor", x2fn, waits=[("sv", hd)], inc="stn")
        s.dma("gpsimd", h_dram[ncx * 128:(ncx + 1) * 128, :], ksub[:, 0:1024],
              waits=[("sv", hd)], inc="sr")
    gate_done_sr = s.at("sr")
    x2mm = s.at("stn")
    def x2ev(sc):
        return [sc.copy(ap(x2sb, 0, [[PS(x2sb), 86], [1, 512]]),
                        ap(P[4], 0, [[PS(P[4]), 86], [1, 512]])),
                sc.copy(ap(x2sb, 512, [[PS(x2sb), 86], [1, 512]]),
                        ap(P[5], 0, [[PS(P[5]), 86], [1, 512]]))]
    s.emit("scalar", x2ev, waits=[("stn", x2mm)], inc="se")
    x2d = s.at("se")

    if "h" in taps:
        s.dma("sync", tap_t["th"][:, :], h_dram[:, :], waits=[("sr", gate_done_sr)], inc="sr")

    if stop_after == "h":
        fin = s.at("sr")
        x2w = s.at("se")
        def earlyw3(e):
            return e.nop()
        s.emit("sync", earlyw3, waits=[("sr", fin), ("se", x2w)])
        block = st.enter_context(nc.Block())
        s.build(block)
        st.close()
        s._stack.close()
        return nc, tap_t

    for jc in range(NGI):
        s.dma("sync", bstr[:, 0:HID],
              dbytes[jc * 128:(jc + 1) * 128, :],
              waits=[("sh", s.at("sh")), ("sr", s.at("sr")), ("sv", s.at("sv"))], inc="sx")
        s.dma("sync", rstr[:, 0:HID],
              ap(ds16, jc * HID, [[0, 128], [1, HID]]),
              waits=[("sh", s.at("sh"))], inc="sx")
        dld = s.at("sx")
        def dqd(v, dld=dld):
            return v.tensor_scalar(ustr[:, :], bstr[:, 0:HID],
                                   shf[:, 0:1], 15, ALU.logical_shift_right, ALU.bitwise_and)
        s.emit("vector", dqd, waits=[("sx", dld), ("se", s.at("se"))], inc="sv")
        uq3 = s.at("sv")
        def cst3(sc):
            return sc.activation(cstr[:, :], ustr[:, :], AF.Copy, bias=-8.0)
        s.emit("scalar", cst3, waits=[("sv", uq3), ("sh", s.at("sh"))], inc="se")
        csd3 = s.at("se")
        def sclm3(g, csd3=csd3):
            return g.tensor_tensor(wstr[:, 0:HID], cstr[:, :], rstr[:, 0:HID], ALU.mult)
        s.emit("vector", sclm3, waits=[("se", csd3), ("sr", s.at("sr"))], inc="sh")
        wdd = s.at("sh")
        s.dma("gpsimd", wd_dram[jc * 128:(jc + 1) * 128, :], wstr[:, 0:HID],
              waits=[("sh", wdd)], inc="sr")
    wd_done = s.at("sr")

    if "wd" in taps:
        s.dma("sync", tap_t["twd"][:, :], wd_dram[0:256, :], waits=[("sr", wd_done)], inc="sr")

    for mb in range(8):
        s.dma("sync", cdnb[0:86, 0:512], cd_dram[:, mb * 512:(mb + 1) * 512],
              waits=[("stn", s.at("stn")), ("sr", wd_done)], inc="sx")
        cdl = s.at("sx")
        for th in range(2):
            for jg in range(0, NGI, 4):
                nj = min(4, NGI - jg)
                buf = (jg // 4) % 2
                wsrc = ap(wd_dram, jg * 128 * HID + mb * 512,
                          [[HID, 128], [128 * HID, nj], [1, 512]])
                s.dma("sync", ap(wstr, buf * 2048, [[PS(wstr), 128], [512, nj], [1, 512]]),
                      wsrc,
                      waits=[("stn", s.at("stn")), ("sr", wd_done), ("sh", s.at("sh"))], inc="sx")
                hsrc = ap(h_dram, jg * 128 * T + th * 512,
                          [[T, 128], [128 * T, nj], [1, 512]])
                s.dma("sync", ap(hdown, buf * 2048, [[PS(hdown), 128], [512, nj], [1, 512]]),
                      hsrc,
                      waits=[("sr", gate_done_sr)], inc="sx")
                dl = s.at("sx")
                def dmm(t_, buf=buf, jg=jg, nj=nj):
                    outs = []
                    for j2 in range(nj):
                        for tb4 in range(4):
                            outs.append(t_.matmul(
                                P[tb4][:, 0:512],
                                ap(hdown, buf * 2048 + j2 * 512 + tb4 * 128,
                                   [[PS(hdown), 128], [1, 128]]),
                                ap(wstr, buf * 2048 + j2 * 512, [[PS(wstr), 128], [1, 512]]),
                                start=(jg + j2 == 0), stop=False))
                    return outs
                s.emit("tensor", dmm, waits=[("sx", dl), ("sv", s.at("sv")), ("se", x2d)], inc="stn")
            def dcor(t_, mb=mb, th=th):
                outs = []
                for tb4 in range(4):
                    tbg = th * 4 + tb4
                    outs.append(t_.matmul(
                        P[tb4][:, 0:512],
                        ap(x2sb, tbg * 128, [[PS(x2sb), 86], [1, 128]]),
                        ap(cdnb, 0, [[PS(cdnb), 86], [1, 512]]),
                        start=False, stop=True))
                return outs
            s.emit("tensor", dcor, waits=[("se", x2d), ("sx", cdl)], inc="stn")
            dmd = s.at("stn")
            if "dp0" in taps and mb == 0 and th == 0:
                tap_t["tdp"] = nc.dram_tensor("tdp", [128, 512], F32, kind="ExternalOutput")
                def dpcp(sc):
                    return sc.copy(x2f[:, 0:512], P[0][:, 0:512])
                s.emit("scalar", dpcp, waits=[("stn", dmd)], inc="se")
                s.dma("sync", tap_t["tdp"][:, :], x2f[:, 0:512], waits=[("se", s.at("se"))], inc="sr")
            for tb4 in range(4):
                tbg = th * 4 + tb4
                def xtf(t_, mb=mb, tbg=tbg):
                    outs = []
                    for mcc in range(4):
                        mc = mb * 4 + mcc
                        outs.append(t_.transpose(
                            ap(PT, mcc * 128, [[PS(PT), 128], [1, 128]]),
                            ap(xn, mc * T + tbg * 128, [[PS(xn), 128], [1, 128]]),
                            idm2[:, :]))
                    return outs
                s.emit("tensor", xtf, waits=[("se", s.at("se")), ("stn", dmd)], inc="stn")
                xtd = s.at("stn")
                def xte(sc):
                    return sc.copy(ap(wsb2, 0, [[PS(wsb2), 128], [1, 512]]),
                                   ap(PT, 0, [[PS(PT), 128], [1, 512]]))
                s.emit("scalar", xte, waits=[("stn", xtd)], inc="se")
                xted = s.at("se")
                def res2(v, mb=mb, tbg=tbg, tb4=tb4):
                    return v.tensor_tensor(
                        ap(Abig, tbg * 4096 + mb * 512, [[PS(Abig), 128], [1, 512]]),
                        P[tb4][:, 0:512],
                        ap(wsb2, 0, [[PS(wsb2), 128], [1, 512]]),
                        ALU.add)
                s.emit("vector", res2, waits=[("se", xted)], inc="sv")
                r2d = s.at("sv")
                def sq2(sc, tbg=tbg, mb=mb):
                    return sc.activation(ap(w2, 0, [[PS(w2), 128], [1, 512]]),
                                         ap(Abig, tbg * 4096 + mb * 512, [[PS(Abig), 128], [1, 512]]),
                                         AF.Square)
                s.emit("scalar", sq2, waits=[("sv", r2d)], inc="se")
                sq2d = s.at("se")
                def st2(v, tbg=tbg, mb=mb):
                    a = v.reduce_sum(stats[:, tbg * 8 + mb: tbg * 8 + mb + 1],
                                     ap(Abig, tbg * 4096 + mb * 512, [[PS(Abig), 128], [1, 512]]),
                                     axis=AX.X)
                    b = v.reduce_sum(stats[:, 64 + tbg * 8 + mb: 64 + tbg * 8 + mb + 1],
                                     ap(w2, 0, [[PS(w2), 128], [1, 512]]), axis=AX.X)
                    return [a, b]
                s.emit("vector", st2, waits=[("se", sq2d)], inc="sv")
    down_sv = s.at("sv")
    if "sum2" in taps:
        tap_t["ts2"] = nc.dram_tensor("ts2", [128, 32768], BF16, kind="ExternalOutput")
        s.dma("sync", tap_t["ts2"][:, :], ap(Abig, 0, [[PS(Abig), 128], [1, 32768]]),
              waits=[("sv", down_sv)], inc="sr")

    def fin1(v):
        return [
            v.reduce_sum(stats[:, 128:136], ap(stats, 0, [[PS(stats), 128], [8, 8], [1, 8]]), axis=AX.X),
            v.reduce_sum(stats[:, 136:144], ap(stats, 64, [[PS(stats), 128], [8, 8], [1, 8]]), axis=AX.X),
        ]
    s.emit("vector", fin1, waits=[("sv", down_sv)], inc="sv")
    def fin2(v):
        o = [v.tensor_scalar(stats[:, 128:136], stats[:, 128:136], cN[:, C_INV4096:C_INV4096 + 1], None, ALU.mult),
             v.tensor_scalar(stats[:, 136:144], stats[:, 136:144], cN[:, C_INV4096:C_INV4096 + 1], None, ALU.mult)]
        v.drain()
        o.append(v.tensor_tensor(stats[:, 160:168], stats[:, 128:136], stats[:, 128:136], ALU.mult))
        v.drain()
        o.append(v.tensor_tensor(stats[:, 160:168], stats[:, 136:144], stats[:, 160:168], ALU.subtract))
        v.drain()
        o.append(v.tensor_scalar(stats[:, 160:168], stats[:, 160:168], cN[:, C_EPS:C_EPS + 1], None, ALU.add))
        return o
    s.emit("vector", fin2, waits=[("sv", s.at("sv"))], inc="sv")
    def fin3(sc):
        return sc.activation(stats[:, 144:152], stats[:, 160:168], AF.Sqrt)
    s.emit("scalar", fin3, waits=[("sv", s.at("sv"))], inc="se")
    def fin4(v):
        r0 = v.reciprocal(stats[:, 144:152], stats[:, 144:152])
        v.drain()
        a = v.tensor_tensor(stats[:, 152:160], stats[:, 128:136], stats[:, 144:152], ALU.mult)
        v.drain()
        b = v.tensor_scalar(stats[:, 152:160], stats[:, 152:160], cN[:, C_NEG1:C_NEG1 + 1], None, ALU.mult)
        return [r0, a, b]
    s.emit("vector", fin4, waits=[("se", s.at("se"))], inc="sv")
    ln2d = s.at("sv")

    for tb in range(8):
        for piece in range(4):
            def outp(v, tb=tb, piece=piece):
                return v.tensor_scalar(
                    x2f[:, :],
                    ap(Abig, tb * 4096 + piece * 1024, [[PS(Abig), 128], [1, 1024]]),
                    stats[:, 144 + tb:145 + tb], stats[:, 152 + tb:153 + tb],
                    ALU.mult, ALU.add)
            s.emit("vector", outp, waits=[("sv", ln2d), ("sr", s.at("sr"))], inc="sv")
            od = s.at("sv")
            s.dma("sync", y[tb * 128:(tb + 1) * 128, piece * 1024:(piece + 1) * 1024],
                  x2f[:, :], waits=[("sv", od)], inc="sr")
    final_sr = s.at("sr")
    def finw(e):
        return e.nop()
    s.emit("sync", finw, waits=[("sr", final_sr)])

    block = st.enter_context(nc.Block())
    s.build(block)
    st.close()
    s._stack.close()
    return nc, tap_t


def _host_stage(inputs):
    d = {}
    f32 = np.float32

    def restage_w(w):
        wt = np.ascontiguousarray(np.asarray(w, f32).T)
        return np.ascontiguousarray(
            wt.reshape(NJC, 128, NJC, 128).transpose(2, 1, 0, 3).reshape(NJC * 128, HID))

    d["wqt"] = restage_w(inputs["L0_attn_Wq"])
    d["wkt"] = restage_w(inputs["L0_attn_Wk"])
    d["wvt"] = restage_w(inputs["L0_attn_Wv"])
    wot_nat = np.ascontiguousarray(np.asarray(inputs["L0_attn_Wo"], f32).T)
    hp = np.arange(4096) // 64
    dd = np.arange(4096) % 64
    imap = (2 * (hp % 32) + hp // 32) * 64 + dd
    wot_p = wot_nat[imap]
    d["wot"] = np.ascontiguousarray(
        wot_p.reshape(NJC, 128, NJC, 128).transpose(2, 1, 0, 3).reshape(NJC * 128, HID))

    def tile3(a, b, c):
        out = np.zeros((128, 3), f32)
        out[:, 0] = np.tile(np.asarray(a, f32), 2)
        out[:, 1] = np.tile(np.asarray(b, f32), 2)
        out[:, 2] = np.tile(np.asarray(c, f32), 2)
        return out
    d["maskq"] = tile3(inputs["L0_attn_mu_q"], inputs["L0_attn_sigma_q"], inputs["L0_attn_eps_q"])
    d["maskkv"] = tile3(inputs["L0_attn_mu_kv"], inputs["L0_attn_sigma_kv"], inputs["L0_attn_eps_kv"])

    cv = np.zeros((128, 8), f32)
    cv[:, C_EXPB] = -4.0
    cv[:, C_INV4096] = 1.0 / 4096.0
    cv[:, C_EPS] = 1e-5
    cv[:, C_NEG1] = -1.0
    cv[:, C_ONE] = 1.0
    cv[:, C_NEG8] = -8.0
    d["consts"] = cv

    idm = np.zeros((128, 128), bfdt)
    idm[0:64, 0:64] = np.eye(64)
    idm[64:128, 0:64] = np.eye(64)
    d["idmat"] = idm
    d["idmat2"] = np.eye(128).astype(bfdt)
    o1 = np.zeros((128, 2), bfdt)
    o1[:, 0] = 1
    d["ones128"] = o1
    d["onesrow"] = np.ones((1, 128), bfdt)
    d["onesrow32"] = np.ones((1, 128), f32)
    w32_ = np.zeros((128, 63), bfdt)
    w32_[:, 31] = 1
    d["wide32"] = w32_
    w86_ = np.zeros((128, 171), bfdt)
    w86_[:, 85] = 1
    d["wide86"] = w86_
    d["shiftpp"] = np.tile(np.array([0, 4], np.uint8), 64).reshape(128, 1)

    def bytes_rep(qw):
        K8, N = qw.shape
        b = np.ascontiguousarray(qw).view(np.uint8).reshape(K8, N, 4).transpose(0, 2, 1).reshape(K8 * 4, N)
        return np.repeat(b, 2, axis=0)

    gb = bytes_rep(np.asarray(inputs["L0_ffn_gate_qweight"], np.int32))
    ub = bytes_rep(np.asarray(inputs["L0_ffn_up_qweight"], np.int32))
    db = bytes_rep(np.asarray(inputs["L0_ffn_down_qweight"], np.int32))
    d["gbytes"] = np.ascontiguousarray(
        gb.reshape(NJC, 128, NGI, 128).transpose(2, 1, 0, 3).reshape(NGI * 128, NJC * 128))
    d["ubytes"] = np.ascontiguousarray(
        ub.reshape(NJC, 128, NGI, 128).transpose(2, 1, 0, 3).reshape(NGI * 128, NJC * 128))
    d["dbytes"] = np.ascontiguousarray(db)

    gs = np.asarray(inputs["L0_ffn_gate_scales"], f32)
    us = np.asarray(inputs["L0_ffn_up_scales"], f32)
    ds = np.asarray(inputs["L0_ffn_down_scales"], f32)
    d["gscale"] = np.ascontiguousarray(gs.reshape(NJC, NGI, 128).transpose(1, 0, 2).reshape(NGI, NJC * 128))
    d["uscale"] = np.ascontiguousarray(us.reshape(NJC, NGI, 128).transpose(1, 0, 2).reshape(NGI, NJC * 128))
    d["dscale"] = np.ascontiguousarray(ds)
    d["gsc_nat"] = np.ascontiguousarray(gs)
    d["usc_nat"] = np.ascontiguousarray(us)
    d["gzer"] = np.ascontiguousarray(np.asarray(inputs["L0_ffn_gate_qzeros"], np.int32))
    d["uzer"] = np.ascontiguousarray(np.asarray(inputs["L0_ffn_up_qzeros"], np.int32))
    d["dzer"] = np.ascontiguousarray(np.asarray(inputs["L0_ffn_down_qzeros"], np.int32))
    return d


def kernel(**inputs):
    x = np.ascontiguousarray(np.asarray(inputs["L0_attn_hidden_states"], np.float32).reshape(-1, HID))
    shared = _host_stage(inputs)
    nc, _ = build_nc()
    in_maps = []
    for c in range(NCORE):
        m = dict(shared)
        m["xin"] = np.ascontiguousarray(x[c * T:(c + 1) * T])
        in_maps.append(m)
    res = run_bass_kernel_spmd(nc, in_maps, core_ids=list(range(NCORE)))
    out = np.concatenate([r["y"] for r in res.results], axis=0)
    return np.ascontiguousarray(out.reshape(2, 4096, HID))
